# version 23
# baseline (speedup 1.0000x reference)
from contextlib import ExitStack

import numpy as np

import concourse.bass as bass
import concourse.mybir as mybir
from concourse.bass_utils import run_bass_kernel_spmd

F32 = mybir.dt.float32
BF16 = mybir.dt.bfloat16
AF = mybir.ActivationFunctionType
ALU = mybir.AluOpType


class Cfg:
    def __init__(self, D=2048, T=2048, P=2048, H=5461, TS=512):
        self.D, self.T, self.P, self.H, self.TS = D, T, P, H, TS
        self.KC = D // 128
        self.HC = (H + 127) // 128
        self.HP = self.HC * 128
        self.n_pre = P // TS
        self.n_own = T // TS
        self.SCALE = 128.0 ** -0.5
        self.EPS = 1e-5
        self.RC = 12582912.0


CFG = Cfg()

ENGINES = ("tensor", "vector", "scalar", "sync", "gpsimd")
ENG_SEM = {e: "E_" + e for e in ENGINES}


class Buf:

    def __init__(self, kb, handle, name):
        self.kb = kb
        self.h = handle
        self.name = name
        self.writes = {}
        self.readers = {}

    def ap(self):
        return self.h[:]

    def _merge(self, d, tk):
        s, v = tk
        if d.get(s, 0) < v:
            d[s] = v


class KB:
    def __init__(self, nc, stack):
        self.nc = nc
        self.stack = stack
        self.semh = {}
        self.semc = {}
        self.ops = {e: [] for e in ENGINES}
        self.waited = {e: {} for e in ENGINES}
        self.barrier_deps = {}
        for e in ENGINES:
            self._sem(ENG_SEM[e])

    def _sem(self, name):
        if name not in self.semh:
            self.semh[name] = self.stack.enter_context(self.nc.semaphore(name))
            self.semc[name] = 0
        return name

    def sbuf(self, name, shape, dtype=F32, stack=None):
        h = (stack or self.stack).enter_context(
            self.nc.sbuf_tensor(name, list(shape), dtype))
        return Buf(self, h, name)

    def psum(self, name, shape=(128, 512), dtype=F32):
        h = self.stack.enter_context(self.nc.psum_tensor(name, list(shape), dtype))
        return Buf(self, h, name)

    def dram_buf(self, name, shape, dtype=F32, kind=None):
        if kind is None:
            h = self.nc.dram_tensor(name, list(shape), dtype)
        else:
            h = self.nc.dram_tensor(name, list(shape), dtype, kind=kind)
        return Buf(self, h, name)

    def barrier(self):
        self.barrier_deps = dict(self.semc)

    def _collect(self, engine, reads, writes):
        deps = {}
        for s, v in self.barrier_deps.items():
            if deps.get(s, 0) < v:
                deps[s] = v
        for b in reads:
            for s, v in b.writes.items():
                if deps.get(s, 0) < v:
                    deps[s] = v
        for b in writes:
            for d in (b.writes, b.readers):
                for s, v in d.items():
                    if deps.get(s, 0) < v:
                        deps[s] = v
        waits = []
        wt = self.waited[engine]
        for s in sorted(deps):
            v = deps[s]
            if self.semc.get(s, 0) < v:
                raise RuntimeError(f"dep on future ticket {s}={v}")
            if wt.get(s, 0) < v:
                waits.append((s, v))
                wt[s] = v
        return waits

    def op(self, engine, fn, reads=(), writes=(), tag=""):
        waits = self._collect(engine, reads, writes)
        sem = ENG_SEM[engine]
        self.semc[sem] += 1
        tk = (sem, self.semc[sem])
        for b in writes:
            b._merge(b.writes, tk)
        for b in reads:
            b._merge(b.readers, tk)
        self.ops[engine].append((waits, fn, sem, 1, tag))
        return tk

    def dma(self, engine, buf_sem_owner, fn, reads=(), writes=(), tag=""):
        sem = self._sem("d_" + buf_sem_owner)
        waits = self._collect(engine, reads, writes)
        self.semc[sem] += 16
        tk = (sem, self.semc[sem])
        for b in writes:
            b._merge(b.writes, tk)
        for b in reads:
            b._merge(b.readers, tk)
        self.ops[engine].append((waits, fn, sem, None, tag))
        return tk

    def raw_wait(self, engine, extra=()):
        deps = {}
        for s, v in extra:
            if deps.get(s, 0) < v:
                deps[s] = v
        waits = []
        wt = self.waited[engine]
        for s in sorted(deps):
            v = deps[s]
            if wt.get(s, 0) < v:
                waits.append((s, v))
                wt[s] = v
        self.ops[engine].append((waits, None, None, None, "final_wait"))

    def emit(self):
        nc = self.nc
        with nc.Block() as block:
            def emitter(engine):
                def run(eng):
                    for waits, fn, sem, amt, tag in self.ops[engine]:
                        for s, v in waits:
                            eng.wait_ge(self.semh[s], v)
                        if fn is None:
                            continue
                        if amt is None:
                            fn(eng, self.semh[sem])
                        else:
                            r = fn(eng)
                            r = r[-1] if isinstance(r, (list, tuple)) else r
                            r.then_inc(self.semh[sem], amt)
                return run
            block.tensor(emitter("tensor"))
            block.vector(emitter("vector"))
            block.scalar(emitter("scalar"))
            block.sync(emitter("sync"))
            block.gpsimd(emitter("gpsimd"))




def build_nc(cfg=CFG):
    nc = bass.Bass("TRN2", target_bir_lowering=False)
    stack = ExitStack()
    kb = KB(nc, stack)
    D, T, P, H, TS = cfg.D, cfg.T, cfg.P, cfg.H, cfg.TS
    KC, HC, HP = cfg.KC, cfg.HC, cfg.HP

    xT = nc.dram_tensor("xT", [D, T], F32, kind="ExternalInput")
    xTpre = nc.dram_tensor("xTpre", [D, P], F32, kind="ExternalInput")
    wnames = ("wkT", "wvT", "wqT", "woT")
    wT = {n: nc.dram_tensor(n, [D, D], F32, kind="ExternalInput") for n in wnames}
    biases = {n: nc.dram_tensor(n, [D], F32, kind="ExternalInput")
              for n in ("bq", "bk", "bv", "bo")}
    w1T = nc.dram_tensor("w1T", [D, HP], F32, kind="ExternalInput")
    w3T = nc.dram_tensor("w3T", [D, HP], F32, kind="ExternalInput")
    w2T = nc.dram_tensor("w2T", [HP, D], F32, kind="ExternalInput")
    pre_mask = nc.dram_tensor("pre_mask", [128, 1], F32, kind="ExternalInput")
    outT = kb.dram_buf("outT", [D, T], kind="ExternalOutput")
    x2T = kb.dram_buf("x2T", [D, T])
    awc = nc.dram_tensor("awc", [D, 8 * D], BF16)
    awc_blk = {(wi, ec): Buf(kb, awc, f"awc{wi}_{ec}")
               for wi in range(4) for ec in range(KC)}
    fqc = kb.dram_buf("fqc", [D, 2 * HP], BF16)
    w2qc = kb.dram_buf("w2qc", [HP, D], BF16)

    def r3(ap, p=128):
        return ap.rearrange("(kc p) t -> p kc t", p=p)

    xT_r = r3(xT.ap())
    xTpre_r = r3(xTpre.ap())
    wT_r = {n: r3(wT[n].ap()) for n in wnames}
    awc_r = r3(awc.ap())
    fqc_r = r3(fqc.h.ap())
    w2qc_r = w2qc.h.ap().rearrange("(hc p) d -> p hc d", p=128)
    w1T_r = r3(w1T.ap())
    w3T_r = r3(w3T.ap())
    w2T_r = w2T.ap().rearrange("(hc p) d -> p hc d", p=128)
    x2T_r = r3(x2T.h.ap())
    outT_r = r3(outT.h.ap())

    ones_c = kb.sbuf("ones_c", [128, 1])
    ones_r = kb.sbuf("ones_r", [1, 128])
    bias_sb = kb.sbuf("bias_sb", [128, 4 * KC])
    mask_sb = kb.sbuf("mask_sb", [128, 1])
    prefix = kb.sbuf("prefix", [128, KC])
    stat = kb.sbuf("stat", [1, 5 * TS])
    svals = kb.sbuf("svals", [128, 8])
    sval_tmp = kb.sbuf("sval_tmp", [1, 8])
    stats_cols = kb.sbuf("stats_cols", [128, 224])
    stats_red = kb.sbuf("stats_red", [128, 3])
    const_sb = kb.sbuf("const_sb", [128, 2])
    sq_pp = [kb.sbuf(f"sq{i}", [128, TS]) for i in range(2)]
    ln32_pp = [kb.sbuf(f"ln32_{i}", [128, TS]) for i in range(2)]
    xsg = [kb.sbuf(f"xsg{i}", [128, 2 * TS]) for i in range(2)]

    pb = [kb.psum(f"pb{i}", (128, TS)) for i in range(8)]

    kb.op("vector", lambda e: e.memset(ones_c.ap(), 1.0), writes=[ones_c], tag="init")
    kb.op("vector", lambda e: e.memset(ones_r.ap(), 1.0), writes=[ones_r], tag="init")
    kb.op("vector", lambda e: e.memset(prefix.ap(), 0.0), writes=[prefix], tag="init")
    kb.op("vector", lambda e: e.memset(const_sb.h[:, 0:1], cfg.EPS),
          writes=[const_sb], tag="init")
    kb.op("vector", lambda e: e.memset(const_sb.h[:, 1:2], cfg.RC),
          writes=[const_sb], tag="init")

    def load_simple(buf, out_ap, in_ap, tag=""):
        def fn(eng, semh, out_ap=out_ap, in_ap=in_ap):
            eng.dma_start(out=out_ap, in_=in_ap,
                          allow_slow_non_contiguous=True).then_inc(semh, 16)
        return kb.dma("sync", buf.name, fn, writes=[buf], tag=tag)

    for j, n in enumerate(("bq", "bk", "bv", "bo")):
        load_simple(bias_sb, bias_sb.h[:, j * KC:(j + 1) * KC],
                    biases[n].ap().rearrange("(kc p) -> p kc", p=128), tag="bias")
    load_simple(mask_sb, mask_sb.ap(), pre_mask.ap(), tag="mask")

    def split_pair(src_buf, src_ap, hi_buf, hi_ap, lo_buf, lo_ap, tag):
        kb.op("scalar", lambda e: e.copy(hi_ap, src_ap),
              reads=[src_buf], writes=[hi_buf], tag=tag + "_hi")
        kb.op("vector",
              lambda e: e.scalar_tensor_tensor(lo_ap, src_ap, 1.0, hi_ap,
                                               ALU.mult, ALU.subtract),
              reads=[src_buf, hi_buf], writes=[lo_buf], tag=tag + "_lo")

    ln_hi = kb.sbuf("ln_hi", [128, KC * TS], BF16)
    ln_lo = kb.sbuf("ln_lo", [128, KC * TS], BF16)

    stackA = ExitStack()
    zeros_ts = kb.sbuf("zeros_ts", [128, TS], stack=stackA)
    kb.op("vector", lambda e: e.memset(zeros_ts.ap(), 0.0), writes=[zeros_ts],
          tag="init")

    f0_pp = [kb.sbuf(f"f0_{i}", [128, 1376], stack=stackA) for i in range(2)]
    ld_pp = [kb.sbuf(f"bgld{i}", [128, KC * 128], stack=stackA) for i in range(2)]
    q_pp = [kb.sbuf(f"bgq{i}", [128, KC * 128], BF16, stack=stackA)
            for i in range(2)]

    def bg_stream():
        pend = [None]

        def flush():
            if pend[0] is not None:
                pend[0]()
                pend[0] = None

        worder = [(wi, n, ec) for ec in range(KC) for wi, n in ((0, "wkT"),
                                                               (1, "wvT"))]
        worder += [(2, "wqT", ec) for ec in range(KC)]
        worder += [(3, "woT", ec) for ec in range(KC)]
        for i, (wi, n, ec) in enumerate(worder):
            buf = ld_pp[i % 2]

            def fnl(eng, semh, n=n, ec=ec, buf=buf):
                eng.dma_start(out=buf.ap(), in_=wT_r[n][:, :, ec * 128:(ec + 1) * 128]
                              ).then_inc(semh, 16)
            kb.dma("sync", buf.name, fnl, writes=[buf], tag=f"wspl_{n}{ec}")
            flush()

            def compute(wi=wi, n=n, ec=ec, buf=buf):
                split_pair(buf, buf.ap(), q_pp[0], q_pp[0].ap(),
                           q_pp[1], q_pp[1].ap(), f"wsp_{n}{ec}")
                base = ec * 1024 + wi * 256

                def fsh(eng, semh, base=base):
                    eng.dma_start(out=awc_r[:, :, base:base + 128],
                                  in_=q_pp[0].h.ap().rearrange(
                                      "p (kc c) -> p kc c", kc=KC)
                                  ).then_inc(semh, 16)
                kb.dma("gpsimd", q_pp[0].name, fsh, reads=[q_pp[0]],
                       writes=[awc_blk[(wi, ec)]], tag=f"wsps_{n}{ec}h")

                def fsl(eng, semh, base=base):
                    eng.dma_start(out=awc_r[:, :, base + 128:base + 256],
                                  in_=q_pp[1].h.ap().rearrange(
                                      "p (kc c) -> p kc c", kc=KC)
                                  ).then_inc(semh, 16)
                kb.dma("gpsimd", q_pp[1].name, fsl, reads=[q_pp[1]],
                       writes=[awc_blk[(wi, ec)]], tag=f"wsps_{n}{ec}l")
            pend[0] = compute
            yield
        flush()

        F0W = 1376
        col = 0
        f0_ncols = {0: [], 1: [], 2: []}
        for wi, wr, tot in ((0, w1T_r, HP), (1, w3T_r, HP), (2, w2T_r, D)):
            nch = wr.shape[1]
            for kc in range(nch):
                for o0 in range(0, tot, F0W):
                    width = min(o0 + F0W, tot) - o0
                    buf = f0_pp[col % 2]
                    src = wr[:, kc, o0:o0 + width]

                    def fnl(eng, semh, src=src, width=width, buf=buf):
                        eng.dma_start(out=buf.h[:, 0:width], in_=src
                                      ).then_inc(semh, 16)
                    kb.dma("sync", buf.name, fnl, writes=[buf], tag="f0_load")
                    flush()

                    def compute(width=width, col=col, buf=buf):
                        kb.op("vector", lambda e: e.reduce_sum(
                                  stats_cols.h[:, col:col + 1],
                                  buf.h[:, 0:width], axis=mybir.AxisListType.X,
                                  apply_absolute_value=True),
                              reads=[buf], writes=[stats_cols], tag="f0_red")
                    pend[0] = compute
                    f0_ncols[wi].append(col)
                    col += 1
                    yield
        flush()

        for wi in range(3):
            cols = f0_ncols[wi]
            lo_, hi_ = min(cols), max(cols) + 1
            kb.op("vector", lambda e, wi=wi, lo_=lo_, hi_=hi_: e.reduce_sum(
                      stats_red.h[:, wi:wi + 1], stats_cols.h[:, lo_:hi_],
                      axis=mybir.AxisListType.X),
                  reads=[stats_cols], writes=[stats_red], tag=f"f0red{wi}")
            kb.op("tensor", lambda e, wi=wi: e.matmul(
                      pb[0].h[0:1, wi:wi + 1], ones_c.ap(),
                      stats_red.h[:, wi:wi + 1], start=True, stop=True),
                  reads=[stats_red, ones_c], writes=[pb[0]], tag=f"f0tot{wi}")
        cnt = float(D) * H
        kb.op("vector", lambda e: e.tensor_scalar_mul(
                  sval_tmp.h[0:1, 0:3], pb[0].h[0:1, 0:3], 1.0 / cnt),
              reads=[pb[0]], writes=[sval_tmp], tag="f0_s")
        kb.op("vector", lambda e: e.reciprocal(sval_tmp.h[0:1, 3:6],
                                               sval_tmp.h[0:1, 0:3]),
              reads=[sval_tmp], writes=[sval_tmp], tag="f0_invs")
        kb.op("tensor", lambda e: e.matmul(pb[1].h[:, 0:6], ones_r.ap(),
                                           sval_tmp.h[0:1, 0:6],
                                           start=True, stop=True),
              reads=[sval_tmp, ones_r], writes=[pb[1]], tag="f0_bc")
        kb.op("vector", lambda e: e.tensor_copy(svals.h[:, 0:6], pb[1].h[:, 0:6]),
              reads=[pb[1]], writes=[svals], tag="f0_sv")
        yield

        def q_step(i, in_ap_fn, width, wi, store_fn, tag):
            buf = ld_pp[i % 2]
            qb = q_pp[i % 2]

            def fl(eng, semh, buf=buf):
                eng.dma_start(out=buf.h[:, 0:width].rearrange(
                                  "p (n c) -> p n c", c=128),
                              in_=in_ap_fn()).then_inc(semh, 16)
            kb.dma("sync", buf.name, fl, writes=[buf], tag=tag + "_l")
            flush()

            def compute(buf=buf, qb=qb, width=width, wi=wi):
                inv = svals.h[:, 3 + wi:4 + wi]
                sap = buf.h[:, 0:width]
                qap = qb.h[:, 0:width]
                kb.op("scalar", lambda e: e.activation(
                          sap, sap, AF.Identity, bias=const_sb.h[:, 1:2],
                          scale=inv),
                      reads=[buf, svals, const_sb], writes=[buf], tag=tag + "_q1")
                kb.op("vector", lambda e: e.tensor_scalar(
                          sap, sap, cfg.RC, 1.0, ALU.subtract, ALU.min),
                      reads=[buf], writes=[buf], tag=tag + "_q2")
                kb.op("vector", lambda e: e.tensor_scalar(
                          qap, sap, -1.0, None, ALU.max),
                      reads=[buf], writes=[qb], tag=tag + "_q3")

                def fs(eng, semh, qb=qb):
                    eng.dma_start(out=store_fn(),
                                  in_=qb.h[:, 0:width].rearrange(
                                      "p (n c) -> p n c", c=128)
                                  ).then_inc(semh, 16)
                kb.dma("gpsimd", qb.name, fs, reads=[qb], writes=[fqc, w2qc],
                       tag=tag + "_s")
            pend[0] = compute

        i = 0
        for hc in range(HC):
            for wi, wr in ((0, w1T_r), (1, w3T_r)):
                q_step(i,
                       lambda hc=hc, wr=wr: wr[:, :, hc * 128:(hc + 1) * 128],
                       KC * 128, wi,
                       lambda hc=hc, wi=wi: fqc_r[
                           :, :, hc * 256 + wi * 128:hc * 256 + (wi + 1) * 128],
                       f"q13_{hc}_{wi}")
                i += 1
                yield
        HCW = KC
        for dc in range(KC):
            for h0 in range(0, HC, HCW):
                h1_ = min(h0 + HCW, HC)
                w_ = (h1_ - h0) * 128
                q_step(i,
                       lambda dc=dc, h0=h0, h1_=h1_: w2T_r[
                           :, h0:h1_, dc * 128:(dc + 1) * 128],
                       w_, 2,
                       lambda dc=dc, h0=h0, h1_=h1_: w2qc_r[
                           :, h0:h1_, dc * 128:(dc + 1) * 128],
                       f"q2_{dc}_{h0}")
                i += 1
                yield
        flush()

    bg = bg_stream()

    def bg_step(n=1):
        for _ in range(n):
            if next(bg, "END") == "END":
                return False
        return True

    def ln_stats_step(get_x, kc, tag):
        xb, xap = get_x(kc)
        sq = sq_pp[kc % 2]
        kb.op("scalar", lambda e, xap=xap, sq=sq: e.activation(
                  sq.ap(), xap, AF.Square),
              reads=[xb], writes=[sq], tag=tag + "_sq")

        def fn_mm(e, xap=xap, sq=sq, kc=kc):
            return [
                e.matmul(pb[0].h[0:1, 0:TS], ones_c.ap(), xap,
                         start=(kc == 0), stop=(kc == KC - 1)),
                e.matmul(pb[1].h[0:1, 0:TS], ones_c.ap(), sq.ap(),
                         start=(kc == 0), stop=(kc == KC - 1)),
            ]
        kb.op("tensor", fn_mm, reads=[xb, sq, ones_c], writes=[pb[0], pb[1]],
              tag=tag + "_sums")

    def ln_finish(get_x, lnhi, lnlo, tag):
        m = stat.h[0:1, 0:TS]
        esq = stat.h[0:1, TS:2 * TS]
        m2 = stat.h[0:1, 2 * TS:3 * TS]
        var = stat.h[0:1, 3 * TS:4 * TS]
        sd = stat.h[0:1, 2 * TS:3 * TS]
        rr = stat.h[0:1, 4 * TS:5 * TS]
        kb.op("vector", lambda e: e.tensor_scalar_mul(m, pb[0].h[0:1, 0:TS], 1.0 / D),
              reads=[pb[0]], writes=[stat], tag=tag + "_m")
        kb.op("vector", lambda e: e.tensor_scalar_mul(esq, pb[1].h[0:1, 0:TS],
                                                      1.0 / D),
              reads=[pb[1]], writes=[stat], tag=tag + "_esq")
        kb.op("vector", lambda e: e.tensor_mul(m2, m, m), reads=[stat], writes=[stat],
              tag=tag + "_m2")
        kb.op("vector", lambda e: e.tensor_sub(var, esq, m2), reads=[stat],
              writes=[stat], tag=tag + "_var")
        kb.op("scalar", lambda e: e.activation(sd, var, AF.Sqrt,
                                               bias=const_sb.h[0:1, 0:1]),
              reads=[stat, const_sb], writes=[stat], tag=tag + "_sd")
        kb.op("vector", lambda e: e.reciprocal(rr, sd),
              reads=[stat], writes=[stat], tag=tag + "_rstd")
        kb.op("tensor", lambda e: e.matmul(pb[2].h[:, 0:TS], ones_r.ap(), m,
                                           start=True, stop=True),
              reads=[stat, ones_r], writes=[pb[2]], tag=tag + "_bm")
        kb.op("tensor", lambda e: e.matmul(pb[3].h[:, 0:TS], ones_r.ap(), rr,
                                           start=True, stop=True),
              reads=[stat, ones_r], writes=[pb[3]], tag=tag + "_br")
        for kc in range(KC):
            xb, xap = get_x(kc)
            t1 = sq_pp[kc % 2]
            l32 = ln32_pp[kc % 2]
            kb.op("vector", lambda e, xap=xap, t1=t1: e.tensor_sub(
                      t1.ap(), xap, pb[2].h[:, 0:TS]),
                  reads=[xb, pb[2]], writes=[t1], tag=tag + "_ap1")
            kb.op("vector", lambda e, t1=t1, l32=l32: e.tensor_mul(
                      l32.ap(), t1.ap(), pb[3].h[:, 0:TS]),
                  reads=[t1, pb[3]], writes=[l32], tag=tag + "_ap2")
            split_pair(l32, l32.ap(), lnhi, lnhi.h[:, kc * TS:(kc + 1) * TS],
                       lnlo, lnlo.h[:, kc * TS:(kc + 1) * TS], tag + f"_sp{kc}")

    def layer_norm(get_x, lnhi, lnlo, tag):
        for kc in range(KC):
            ln_stats_step(get_x, kc, tag)
        ln_finish(get_x, lnhi, lnlo, tag)

    ctx_hi = kb.sbuf("ctx_hi", [128, KC * TS], BF16, stack=stackA)
    ctx_lo = kb.sbuf("ctx_lo", [128, KC * TS], BF16, stack=stackA)
    wkv_col = [kb.sbuf(f"wkv{i}", [128, KC * 512], BF16, stack=stackA)
               for i in range(2)]
    wq_col = [kb.sbuf(f"wq{i}", [128, KC * 256], BF16, stack=stackA)
              for i in range(2)]
    t_min = kb.sbuf("t_min", [128, TS], stack=stackA)
    t_exp = kb.sbuf("t_exp", [128, TS], stack=stackA)
    t_phik = kb.sbuf("t_phik", [128, TS], stack=stackA)
    t_kv = kb.sbuf("t_kv", [128, TS], stack=stackA)
    t_cum = kb.sbuf("t_cum", [128, TS], stack=stackA)
    t_ctx = kb.sbuf("t_ctx", [128, TS], stack=stackA)
    x2o = [kb.sbuf(f"x2o{i}", [128, 2 * TS], stack=stackA) for i in range(2)]

    def make_x_stream(src_r, c0, tag):
        def get(kc):
            xs = xsg[(kc // 2) % 2]
            if kc % 2 == 0:
                def fl(eng, semh, kc=kc, xs=xs):
                    eng.dma_start(
                        out=xs.h.ap().rearrange("p (g t) -> p g t", g=2),
                        in_=src_r[:, kc:kc + 2, c0:c0 + TS]).then_inc(semh, 16)
                kb.dma("sync", xs.name, fl, writes=[xs], tag=tag + f"_{kc}")
            return xs, xs.h[:, (kc % 2) * TS:(kc % 2 + 1) * TS]
        return get

    def load_wcol(buf, ec, blocks, base, width, tag):
        for wi in blocks:
            assert awc_blk[(wi, ec)].writes, f"awc block ({wi},{ec}) not written"

        def fh(eng, semh, base=base, width=width, buf=buf):
            eng.dma_start(out=buf.h.ap().rearrange("p (kc c) -> p kc c", kc=KC),
                          in_=awc_r[:, :, base:base + width]).then_inc(semh, 16)
        kb.dma("gpsimd", buf.name, fh,
               reads=[awc_blk[(wi, ec)] for wi in blocks], writes=[buf], tag=tag)

    def proj3(psum_b, buf, stride, off, a_hi_b, a_lo_b, tag):
        def fn(e, buf=buf, psum_b=psum_b):
            r = []
            for kc in range(KC):
                w_hi = buf.h[:, kc * stride + off:kc * stride + off + 128]
                w_lo = buf.h[:, kc * stride + off + 128:kc * stride + off + 256]
                a_hi = a_hi_b.h[:, kc * TS:(kc + 1) * TS]
                a_lo = a_lo_b.h[:, kc * TS:(kc + 1) * TS]
                r.append(e.matmul(psum_b.h[:, 0:TS], w_hi, a_hi,
                                  start=(kc == 0), stop=False))
                r.append(e.matmul(psum_b.h[:, 0:TS], w_hi, a_lo,
                                  start=False, stop=False))
                r.append(e.matmul(psum_b.h[:, 0:TS], w_lo, a_hi,
                                  start=False, stop=(kc == KC - 1)))
            return r
        kb.op("tensor", fn, reads=[buf, a_hi_b, a_lo_b], writes=[psum_b], tag=tag)

    def phi_chain(psum_b, bias_col, out_phi, tag):
        bap = bias_sb.h[:, bias_col:bias_col + 1]
        kb.op("vector", lambda e: e.tensor_scalar(
                  t_min.ap(), psum_b.h[:, 0:TS], bap, 0.0, ALU.add, ALU.min),
              reads=[psum_b, bias_sb], writes=[t_min], tag=tag + "_min")
        kb.op("scalar", lambda e: e.activation(t_exp.ap(), t_min.ap(), AF.Exp),
              reads=[t_min], writes=[t_exp], tag=tag + "_exp")
        kb.op("scalar", lambda e: e.activation(t_min.ap(), psum_b.h[:, 0:TS],
                                               AF.Relu, bias=bap),
              reads=[psum_b, bias_sb], writes=[t_min], tag=tag + "_relu")
        kb.op("vector", lambda e: e.tensor_add(out_phi.ap(), t_exp.ap(), t_min.ap()),
              reads=[t_exp, t_min], writes=[out_phi], tag=tag + "_phi")

    bg_step(4)
    n_slices = cfg.n_pre + cfg.n_own

    def slice_src(s):
        is_pre = s < cfg.n_pre
        c0 = (s - (0 if is_pre else cfg.n_pre)) * TS
        return is_pre, c0, (xTpre_r if is_pre else xT_r)

    for s in range(n_slices):
        is_pre, c0, src_r = slice_src(s)

        if s == 0:
            layer_norm(make_x_stream(src_r, c0, "xs0"), ln_hi, ln_lo, "ln1_s0")
        if s + 1 < n_slices:
            _, c1, src1 = slice_src(s + 1)
            gx_next = make_x_stream(src1, c1, f"xs{s + 1}")
        else:
            gx_next = make_x_stream(x2T_r, 0, "x2s0")

        for ec in range(KC):
            ln_stats_step(gx_next, ec, f"lnn_s{s + 1}")
            bg_step(2)
            wkv = wkv_col[ec % 2]
            load_wcol(wkv, ec, (0, 1), ec * 1024, 512, f"wkv_s{s}e{ec}")
            if not is_pre:
                wq = wq_col[ec % 2]
                load_wcol(wq, ec, (2,), ec * 1024 + 512, 256, f"wq_s{s}e{ec}")

            proj3(pb[5], wkv, 512, 0, ln_hi, ln_lo, f"mmk_s{s}e{ec}")
            proj3(pb[6], wkv, 512, 256, ln_hi, ln_lo, f"mmv_s{s}e{ec}")
            if not is_pre:
                proj3(pb[4], wq, 256, 0, ln_hi, ln_lo, f"mmq_s{s}e{ec}")

            phi_chain(pb[5], KC + ec, t_phik, f"pk_s{s}e{ec}")
            kb.op("vector", lambda e, ec=ec: e.scalar_tensor_tensor(
                      t_kv.ap(), pb[6].h[:, 0:TS],
                      bias_sb.h[:, 2 * KC + ec:2 * KC + ec + 1],
                      t_phik.ap(), ALU.add, ALU.mult),
                  reads=[pb[6], bias_sb, t_phik], writes=[t_kv], tag=f"kv_s{s}e{ec}")
            kb.op("vector", lambda e, ec=ec: e.tensor_tensor_scan(
                      t_cum.ap(), t_kv.ap(), zeros_ts.ap(), prefix.h[:, ec:ec + 1],
                      ALU.add, ALU.add),
                  reads=[t_kv, zeros_ts, prefix], writes=[t_cum],
                  tag=f"scan_s{s}e{ec}")
            if is_pre:
                kb.op("vector", lambda e, ec=ec: e.tensor_scalar_mul(
                          prefix.h[:, ec:ec + 1], t_cum.h[:, TS - 1:TS],
                          mask_sb.h[:, 0:1]),
                      reads=[t_cum, mask_sb], writes=[prefix], tag=f"pfx_s{s}e{ec}")
            else:
                kb.op("vector", lambda e, ec=ec: e.tensor_copy(
                          prefix.h[:, ec:ec + 1], t_cum.h[:, TS - 1:TS]),
                      reads=[t_cum], writes=[prefix], tag=f"pfx_s{s}e{ec}")

            if not is_pre:
                phi_chain(pb[4], ec, t_phik, f"pq_s{s}e{ec}")
                kb.op("vector", lambda e: e.scalar_tensor_tensor(
                          t_ctx.ap(), t_cum.ap(), cfg.SCALE, t_phik.ap(),
                          ALU.mult, ALU.mult),
                      reads=[t_cum, t_phik], writes=[t_ctx], tag=f"ctx_s{s}e{ec}")
                split_pair(t_ctx, t_ctx.ap(),
                           ctx_hi, ctx_hi.h[:, ec * TS:(ec + 1) * TS],
                           ctx_lo, ctx_lo.h[:, ec * TS:(ec + 1) * TS],
                           f"ctxsp_s{s}e{ec}")

        if s + 1 < n_slices:
            ln_finish(gx_next, ln_hi, ln_lo, f"lnn_s{s + 1}")

        if not is_pre:
            get_xr = make_x_stream(src_r, c0, f"xr{s}")
            for e2c in range(KC):
                wo = wq_col[e2c % 2]
                load_wcol(wo, e2c, (3,), e2c * 1024 + 768, 256, f"wo_s{s}e{e2c}")
                proj3(pb[7], wo, 256, 0, ctx_hi, ctx_lo, f"mmo_s{s}e{e2c}")
                xres, xres_ap = get_xr(e2c)
                xo = x2o[(e2c // 2) % 2]
                sub = (e2c % 2) * TS
                kb.op("vector", lambda e, e2c=e2c, xo=xo, sub=sub, xres=xres,
                      xres_ap=xres_ap: e.scalar_tensor_tensor(
                          xo.h[:, sub:sub + TS], pb[7].h[:, 0:TS],
                          bias_sb.h[:, 3 * KC + e2c:3 * KC + e2c + 1],
                          xres_ap, ALU.add, ALU.add),
                      reads=[pb[7], bias_sb, xres], writes=[xo],
                      tag=f"x2_s{s}e{e2c}")
                if e2c % 2 == 1:
                    def fn_st(eng, semh, xo=xo, e2c=e2c, c0=c0):
                        eng.dma_start(
                            out=x2T_r[:, e2c - 1:e2c + 1, c0:c0 + TS],
                            in_=xo.h.ap().rearrange("p (g t) -> p g t", g=2)
                            ).then_inc(semh, 16)
                    kb.dma("gpsimd", xo.name, fn_st, reads=[xo], writes=[x2T],
                           tag=f"x2st_s{s}e{e2c}")

    ln_finish(gx_next, ln_hi, ln_lo, "lnn_f0")

    while bg_step():
        pass

    kb.barrier()
    stackA.close()

    g_hi = kb.sbuf("g_hi", [128, HC * TS], BF16)
    g_lo = kb.sbuf("g_lo", [128, HC * TS], BF16)
    q13c = [kb.sbuf(f"q13c{i}", [128, KC * 256], BF16) for i in range(2)]
    w2qb = [kb.sbuf(f"w2qb{i}", [128, HC * 128], BF16) for i in range(2)]
    silu_b = kb.sbuf("silu_b", [128, TS])
    g32 = kb.sbuf("g32", [128, TS])
    x2rd = [kb.sbuf(f"x2rd{i}", [128, TS]) for i in range(2)]
    ob = [kb.sbuf(f"ob{i}", [128, 2 * TS]) for i in range(2)]

    for fs in range(cfg.n_own):
        c0 = fs * TS
        if fs + 1 < cfg.n_own:
            gx2_next = make_x_stream(x2T_r, (fs + 1) * TS, f"x2s{fs + 1}")

        for hc in range(HC):
            if fs + 1 < cfg.n_own and hc < KC:
                ln_stats_step(gx2_next, hc, f"ln2_s{fs + 1}")
            qb = q13c[hc % 2]

            def fn_q13(eng, semh, qb=qb, hc=hc):
                eng.dma_start(out=qb.h.ap().rearrange("p (kc c) -> p kc c", kc=KC),
                              in_=fqc_r[:, :, hc * 256:(hc + 1) * 256]
                              ).then_inc(semh, 16)
            kb.dma("gpsimd", qb.name, fn_q13, reads=[fqc], writes=[qb],
                   tag=f"q13_{fs}_{hc}")

            ph1 = pb[4 + (hc % 2) * 2]
            ph3 = pb[5 + (hc % 2) * 2]

            def fn_h(e, qb, p, off):
                r = []
                for kc in range(KC):
                    w_ap = qb.h[:, kc * 256 + off:kc * 256 + off + 128]
                    r.append(e.matmul(p.h[:, 0:TS], w_ap,
                                      ln_hi.h[:, kc * TS:(kc + 1) * TS],
                                      start=(kc == 0), stop=False))
                    r.append(e.matmul(p.h[:, 0:TS], w_ap,
                                      ln_lo.h[:, kc * TS:(kc + 1) * TS],
                                      start=False, stop=(kc == KC - 1)))
                return r
            kb.op("tensor", lambda e, qb=qb, p=ph1: fn_h(e, qb, p, 0),
                  reads=[qb, ln_hi, ln_lo], writes=[ph1], tag=f"mmh1_{fs}_{hc}")
            kb.op("tensor", lambda e, qb=qb, p=ph3: fn_h(e, qb, p, 128),
                  reads=[qb, ln_hi, ln_lo], writes=[ph3], tag=f"mmh3_{fs}_{hc}")

            kb.op("scalar", lambda e, ph1=ph1: e.activation(
                      silu_b.ap(), ph1.h[:, 0:TS], AF.Silu, scale=svals.h[:, 0:1]),
                  reads=[ph1, svals], writes=[silu_b], tag=f"silu_{fs}_{hc}")
            kb.op("vector", lambda e, ph3=ph3: e.scalar_tensor_tensor(
                      g32.ap(), ph3.h[:, 0:TS], svals.h[:, 1:2], silu_b.ap(),
                      ALU.mult, ALU.mult),
                  reads=[ph3, svals, silu_b], writes=[g32], tag=f"g_{fs}_{hc}")
            split_pair(g32, g32.ap(), g_hi, g_hi.h[:, hc * TS:(hc + 1) * TS],
                       g_lo, g_lo.h[:, hc * TS:(hc + 1) * TS], f"gsp_{fs}_{hc}")

        if fs + 1 < cfg.n_own:
            ln_finish(gx2_next, ln_hi, ln_lo, f"ln2_s{fs + 1}")

        for dc in range(KC):
            wq2 = w2qb[dc % 2]

            def fn_w2(eng, semh, wq2=wq2, dc=dc):
                eng.dma_start(out=wq2.h.ap().rearrange("p (hc c) -> p hc c", c=128),
                              in_=w2qc_r[:, :, dc * 128:(dc + 1) * 128]
                              ).then_inc(semh, 16)
            kb.dma("gpsimd", wq2.name, fn_w2, reads=[w2qc], writes=[wq2],
                   tag=f"w2q_{fs}_{dc}")

            xr = x2rd[dc % 2]

            def fn_xr(eng, semh, xr=xr, dc=dc, c0=c0):
                eng.dma_start(out=xr.ap(), in_=x2T_r[:, dc, c0:c0 + TS]
                              ).then_inc(semh, 16)
            kb.dma("sync", xr.name, fn_xr, reads=[x2T], writes=[xr],
                   tag=f"xr{fs}_{dc}")

            pf = pb[dc % 4]

            def fn_f(e, pf=pf, wq2=wq2):
                r = []
                for hc in range(HC):
                    w_ap = wq2.h[:, hc * 128:(hc + 1) * 128]
                    r.append(e.matmul(pf.h[:, 0:TS], w_ap,
                                      g_hi.h[:, hc * TS:(hc + 1) * TS],
                                      start=(hc == 0), stop=False))
                    r.append(e.matmul(pf.h[:, 0:TS], w_ap,
                                      g_lo.h[:, hc * TS:(hc + 1) * TS],
                                      start=False, stop=(hc == HC - 1)))
                return r
            kb.op("tensor", fn_f, reads=[wq2, g_hi, g_lo], writes=[pf],
                  tag=f"mmf{fs}_{dc}")

            o = ob[(dc // 2) % 2]
            sub = (dc % 2) * TS
            kb.op("vector", lambda e, pf=pf, xr=xr, o=o, sub=sub:
                  e.scalar_tensor_tensor(
                      o.h[:, sub:sub + TS], pf.h[:, 0:TS], svals.h[:, 2:3],
                      xr.ap(), ALU.mult, ALU.add),
                  reads=[pf, svals, xr], writes=[o], tag=f"out{fs}_{dc}")
            if dc % 2 == 1:
                def fn_ost(eng, semh, o=o, dc=dc, c0=c0):
                    eng.dma_start(out=outT_r[:, dc - 1:dc + 1, c0:c0 + TS],
                                  in_=o.h.ap().rearrange("p (g t) -> p g t", g=2)
                                  ).then_inc(semh, 16)
                kb.dma("gpsimd", o.name, fn_ost, reads=[o], writes=[outT],
                       tag=f"ost{fs}_{dc}")

    kb.raw_wait("gpsimd", extra=[(s, kb.semc[s]) for s in
                                 ("d_ob0", "d_ob1", "d_x2o0", "d_x2o1")
                                 if s in kb.semc])
    kb.emit()
    return nc




def kernel(L0_attn_x, L0_attn_Wq, L0_attn_bq, L0_attn_Wk, L0_attn_bk,
           L0_attn_Wv, L0_attn_bv, L0_attn_Wo, L0_attn_bo,
           L0_ffn_w1, L0_ffn_w3, L0_ffn_w2):
    cfg = CFG
    D, T, P, H, HP = cfg.D, cfg.T, cfg.P, cfg.H, cfg.HP
    x = np.ascontiguousarray(np.asarray(L0_attn_x, dtype=np.float32))
    B, L, _ = x.shape

    def t32(a):
        return np.ascontiguousarray(np.asarray(a, dtype=np.float32).T)

    wqT, wkT, wvT, woT = (t32(L0_attn_Wq), t32(L0_attn_Wk),
                          t32(L0_attn_Wv), t32(L0_attn_Wo))
    w1T = np.zeros((D, HP), np.float32)
    w1T[:, :H] = t32(L0_ffn_w1)
    w3T = np.zeros((D, HP), np.float32)
    w3T[:, :H] = t32(L0_ffn_w3)
    w2T = np.zeros((HP, D), np.float32)
    w2T[:H, :] = t32(L0_ffn_w2)
    zeros_pre = np.zeros((D, P), np.float32)

    in_maps = []
    for c in range(8):
        b, h = c // 2, c % 2
        xTc = np.ascontiguousarray(x[b, h * T:(h + 1) * T, :].T)
        if h == 1:
            xTpre = np.ascontiguousarray(x[b, 0:P, :].T)
            mask = np.ones((128, 1), np.float32)
        else:
            xTpre = zeros_pre
            mask = np.zeros((128, 1), np.float32)
        in_maps.append({
            "xT": xTc, "xTpre": xTpre, "pre_mask": mask,
            "wqT": wqT, "wkT": wkT, "wvT": wvT, "woT": woT,
            "bq": np.asarray(L0_attn_bq, np.float32),
            "bk": np.asarray(L0_attn_bk, np.float32),
            "bv": np.asarray(L0_attn_bv, np.float32),
            "bo": np.asarray(L0_attn_bo, np.float32),
            "w1T": w1T, "w3T": w3T, "w2T": w2T,
        })

    nc = build_nc(cfg)
    res = run_bass_kernel_spmd(nc, in_maps, core_ids=list(range(8)))

    out = np.empty((B, L, D), np.float32)
    for c in range(8):
        b, h = c // 2, c % 2
        out[b, h * T:(h + 1) * T, :] = np.asarray(res.results[c]["outT"]).T
    return out


# version 24
# speedup vs baseline: 1.0007x; 1.0007x over previous
from contextlib import ExitStack

import numpy as np

import concourse.bass as bass
import concourse.mybir as mybir
from concourse.bass_utils import run_bass_kernel_spmd

F32 = mybir.dt.float32
BF16 = mybir.dt.bfloat16
AF = mybir.ActivationFunctionType
ALU = mybir.AluOpType


class Cfg:
    def __init__(self, D=2048, T=2048, P=2048, H=5461, TS=512):
        self.D, self.T, self.P, self.H, self.TS = D, T, P, H, TS
        self.KC = D // 128
        self.HC = (H + 127) // 128
        self.HP = self.HC * 128
        self.n_pre = P // TS
        self.n_own = T // TS
        self.SCALE = 128.0 ** -0.5
        self.EPS = 1e-5
        self.RC = 12582912.0


CFG = Cfg()

ENGINES = ("tensor", "vector", "scalar", "sync", "gpsimd")
ENG_SEM = {e: "E_" + e for e in ENGINES}


class Buf:

    def __init__(self, kb, handle, name):
        self.kb = kb
        self.h = handle
        self.name = name
        self.writes = {}
        self.readers = {}

    def ap(self):
        return self.h[:]

    def _merge(self, d, tk):
        s, v = tk
        if d.get(s, 0) < v:
            d[s] = v


class KB:
    def __init__(self, nc, stack):
        self.nc = nc
        self.stack = stack
        self.semh = {}
        self.semc = {}
        self.ops = {e: [] for e in ENGINES}
        self.waited = {e: {} for e in ENGINES}
        self.barrier_deps = {}
        for e in ENGINES:
            self._sem(ENG_SEM[e])

    def _sem(self, name):
        if name not in self.semh:
            self.semh[name] = self.stack.enter_context(self.nc.semaphore(name))
            self.semc[name] = 0
        return name

    def sbuf(self, name, shape, dtype=F32, stack=None):
        h = (stack or self.stack).enter_context(
            self.nc.sbuf_tensor(name, list(shape), dtype))
        return Buf(self, h, name)

    def psum(self, name, shape=(128, 512), dtype=F32):
        h = self.stack.enter_context(self.nc.psum_tensor(name, list(shape), dtype))
        return Buf(self, h, name)

    def dram_buf(self, name, shape, dtype=F32, kind=None):
        if kind is None:
            h = self.nc.dram_tensor(name, list(shape), dtype)
        else:
            h = self.nc.dram_tensor(name, list(shape), dtype, kind=kind)
        return Buf(self, h, name)

    def barrier(self):
        self.barrier_deps = dict(self.semc)

    def _collect(self, engine, reads, writes):
        deps = {}
        for s, v in self.barrier_deps.items():
            if deps.get(s, 0) < v:
                deps[s] = v
        for b in reads:
            for s, v in b.writes.items():
                if deps.get(s, 0) < v:
                    deps[s] = v
        for b in writes:
            for d in (b.writes, b.readers):
                for s, v in d.items():
                    if deps.get(s, 0) < v:
                        deps[s] = v
        waits = []
        wt = self.waited[engine]
        for s in sorted(deps):
            v = deps[s]
            if self.semc.get(s, 0) < v:
                raise RuntimeError(f"dep on future ticket {s}={v}")
            if wt.get(s, 0) < v:
                waits.append((s, v))
                wt[s] = v
        return waits

    def op(self, engine, fn, reads=(), writes=(), tag=""):
        waits = self._collect(engine, reads, writes)
        sem = ENG_SEM[engine]
        self.semc[sem] += 1
        tk = (sem, self.semc[sem])
        for b in writes:
            b._merge(b.writes, tk)
        for b in reads:
            b._merge(b.readers, tk)
        self.ops[engine].append((waits, fn, sem, 1, tag))
        return tk

    def dma(self, engine, buf_sem_owner, fn, reads=(), writes=(), tag=""):
        sem = self._sem("d_" + buf_sem_owner)
        waits = self._collect(engine, reads, writes)
        self.semc[sem] += 16
        tk = (sem, self.semc[sem])
        for b in writes:
            b._merge(b.writes, tk)
        for b in reads:
            b._merge(b.readers, tk)
        self.ops[engine].append((waits, fn, sem, None, tag))
        return tk

    def raw_wait(self, engine, extra=()):
        deps = {}
        for s, v in extra:
            if deps.get(s, 0) < v:
                deps[s] = v
        waits = []
        wt = self.waited[engine]
        for s in sorted(deps):
            v = deps[s]
            if wt.get(s, 0) < v:
                waits.append((s, v))
                wt[s] = v
        self.ops[engine].append((waits, None, None, None, "final_wait"))

    def emit(self):
        nc = self.nc
        with nc.Block() as block:
            def emitter(engine):
                def run(eng):
                    for waits, fn, sem, amt, tag in self.ops[engine]:
                        for s, v in waits:
                            eng.wait_ge(self.semh[s], v)
                        if fn is None:
                            continue
                        if amt is None:
                            fn(eng, self.semh[sem])
                        else:
                            r = fn(eng)
                            r = r[-1] if isinstance(r, (list, tuple)) else r
                            r.then_inc(self.semh[sem], amt)
                return run
            block.tensor(emitter("tensor"))
            block.vector(emitter("vector"))
            block.scalar(emitter("scalar"))
            block.sync(emitter("sync"))
            block.gpsimd(emitter("gpsimd"))




def build_nc(cfg=CFG):
    nc = bass.Bass("TRN2", target_bir_lowering=False)
    stack = ExitStack()
    kb = KB(nc, stack)
    D, T, P, H, TS = cfg.D, cfg.T, cfg.P, cfg.H, cfg.TS
    KC, HC, HP = cfg.KC, cfg.HC, cfg.HP

    xT = nc.dram_tensor("xT", [D, T], F32, kind="ExternalInput")
    xTpre = nc.dram_tensor("xTpre", [D, P], F32, kind="ExternalInput")
    wnames = ("wkT", "wvT", "wqT", "woT")
    wT = {n: nc.dram_tensor(n, [D, D], F32, kind="ExternalInput") for n in wnames}
    biases = {n: nc.dram_tensor(n, [D], F32, kind="ExternalInput")
              for n in ("bq", "bk", "bv", "bo")}
    w1T = nc.dram_tensor("w1T", [D, HP], F32, kind="ExternalInput")
    w3T = nc.dram_tensor("w3T", [D, HP], F32, kind="ExternalInput")
    w2T = nc.dram_tensor("w2T", [HP, D], F32, kind="ExternalInput")
    pre_mask = nc.dram_tensor("pre_mask", [128, 1], F32, kind="ExternalInput")
    outT = kb.dram_buf("outT", [D, T], kind="ExternalOutput")
    x2T = kb.dram_buf("x2T", [D, T])
    awc = nc.dram_tensor("awc", [D, 8 * D], BF16)
    awc_blk = {(wi, ec): Buf(kb, awc, f"awc{wi}_{ec}")
               for wi in range(4) for ec in range(KC)}
    fqc = kb.dram_buf("fqc", [D, 2 * HP], BF16)
    w2qc = kb.dram_buf("w2qc", [HP, D], BF16)

    def r3(ap, p=128):
        return ap.rearrange("(kc p) t -> p kc t", p=p)

    xT_r = r3(xT.ap())
    xTpre_r = r3(xTpre.ap())
    wT_r = {n: r3(wT[n].ap()) for n in wnames}
    awc_r = r3(awc.ap())
    fqc_r = r3(fqc.h.ap())
    w2qc_r = w2qc.h.ap().rearrange("(hc p) d -> p hc d", p=128)
    w1T_r = r3(w1T.ap())
    w3T_r = r3(w3T.ap())
    w2T_r = w2T.ap().rearrange("(hc p) d -> p hc d", p=128)
    x2T_r = r3(x2T.h.ap())
    outT_r = r3(outT.h.ap())

    ones_c = kb.sbuf("ones_c", [128, 1])
    ones_r = kb.sbuf("ones_r", [1, 128])
    bias_sb = kb.sbuf("bias_sb", [128, 4 * KC])
    mask_sb = kb.sbuf("mask_sb", [128, 1])
    prefix = kb.sbuf("prefix", [128, KC])
    stat = kb.sbuf("stat", [1, 5 * TS])
    svals = kb.sbuf("svals", [128, 8])
    sval_tmp = kb.sbuf("sval_tmp", [1, 8])
    stats_cols = kb.sbuf("stats_cols", [128, 224])
    stats_red = kb.sbuf("stats_red", [128, 3])
    const_sb = kb.sbuf("const_sb", [128, 2])
    sq_pp = [kb.sbuf(f"sq{i}", [128, TS]) for i in range(2)]
    ln32_pp = [kb.sbuf(f"ln32_{i}", [128, TS]) for i in range(2)]
    xsg = [kb.sbuf(f"xsg{i}", [128, 2 * TS]) for i in range(2)]

    pb = [kb.psum(f"pb{i}", (128, TS)) for i in range(8)]

    kb.op("vector", lambda e: e.memset(ones_c.ap(), 1.0), writes=[ones_c], tag="init")
    kb.op("vector", lambda e: e.memset(ones_r.ap(), 1.0), writes=[ones_r], tag="init")
    kb.op("vector", lambda e: e.memset(prefix.ap(), 0.0), writes=[prefix], tag="init")
    kb.op("vector", lambda e: e.memset(const_sb.h[:, 0:1], cfg.EPS),
          writes=[const_sb], tag="init")
    kb.op("vector", lambda e: e.memset(const_sb.h[:, 1:2], cfg.RC),
          writes=[const_sb], tag="init")

    def load_simple(buf, out_ap, in_ap, tag=""):
        def fn(eng, semh, out_ap=out_ap, in_ap=in_ap):
            eng.dma_start(out=out_ap, in_=in_ap,
                          allow_slow_non_contiguous=True).then_inc(semh, 16)
        return kb.dma("sync", buf.name, fn, writes=[buf], tag=tag)

    for j, n in enumerate(("bq", "bk", "bv", "bo")):
        load_simple(bias_sb, bias_sb.h[:, j * KC:(j + 1) * KC],
                    biases[n].ap().rearrange("(kc p) -> p kc", p=128), tag="bias")
    load_simple(mask_sb, mask_sb.ap(), pre_mask.ap(), tag="mask")

    def split_pair(src_buf, src_ap, hi_buf, hi_ap, lo_buf, lo_ap, tag):
        kb.op("scalar", lambda e: e.copy(hi_ap, src_ap),
              reads=[src_buf], writes=[hi_buf], tag=tag + "_hi")
        kb.op("vector",
              lambda e: e.scalar_tensor_tensor(lo_ap, src_ap, 1.0, hi_ap,
                                               ALU.mult, ALU.subtract),
              reads=[src_buf, hi_buf], writes=[lo_buf], tag=tag + "_lo")

    ln_hi = kb.sbuf("ln_hi", [128, KC * TS], BF16)
    ln_lo = kb.sbuf("ln_lo", [128, KC * TS], BF16)

    stackA = ExitStack()
    zeros_ts = kb.sbuf("zeros_ts", [128, TS], stack=stackA)
    kb.op("vector", lambda e: e.memset(zeros_ts.ap(), 0.0), writes=[zeros_ts],
          tag="init")

    f0_pp = [kb.sbuf(f"f0_{i}", [128, 1376], stack=stackA) for i in range(2)]
    ld_pp = [kb.sbuf(f"bgld{i}", [128, KC * 128], stack=stackA) for i in range(2)]
    q_pp = [kb.sbuf(f"bgq{i}", [128, KC * 128], BF16, stack=stackA)
            for i in range(2)]

    def bg_stream():
        pend = [None]

        def flush():
            if pend[0] is not None:
                pend[0]()
                pend[0] = None

        worder = [(wi, n, ec) for ec in range(KC) for wi, n in ((0, "wkT"),
                                                               (1, "wvT"))]
        worder += [(2, "wqT", ec) for ec in range(KC)]
        worder += [(3, "woT", ec) for ec in range(KC)]
        for i, (wi, n, ec) in enumerate(worder):
            buf = ld_pp[i % 2]

            def fnl(eng, semh, n=n, ec=ec, buf=buf):
                eng.dma_start(out=buf.ap(), in_=wT_r[n][:, :, ec * 128:(ec + 1) * 128]
                              ).then_inc(semh, 16)
            kb.dma("sync", buf.name, fnl, writes=[buf], tag=f"wspl_{n}{ec}")
            flush()

            def compute(wi=wi, n=n, ec=ec, buf=buf):
                split_pair(buf, buf.ap(), q_pp[0], q_pp[0].ap(),
                           q_pp[1], q_pp[1].ap(), f"wsp_{n}{ec}")
                base = ec * 1024 + wi * 256

                def fsh(eng, semh, base=base):
                    eng.dma_start(out=awc_r[:, :, base:base + 128],
                                  in_=q_pp[0].h.ap().rearrange(
                                      "p (kc c) -> p kc c", kc=KC)
                                  ).then_inc(semh, 16)
                kb.dma("gpsimd", q_pp[0].name, fsh, reads=[q_pp[0]],
                       writes=[awc_blk[(wi, ec)]], tag=f"wsps_{n}{ec}h")

                def fsl(eng, semh, base=base):
                    eng.dma_start(out=awc_r[:, :, base + 128:base + 256],
                                  in_=q_pp[1].h.ap().rearrange(
                                      "p (kc c) -> p kc c", kc=KC)
                                  ).then_inc(semh, 16)
                kb.dma("gpsimd", q_pp[1].name, fsl, reads=[q_pp[1]],
                       writes=[awc_blk[(wi, ec)]], tag=f"wsps_{n}{ec}l")
            pend[0] = compute
            yield
        flush()

        F0W = 1376
        col = 0
        f0_ncols = {0: [], 1: [], 2: []}
        for wi, wr, tot in ((0, w1T_r, HP), (1, w3T_r, HP), (2, w2T_r, D)):
            nch = wr.shape[1]
            for kc in range(nch):
                for o0 in range(0, tot, F0W):
                    width = min(o0 + F0W, tot) - o0
                    buf = f0_pp[col % 2]
                    src = wr[:, kc, o0:o0 + width]

                    def fnl(eng, semh, src=src, width=width, buf=buf):
                        eng.dma_start(out=buf.h[:, 0:width], in_=src
                                      ).then_inc(semh, 16)
                    kb.dma("sync", buf.name, fnl, writes=[buf], tag="f0_load")
                    flush()

                    def compute(width=width, col=col, buf=buf):
                        kb.op("vector", lambda e: e.reduce_sum(
                                  stats_cols.h[:, col:col + 1],
                                  buf.h[:, 0:width], axis=mybir.AxisListType.X,
                                  apply_absolute_value=True),
                              reads=[buf], writes=[stats_cols], tag="f0_red")
                    pend[0] = compute
                    f0_ncols[wi].append(col)
                    col += 1
                    yield
        flush()

        for wi in range(3):
            cols = f0_ncols[wi]
            lo_, hi_ = min(cols), max(cols) + 1
            kb.op("vector", lambda e, wi=wi, lo_=lo_, hi_=hi_: e.reduce_sum(
                      stats_red.h[:, wi:wi + 1], stats_cols.h[:, lo_:hi_],
                      axis=mybir.AxisListType.X),
                  reads=[stats_cols], writes=[stats_red], tag=f"f0red{wi}")
            kb.op("tensor", lambda e, wi=wi: e.matmul(
                      pb[7].h[0:1, wi:wi + 1], ones_c.ap(),
                      stats_red.h[:, wi:wi + 1], start=True, stop=True),
                  reads=[stats_red, ones_c], writes=[pb[7]], tag=f"f0tot{wi}")
        cnt = float(D) * H
        kb.op("vector", lambda e: e.tensor_scalar_mul(
                  sval_tmp.h[0:1, 0:3], pb[7].h[0:1, 0:3], 1.0 / cnt),
              reads=[pb[7]], writes=[sval_tmp], tag="f0_s")
        kb.op("vector", lambda e: e.reciprocal(sval_tmp.h[0:1, 3:6],
                                               sval_tmp.h[0:1, 0:3]),
              reads=[sval_tmp], writes=[sval_tmp], tag="f0_invs")
        kb.op("tensor", lambda e: e.matmul(pb[7].h[:, 8:14], ones_r.ap(),
                                           sval_tmp.h[0:1, 0:6],
                                           start=True, stop=True),
              reads=[sval_tmp, ones_r], writes=[pb[7]], tag="f0_bc")
        kb.op("vector", lambda e: e.tensor_copy(svals.h[:, 0:6], pb[7].h[:, 8:14]),
              reads=[pb[7]], writes=[svals], tag="f0_sv")
        yield

        def q_step(i, in_ap_fn, width, wi, store_fn, tag):
            buf = ld_pp[i % 2]
            qb = q_pp[i % 2]

            def fl(eng, semh, buf=buf):
                eng.dma_start(out=buf.h[:, 0:width].rearrange(
                                  "p (n c) -> p n c", c=128),
                              in_=in_ap_fn()).then_inc(semh, 16)
            kb.dma("sync", buf.name, fl, writes=[buf], tag=tag + "_l")
            flush()

            def compute(buf=buf, qb=qb, width=width, wi=wi):
                inv = svals.h[:, 3 + wi:4 + wi]
                sap = buf.h[:, 0:width]
                qap = qb.h[:, 0:width]
                kb.op("scalar", lambda e: e.activation(
                          sap, sap, AF.Identity, bias=const_sb.h[:, 1:2],
                          scale=inv),
                      reads=[buf, svals, const_sb], writes=[buf], tag=tag + "_q1")
                kb.op("vector", lambda e: e.tensor_scalar(
                          sap, sap, cfg.RC, 1.0, ALU.subtract, ALU.min),
                      reads=[buf], writes=[buf], tag=tag + "_q2")
                kb.op("vector", lambda e: e.tensor_scalar(
                          qap, sap, -1.0, None, ALU.max),
                      reads=[buf], writes=[qb], tag=tag + "_q3")

                def fs(eng, semh, qb=qb):
                    eng.dma_start(out=store_fn(),
                                  in_=qb.h[:, 0:width].rearrange(
                                      "p (n c) -> p n c", c=128)
                                  ).then_inc(semh, 16)
                kb.dma("gpsimd", qb.name, fs, reads=[qb], writes=[fqc, w2qc],
                       tag=tag + "_s")
            pend[0] = compute

        i = 0
        for hc in range(HC):
            for wi, wr in ((0, w1T_r), (1, w3T_r)):
                q_step(i,
                       lambda hc=hc, wr=wr: wr[:, :, hc * 128:(hc + 1) * 128],
                       KC * 128, wi,
                       lambda hc=hc, wi=wi: fqc_r[
                           :, :, hc * 256 + wi * 128:hc * 256 + (wi + 1) * 128],
                       f"q13_{hc}_{wi}")
                i += 1
                yield
        HCW = KC
        for dc in range(KC):
            for h0 in range(0, HC, HCW):
                h1_ = min(h0 + HCW, HC)
                w_ = (h1_ - h0) * 128
                q_step(i,
                       lambda dc=dc, h0=h0, h1_=h1_: w2T_r[
                           :, h0:h1_, dc * 128:(dc + 1) * 128],
                       w_, 2,
                       lambda dc=dc, h0=h0, h1_=h1_: w2qc_r[
                           :, h0:h1_, dc * 128:(dc + 1) * 128],
                       f"q2_{dc}_{h0}")
                i += 1
                yield
        flush()

    bg = bg_stream()

    def bg_step(n=1):
        for _ in range(n):
            if next(bg, "END") == "END":
                return False
        return True

    def ln_stats_step(get_x, kc, tag):
        xb, xap = get_x(kc)
        sq = sq_pp[kc % 2]
        kb.op("scalar", lambda e, xap=xap, sq=sq: e.activation(
                  sq.ap(), xap, AF.Square),
              reads=[xb], writes=[sq], tag=tag + "_sq")

        def fn_mm(e, xap=xap, sq=sq, kc=kc):
            return [
                e.matmul(pb[0].h[0:1, 0:TS], ones_c.ap(), xap,
                         start=(kc == 0), stop=(kc == KC - 1)),
                e.matmul(pb[1].h[0:1, 0:TS], ones_c.ap(), sq.ap(),
                         start=(kc == 0), stop=(kc == KC - 1)),
            ]
        kb.op("tensor", fn_mm, reads=[xb, sq, ones_c], writes=[pb[0], pb[1]],
              tag=tag + "_sums")

    def ln_finish(get_x, lnhi, lnlo, tag):
        m = stat.h[0:1, 0:TS]
        esq = stat.h[0:1, TS:2 * TS]
        m2 = stat.h[0:1, 2 * TS:3 * TS]
        var = stat.h[0:1, 3 * TS:4 * TS]
        sd = stat.h[0:1, 2 * TS:3 * TS]
        rr = stat.h[0:1, 4 * TS:5 * TS]
        kb.op("vector", lambda e: e.tensor_scalar_mul(m, pb[0].h[0:1, 0:TS], 1.0 / D),
              reads=[pb[0]], writes=[stat], tag=tag + "_m")
        kb.op("vector", lambda e: e.tensor_scalar_mul(esq, pb[1].h[0:1, 0:TS],
                                                      1.0 / D),
              reads=[pb[1]], writes=[stat], tag=tag + "_esq")
        kb.op("vector", lambda e: e.tensor_mul(m2, m, m), reads=[stat], writes=[stat],
              tag=tag + "_m2")
        kb.op("vector", lambda e: e.tensor_sub(var, esq, m2), reads=[stat],
              writes=[stat], tag=tag + "_var")
        kb.op("scalar", lambda e: e.activation(sd, var, AF.Sqrt,
                                               bias=const_sb.h[0:1, 0:1]),
              reads=[stat, const_sb], writes=[stat], tag=tag + "_sd")
        kb.op("vector", lambda e: e.reciprocal(rr, sd),
              reads=[stat], writes=[stat], tag=tag + "_rstd")
        kb.op("tensor", lambda e: e.matmul(pb[2].h[:, 0:TS], ones_r.ap(), m,
                                           start=True, stop=True),
              reads=[stat, ones_r], writes=[pb[2]], tag=tag + "_bm")
        kb.op("tensor", lambda e: e.matmul(pb[3].h[:, 0:TS], ones_r.ap(), rr,
                                           start=True, stop=True),
              reads=[stat, ones_r], writes=[pb[3]], tag=tag + "_br")
        for kc in range(KC):
            xb, xap = get_x(kc)
            t1 = sq_pp[kc % 2]
            l32 = ln32_pp[kc % 2]
            kb.op("vector", lambda e, xap=xap, t1=t1: e.tensor_sub(
                      t1.ap(), xap, pb[2].h[:, 0:TS]),
                  reads=[xb, pb[2]], writes=[t1], tag=tag + "_ap1")
            kb.op("vector", lambda e, t1=t1, l32=l32: e.tensor_mul(
                      l32.ap(), t1.ap(), pb[3].h[:, 0:TS]),
                  reads=[t1, pb[3]], writes=[l32], tag=tag + "_ap2")
            split_pair(l32, l32.ap(), lnhi, lnhi.h[:, kc * TS:(kc + 1) * TS],
                       lnlo, lnlo.h[:, kc * TS:(kc + 1) * TS], tag + f"_sp{kc}")

    def layer_norm(get_x, lnhi, lnlo, tag):
        for kc in range(KC):
            ln_stats_step(get_x, kc, tag)
        ln_finish(get_x, lnhi, lnlo, tag)

    ctx_hi = kb.sbuf("ctx_hi", [128, KC * TS], BF16, stack=stackA)
    ctx_lo = kb.sbuf("ctx_lo", [128, KC * TS], BF16, stack=stackA)
    wkv_col = [kb.sbuf(f"wkv{i}", [128, KC * 512], BF16, stack=stackA)
               for i in range(2)]
    wq_col = [kb.sbuf(f"wq{i}", [128, KC * 256], BF16, stack=stackA)
              for i in range(2)]
    t_min = kb.sbuf("t_min", [128, TS], stack=stackA)
    t_exp = kb.sbuf("t_exp", [128, TS], stack=stackA)
    t_phik = kb.sbuf("t_phik", [128, TS], stack=stackA)
    t_kv = kb.sbuf("t_kv", [128, TS], stack=stackA)
    t_cum = kb.sbuf("t_cum", [128, TS], stack=stackA)
    t_ctx = kb.sbuf("t_ctx", [128, TS], stack=stackA)
    x2o = [kb.sbuf(f"x2o{i}", [128, 2 * TS], stack=stackA) for i in range(2)]

    def make_x_stream(src_r, c0, tag):
        def get(kc):
            xs = xsg[(kc // 2) % 2]
            if kc % 2 == 0:
                def fl(eng, semh, kc=kc, xs=xs):
                    eng.dma_start(
                        out=xs.h.ap().rearrange("p (g t) -> p g t", g=2),
                        in_=src_r[:, kc:kc + 2, c0:c0 + TS]).then_inc(semh, 16)
                kb.dma("sync", xs.name, fl, writes=[xs], tag=tag + f"_{kc}")
            return xs, xs.h[:, (kc % 2) * TS:(kc % 2 + 1) * TS]
        return get

    def load_wcol(buf, ec, blocks, base, width, tag):
        for wi in blocks:
            assert awc_blk[(wi, ec)].writes, f"awc block ({wi},{ec}) not written"

        def fh(eng, semh, base=base, width=width, buf=buf):
            eng.dma_start(out=buf.h.ap().rearrange("p (kc c) -> p kc c", kc=KC),
                          in_=awc_r[:, :, base:base + width]).then_inc(semh, 16)
        kb.dma("gpsimd", buf.name, fh,
               reads=[awc_blk[(wi, ec)] for wi in blocks], writes=[buf], tag=tag)

    def proj3(psum_b, buf, stride, off, a_hi_b, a_lo_b, tag):
        def fn(e, buf=buf, psum_b=psum_b):
            r = []
            for kc in range(KC):
                w_hi = buf.h[:, kc * stride + off:kc * stride + off + 128]
                w_lo = buf.h[:, kc * stride + off + 128:kc * stride + off + 256]
                a_hi = a_hi_b.h[:, kc * TS:(kc + 1) * TS]
                a_lo = a_lo_b.h[:, kc * TS:(kc + 1) * TS]
                r.append(e.matmul(psum_b.h[:, 0:TS], w_hi, a_hi,
                                  start=(kc == 0), stop=False))
                r.append(e.matmul(psum_b.h[:, 0:TS], w_hi, a_lo,
                                  start=False, stop=False))
                r.append(e.matmul(psum_b.h[:, 0:TS], w_lo, a_hi,
                                  start=False, stop=(kc == KC - 1)))
            return r
        kb.op("tensor", fn, reads=[buf, a_hi_b, a_lo_b], writes=[psum_b], tag=tag)

    def phi_chain(psum_b, bias_col, out_phi, tag):
        bap = bias_sb.h[:, bias_col:bias_col + 1]
        kb.op("vector", lambda e: e.tensor_scalar(
                  t_min.ap(), psum_b.h[:, 0:TS], bap, 0.0, ALU.add, ALU.min),
              reads=[psum_b, bias_sb], writes=[t_min], tag=tag + "_min")
        kb.op("scalar", lambda e: e.activation(t_exp.ap(), t_min.ap(), AF.Exp),
              reads=[t_min], writes=[t_exp], tag=tag + "_exp")
        kb.op("scalar", lambda e: e.activation(t_min.ap(), psum_b.h[:, 0:TS],
                                               AF.Relu, bias=bap),
              reads=[psum_b, bias_sb], writes=[t_min], tag=tag + "_relu")
        kb.op("vector", lambda e: e.tensor_add(out_phi.ap(), t_exp.ap(), t_min.ap()),
              reads=[t_exp, t_min], writes=[out_phi], tag=tag + "_phi")

    bg_step(4)
    n_slices = cfg.n_pre + cfg.n_own

    def slice_src(s):
        is_pre = s < cfg.n_pre
        c0 = (s - (0 if is_pre else cfg.n_pre)) * TS
        return is_pre, c0, (xTpre_r if is_pre else xT_r)

    for s in range(n_slices):
        is_pre, c0, src_r = slice_src(s)

        if s == 0:
            layer_norm(make_x_stream(src_r, c0, "xs0"), ln_hi, ln_lo, "ln1_s0")
        if s + 1 < n_slices:
            _, c1, src1 = slice_src(s + 1)
            gx_next = make_x_stream(src1, c1, f"xs{s + 1}")
        else:
            gx_next = make_x_stream(x2T_r, 0, "x2s0")

        for ec in range(KC):
            ln_stats_step(gx_next, ec, f"lnn_s{s + 1}")
            bg_step(4 if s == 0 else 3)
            wkv = wkv_col[ec % 2]
            load_wcol(wkv, ec, (0, 1), ec * 1024, 512, f"wkv_s{s}e{ec}")
            if not is_pre:
                wq = wq_col[ec % 2]
                load_wcol(wq, ec, (2,), ec * 1024 + 512, 256, f"wq_s{s}e{ec}")

            proj3(pb[5], wkv, 512, 0, ln_hi, ln_lo, f"mmk_s{s}e{ec}")
            proj3(pb[6], wkv, 512, 256, ln_hi, ln_lo, f"mmv_s{s}e{ec}")
            if not is_pre:
                proj3(pb[4], wq, 256, 0, ln_hi, ln_lo, f"mmq_s{s}e{ec}")

            phi_chain(pb[5], KC + ec, t_phik, f"pk_s{s}e{ec}")
            kb.op("vector", lambda e, ec=ec: e.scalar_tensor_tensor(
                      t_kv.ap(), pb[6].h[:, 0:TS],
                      bias_sb.h[:, 2 * KC + ec:2 * KC + ec + 1],
                      t_phik.ap(), ALU.add, ALU.mult),
                  reads=[pb[6], bias_sb, t_phik], writes=[t_kv], tag=f"kv_s{s}e{ec}")
            kb.op("vector", lambda e, ec=ec: e.tensor_tensor_scan(
                      t_cum.ap(), t_kv.ap(), zeros_ts.ap(), prefix.h[:, ec:ec + 1],
                      ALU.add, ALU.add),
                  reads=[t_kv, zeros_ts, prefix], writes=[t_cum],
                  tag=f"scan_s{s}e{ec}")
            if is_pre:
                kb.op("vector", lambda e, ec=ec: e.tensor_scalar_mul(
                          prefix.h[:, ec:ec + 1], t_cum.h[:, TS - 1:TS],
                          mask_sb.h[:, 0:1]),
                      reads=[t_cum, mask_sb], writes=[prefix], tag=f"pfx_s{s}e{ec}")
            else:
                kb.op("vector", lambda e, ec=ec: e.tensor_copy(
                          prefix.h[:, ec:ec + 1], t_cum.h[:, TS - 1:TS]),
                      reads=[t_cum], writes=[prefix], tag=f"pfx_s{s}e{ec}")

            if not is_pre:
                phi_chain(pb[4], ec, t_phik, f"pq_s{s}e{ec}")
                kb.op("vector", lambda e: e.scalar_tensor_tensor(
                          t_ctx.ap(), t_cum.ap(), cfg.SCALE, t_phik.ap(),
                          ALU.mult, ALU.mult),
                      reads=[t_cum, t_phik], writes=[t_ctx], tag=f"ctx_s{s}e{ec}")
                split_pair(t_ctx, t_ctx.ap(),
                           ctx_hi, ctx_hi.h[:, ec * TS:(ec + 1) * TS],
                           ctx_lo, ctx_lo.h[:, ec * TS:(ec + 1) * TS],
                           f"ctxsp_s{s}e{ec}")

        if s + 1 < n_slices:
            ln_finish(gx_next, ln_hi, ln_lo, f"lnn_s{s + 1}")

        if not is_pre:
            get_xr = make_x_stream(src_r, c0, f"xr{s}")
            for e2c in range(KC):
                bg_step(1)
                wo = wq_col[e2c % 2]
                load_wcol(wo, e2c, (3,), e2c * 1024 + 768, 256, f"wo_s{s}e{e2c}")
                proj3(pb[7], wo, 256, 0, ctx_hi, ctx_lo, f"mmo_s{s}e{e2c}")
                xres, xres_ap = get_xr(e2c)
                xo = x2o[(e2c // 2) % 2]
                sub = (e2c % 2) * TS
                kb.op("vector", lambda e, e2c=e2c, xo=xo, sub=sub, xres=xres,
                      xres_ap=xres_ap: e.scalar_tensor_tensor(
                          xo.h[:, sub:sub + TS], pb[7].h[:, 0:TS],
                          bias_sb.h[:, 3 * KC + e2c:3 * KC + e2c + 1],
                          xres_ap, ALU.add, ALU.add),
                      reads=[pb[7], bias_sb, xres], writes=[xo],
                      tag=f"x2_s{s}e{e2c}")
                if e2c % 2 == 1:
                    def fn_st(eng, semh, xo=xo, e2c=e2c, c0=c0):
                        eng.dma_start(
                            out=x2T_r[:, e2c - 1:e2c + 1, c0:c0 + TS],
                            in_=xo.h.ap().rearrange("p (g t) -> p g t", g=2)
                            ).then_inc(semh, 16)
                    kb.dma("gpsimd", xo.name, fn_st, reads=[xo], writes=[x2T],
                           tag=f"x2st_s{s}e{e2c}")

    ln_finish(gx_next, ln_hi, ln_lo, "lnn_f0")

    while bg_step():
        pass

    kb.barrier()
    stackA.close()

    g_hi = kb.sbuf("g_hi", [128, HC * TS], BF16)
    g_lo = kb.sbuf("g_lo", [128, HC * TS], BF16)
    q13c = [kb.sbuf(f"q13c{i}", [128, KC * 256], BF16) for i in range(2)]
    w2qb = [kb.sbuf(f"w2qb{i}", [128, HC * 128], BF16) for i in range(2)]
    silu_b = kb.sbuf("silu_b", [128, TS])
    g32 = kb.sbuf("g32", [128, TS])
    x2rd = [kb.sbuf(f"x2rd{i}", [128, TS]) for i in range(2)]
    ob = [kb.sbuf(f"ob{i}", [128, 2 * TS]) for i in range(2)]

    for fs in range(cfg.n_own):
        c0 = fs * TS
        if fs + 1 < cfg.n_own:
            gx2_next = make_x_stream(x2T_r, (fs + 1) * TS, f"x2s{fs + 1}")

        for hc in range(HC):
            if fs + 1 < cfg.n_own and hc < KC:
                ln_stats_step(gx2_next, hc, f"ln2_s{fs + 1}")
            qb = q13c[hc % 2]

            def fn_q13(eng, semh, qb=qb, hc=hc):
                eng.dma_start(out=qb.h.ap().rearrange("p (kc c) -> p kc c", kc=KC),
                              in_=fqc_r[:, :, hc * 256:(hc + 1) * 256]
                              ).then_inc(semh, 16)
            kb.dma("gpsimd", qb.name, fn_q13, reads=[fqc], writes=[qb],
                   tag=f"q13_{fs}_{hc}")

            ph1 = pb[4 + (hc % 2) * 2]
            ph3 = pb[5 + (hc % 2) * 2]

            def fn_h(e, qb, p, off):
                r = []
                for kc in range(KC):
                    w_ap = qb.h[:, kc * 256 + off:kc * 256 + off + 128]
                    r.append(e.matmul(p.h[:, 0:TS], w_ap,
                                      ln_hi.h[:, kc * TS:(kc + 1) * TS],
                                      start=(kc == 0), stop=False))
                    r.append(e.matmul(p.h[:, 0:TS], w_ap,
                                      ln_lo.h[:, kc * TS:(kc + 1) * TS],
                                      start=False, stop=(kc == KC - 1)))
                return r
            kb.op("tensor", lambda e, qb=qb, p=ph1: fn_h(e, qb, p, 0),
                  reads=[qb, ln_hi, ln_lo], writes=[ph1], tag=f"mmh1_{fs}_{hc}")
            kb.op("tensor", lambda e, qb=qb, p=ph3: fn_h(e, qb, p, 128),
                  reads=[qb, ln_hi, ln_lo], writes=[ph3], tag=f"mmh3_{fs}_{hc}")

            kb.op("scalar", lambda e, ph1=ph1: e.activation(
                      silu_b.ap(), ph1.h[:, 0:TS], AF.Silu, scale=svals.h[:, 0:1]),
                  reads=[ph1, svals], writes=[silu_b], tag=f"silu_{fs}_{hc}")
            kb.op("vector", lambda e, ph3=ph3: e.scalar_tensor_tensor(
                      g32.ap(), ph3.h[:, 0:TS], svals.h[:, 1:2], silu_b.ap(),
                      ALU.mult, ALU.mult),
                  reads=[ph3, svals, silu_b], writes=[g32], tag=f"g_{fs}_{hc}")
            split_pair(g32, g32.ap(), g_hi, g_hi.h[:, hc * TS:(hc + 1) * TS],
                       g_lo, g_lo.h[:, hc * TS:(hc + 1) * TS], f"gsp_{fs}_{hc}")

        if fs + 1 < cfg.n_own:
            ln_finish(gx2_next, ln_hi, ln_lo, f"ln2_s{fs + 1}")

        for dc in range(KC):
            wq2 = w2qb[dc % 2]

            def fn_w2(eng, semh, wq2=wq2, dc=dc):
                eng.dma_start(out=wq2.h.ap().rearrange("p (hc c) -> p hc c", c=128),
                              in_=w2qc_r[:, :, dc * 128:(dc + 1) * 128]
                              ).then_inc(semh, 16)
            kb.dma("gpsimd", wq2.name, fn_w2, reads=[w2qc], writes=[wq2],
                   tag=f"w2q_{fs}_{dc}")

            xr = x2rd[dc % 2]

            def fn_xr(eng, semh, xr=xr, dc=dc, c0=c0):
                eng.dma_start(out=xr.ap(), in_=x2T_r[:, dc, c0:c0 + TS]
                              ).then_inc(semh, 16)
            kb.dma("sync", xr.name, fn_xr, reads=[x2T], writes=[xr],
                   tag=f"xr{fs}_{dc}")

            pf = pb[dc % 4]

            def fn_f(e, pf=pf, wq2=wq2):
                r = []
                for hc in range(HC):
                    w_ap = wq2.h[:, hc * 128:(hc + 1) * 128]
                    r.append(e.matmul(pf.h[:, 0:TS], w_ap,
                                      g_hi.h[:, hc * TS:(hc + 1) * TS],
                                      start=(hc == 0), stop=False))
                    r.append(e.matmul(pf.h[:, 0:TS], w_ap,
                                      g_lo.h[:, hc * TS:(hc + 1) * TS],
                                      start=False, stop=(hc == HC - 1)))
                return r
            kb.op("tensor", fn_f, reads=[wq2, g_hi, g_lo], writes=[pf],
                  tag=f"mmf{fs}_{dc}")

            o = ob[(dc // 2) % 2]
            sub = (dc % 2) * TS
            kb.op("vector", lambda e, pf=pf, xr=xr, o=o, sub=sub:
                  e.scalar_tensor_tensor(
                      o.h[:, sub:sub + TS], pf.h[:, 0:TS], svals.h[:, 2:3],
                      xr.ap(), ALU.mult, ALU.add),
                  reads=[pf, svals, xr], writes=[o], tag=f"out{fs}_{dc}")
            if dc % 2 == 1:
                def fn_ost(eng, semh, o=o, dc=dc, c0=c0):
                    eng.dma_start(out=outT_r[:, dc - 1:dc + 1, c0:c0 + TS],
                                  in_=o.h.ap().rearrange("p (g t) -> p g t", g=2)
                                  ).then_inc(semh, 16)
                kb.dma("gpsimd", o.name, fn_ost, reads=[o], writes=[outT],
                       tag=f"ost{fs}_{dc}")

    kb.raw_wait("gpsimd", extra=[(s, kb.semc[s]) for s in
                                 ("d_ob0", "d_ob1", "d_x2o0", "d_x2o1")
                                 if s in kb.semc])
    kb.emit()
    return nc




def kernel(L0_attn_x, L0_attn_Wq, L0_attn_bq, L0_attn_Wk, L0_attn_bk,
           L0_attn_Wv, L0_attn_bv, L0_attn_Wo, L0_attn_bo,
           L0_ffn_w1, L0_ffn_w3, L0_ffn_w2):
    cfg = CFG
    D, T, P, H, HP = cfg.D, cfg.T, cfg.P, cfg.H, cfg.HP
    x = np.ascontiguousarray(np.asarray(L0_attn_x, dtype=np.float32))
    B, L, _ = x.shape

    def t32(a):
        return np.ascontiguousarray(np.asarray(a, dtype=np.float32).T)

    wqT, wkT, wvT, woT = (t32(L0_attn_Wq), t32(L0_attn_Wk),
                          t32(L0_attn_Wv), t32(L0_attn_Wo))
    w1T = np.zeros((D, HP), np.float32)
    w1T[:, :H] = t32(L0_ffn_w1)
    w3T = np.zeros((D, HP), np.float32)
    w3T[:, :H] = t32(L0_ffn_w3)
    w2T = np.zeros((HP, D), np.float32)
    w2T[:H, :] = t32(L0_ffn_w2)
    zeros_pre = np.zeros((D, P), np.float32)

    in_maps = []
    for c in range(8):
        b, h = c // 2, c % 2
        xTc = np.ascontiguousarray(x[b, h * T:(h + 1) * T, :].T)
        if h == 1:
            xTpre = np.ascontiguousarray(x[b, 0:P, :].T)
            mask = np.ones((128, 1), np.float32)
        else:
            xTpre = zeros_pre
            mask = np.zeros((128, 1), np.float32)
        in_maps.append({
            "xT": xTc, "xTpre": xTpre, "pre_mask": mask,
            "wqT": wqT, "wkT": wkT, "wvT": wvT, "woT": woT,
            "bq": np.asarray(L0_attn_bq, np.float32),
            "bk": np.asarray(L0_attn_bk, np.float32),
            "bv": np.asarray(L0_attn_bv, np.float32),
            "bo": np.asarray(L0_attn_bo, np.float32),
            "w1T": w1T, "w3T": w3T, "w2T": w2T,
        })

    nc = build_nc(cfg)
    res = run_bass_kernel_spmd(nc, in_maps, core_ids=list(range(8)))

    out = np.empty((B, L, D), np.float32)
    for c in range(8):
        b, h = c // 2, c % 2
        out[b, h * T:(h + 1) * T, :] = np.asarray(res.results[c]["outT"]).T
    return out


# version 25
# speedup vs baseline: 1.1609x; 1.1601x over previous
from contextlib import ExitStack

import numpy as np

import concourse.bass as bass
import concourse.mybir as mybir
from concourse.bass_utils import run_bass_kernel_spmd

F32 = mybir.dt.float32
BF16 = mybir.dt.bfloat16
AF = mybir.ActivationFunctionType
ALU = mybir.AluOpType


class Cfg:
    def __init__(self, D=2048, T=2048, P=2048, H=5461, TS=512):
        self.D, self.T, self.P, self.H, self.TS = D, T, P, H, TS
        self.KC = D // 128
        self.HC = (H + 127) // 128
        self.HP = self.HC * 128
        self.n_pre = P // TS
        self.n_own = T // TS
        self.SCALE = 128.0 ** -0.5
        self.EPS = 1e-5
        self.RC = 12582912.0


CFG = Cfg()

ENGINES = ("tensor", "vector", "scalar", "sync", "gpsimd")
ENG_SEM = {e: "E_" + e for e in ENGINES}


class Buf:

    def __init__(self, kb, handle, name):
        self.kb = kb
        self.h = handle
        self.name = name
        self.writes = {}
        self.readers = {}

    def ap(self):
        return self.h[:]

    def _merge(self, d, tk):
        s, v = tk
        if d.get(s, 0) < v:
            d[s] = v


class KB:
    def __init__(self, nc, stack):
        self.nc = nc
        self.stack = stack
        self.semh = {}
        self.semc = {}
        self.ops = {e: [] for e in ENGINES}
        self.waited = {e: {} for e in ENGINES}
        self.barrier_deps = {}
        for e in ENGINES:
            self._sem(ENG_SEM[e])

    def _sem(self, name):
        if name not in self.semh:
            self.semh[name] = self.stack.enter_context(self.nc.semaphore(name))
            self.semc[name] = 0
        return name

    def sbuf(self, name, shape, dtype=F32, stack=None):
        h = (stack or self.stack).enter_context(
            self.nc.sbuf_tensor(name, list(shape), dtype))
        return Buf(self, h, name)

    def psum(self, name, shape=(128, 512), dtype=F32):
        h = self.stack.enter_context(self.nc.psum_tensor(name, list(shape), dtype))
        return Buf(self, h, name)

    def dram_buf(self, name, shape, dtype=F32, kind=None):
        if kind is None:
            h = self.nc.dram_tensor(name, list(shape), dtype)
        else:
            h = self.nc.dram_tensor(name, list(shape), dtype, kind=kind)
        return Buf(self, h, name)

    def barrier(self):
        self.barrier_deps = dict(self.semc)

    def _collect(self, engine, reads, writes):
        deps = {}
        for s, v in self.barrier_deps.items():
            if deps.get(s, 0) < v:
                deps[s] = v
        for b in reads:
            for s, v in b.writes.items():
                if deps.get(s, 0) < v:
                    deps[s] = v
        for b in writes:
            for d in (b.writes, b.readers):
                for s, v in d.items():
                    if deps.get(s, 0) < v:
                        deps[s] = v
        waits = []
        wt = self.waited[engine]
        for s in sorted(deps):
            v = deps[s]
            if self.semc.get(s, 0) < v:
                raise RuntimeError(f"dep on future ticket {s}={v}")
            if wt.get(s, 0) < v:
                waits.append((s, v))
                wt[s] = v
        return waits

    def op(self, engine, fn, reads=(), writes=(), tag=""):
        waits = self._collect(engine, reads, writes)
        sem = ENG_SEM[engine]
        self.semc[sem] += 1
        tk = (sem, self.semc[sem])
        for b in writes:
            b._merge(b.writes, tk)
        for b in reads:
            b._merge(b.readers, tk)
        self.ops[engine].append((waits, fn, sem, 1, tag))
        return tk

    def dma(self, engine, buf_sem_owner, fn, reads=(), writes=(), tag=""):
        sem = self._sem("d_" + buf_sem_owner)
        waits = self._collect(engine, reads, writes)
        self.semc[sem] += 16
        tk = (sem, self.semc[sem])
        for b in writes:
            b._merge(b.writes, tk)
        for b in reads:
            b._merge(b.readers, tk)
        self.ops[engine].append((waits, fn, sem, None, tag))
        return tk

    def raw_wait(self, engine, extra=()):
        deps = {}
        for s, v in extra:
            if deps.get(s, 0) < v:
                deps[s] = v
        waits = []
        wt = self.waited[engine]
        for s in sorted(deps):
            v = deps[s]
            if wt.get(s, 0) < v:
                waits.append((s, v))
                wt[s] = v
        self.ops[engine].append((waits, None, None, None, "final_wait"))

    def emit(self):
        nc = self.nc
        with nc.Block() as block:
            def emitter(engine):
                def run(eng):
                    for waits, fn, sem, amt, tag in self.ops[engine]:
                        for s, v in waits:
                            eng.wait_ge(self.semh[s], v)
                        if fn is None:
                            continue
                        if amt is None:
                            fn(eng, self.semh[sem])
                        else:
                            r = fn(eng)
                            r = r[-1] if isinstance(r, (list, tuple)) else r
                            r.then_inc(self.semh[sem], amt)
                return run
            block.tensor(emitter("tensor"))
            block.vector(emitter("vector"))
            block.scalar(emitter("scalar"))
            block.sync(emitter("sync"))
            block.gpsimd(emitter("gpsimd"))




def build_nc(cfg=CFG):
    nc = bass.Bass("TRN2", target_bir_lowering=False)
    stack = ExitStack()
    kb = KB(nc, stack)
    D, T, P, H, TS = cfg.D, cfg.T, cfg.P, cfg.H, cfg.TS
    KC, HC, HP = cfg.KC, cfg.HC, cfg.HP

    xT = nc.dram_tensor("xT", [D, T], F32, kind="ExternalInput")
    xTpre = nc.dram_tensor("xTpre", [D, P], F32, kind="ExternalInput")
    wnames = ("wkT", "wvT", "wqT", "woT")
    wT = {n: nc.dram_tensor(n, [D, D], F32, kind="ExternalInput") for n in wnames}
    biases = {n: nc.dram_tensor(n, [D], F32, kind="ExternalInput")
              for n in ("bq", "bk", "bv", "bo")}
    w1T = nc.dram_tensor("w1T", [D, HP], F32, kind="ExternalInput")
    w3T = nc.dram_tensor("w3T", [D, HP], F32, kind="ExternalInput")
    w2T = nc.dram_tensor("w2T", [HP, D], F32, kind="ExternalInput")
    pre_mask = nc.dram_tensor("pre_mask", [128, 1], F32, kind="ExternalInput")
    outT = kb.dram_buf("outT", [D, T], kind="ExternalOutput")
    x2T = kb.dram_buf("x2T", [D, T])
    awc = nc.dram_tensor("awc", [D, 8 * D], BF16)
    awc_blk = {(wi, ec): Buf(kb, awc, f"awc{wi}_{ec}")
               for wi in range(4) for ec in range(KC)}
    fqc = kb.dram_buf("fqc", [D, 2 * HP], BF16)
    w2qc = kb.dram_buf("w2qc", [HP, D], BF16)

    def r3(ap, p=128):
        return ap.rearrange("(kc p) t -> p kc t", p=p)

    xT_r = r3(xT.ap())
    xTpre_r = r3(xTpre.ap())
    wT_r = {n: r3(wT[n].ap()) for n in wnames}
    awc_r = r3(awc.ap())
    fqc_r = r3(fqc.h.ap())
    w2qc_r = w2qc.h.ap().rearrange("(hc p) d -> p hc d", p=128)
    w1T_r = r3(w1T.ap())
    w3T_r = r3(w3T.ap())
    w2T_r = w2T.ap().rearrange("(hc p) d -> p hc d", p=128)
    x2T_r = r3(x2T.h.ap())
    outT_r = r3(outT.h.ap())

    ones_c = kb.sbuf("ones_c", [128, 1])
    ones_r = kb.sbuf("ones_r", [1, 128])
    bias_sb = kb.sbuf("bias_sb", [128, 4 * KC])
    mask_sb = kb.sbuf("mask_sb", [128, 1])
    prefix = kb.sbuf("prefix", [128, KC])
    stat = kb.sbuf("stat", [1, 5 * TS])
    svals = kb.sbuf("svals", [128, 8])
    sval_tmp = kb.sbuf("sval_tmp", [1, 8])
    stats_cols = kb.sbuf("stats_cols", [128, 224])
    stats_red = kb.sbuf("stats_red", [128, 3])
    const_sb = kb.sbuf("const_sb", [128, 3])
    sq_pp = [kb.sbuf(f"sq{i}", [128, TS]) for i in range(2)]
    ln32_pp = [kb.sbuf(f"ln32_{i}", [128, TS]) for i in range(2)]
    xsg = [kb.sbuf(f"xsg{i}", [128, 2 * TS]) for i in range(2)]

    pb = [kb.psum(f"pb{i}", (128, TS)) for i in range(8)]

    kb.op("vector", lambda e: e.memset(ones_c.ap(), 1.0), writes=[ones_c], tag="init")
    kb.op("vector", lambda e: e.memset(ones_r.ap(), 1.0), writes=[ones_r], tag="init")
    kb.op("vector", lambda e: e.memset(prefix.ap(), 0.0), writes=[prefix], tag="init")
    kb.op("vector", lambda e: e.memset(const_sb.h[:, 0:1], cfg.EPS),
          writes=[const_sb], tag="init")
    kb.op("vector", lambda e: e.memset(const_sb.h[:, 2:3], -cfg.RC),
          writes=[const_sb], tag="init")
    kb.op("vector", lambda e: e.memset(const_sb.h[:, 1:2], cfg.RC),
          writes=[const_sb], tag="init")

    def load_simple(buf, out_ap, in_ap, tag=""):
        def fn(eng, semh, out_ap=out_ap, in_ap=in_ap):
            eng.dma_start(out=out_ap, in_=in_ap,
                          allow_slow_non_contiguous=True).then_inc(semh, 16)
        return kb.dma("sync", buf.name, fn, writes=[buf], tag=tag)

    for j, n in enumerate(("bq", "bk", "bv", "bo")):
        load_simple(bias_sb, bias_sb.h[:, j * KC:(j + 1) * KC],
                    biases[n].ap().rearrange("(kc p) -> p kc", p=128), tag="bias")
    load_simple(mask_sb, mask_sb.ap(), pre_mask.ap(), tag="mask")

    def split_pair(src_buf, src_ap, hi_buf, hi_ap, lo_buf, lo_ap, tag):
        kb.op("scalar", lambda e: e.copy(hi_ap, src_ap),
              reads=[src_buf], writes=[hi_buf], tag=tag + "_hi")
        kb.op("vector",
              lambda e: e.scalar_tensor_tensor(lo_ap, src_ap, 1.0, hi_ap,
                                               ALU.mult, ALU.subtract),
              reads=[src_buf, hi_buf], writes=[lo_buf], tag=tag + "_lo")

    stackA = ExitStack()
    zeros_ts = kb.sbuf("zeros_ts", [128, TS], stack=stackA)
    kb.op("vector", lambda e: e.memset(zeros_ts.ap(), 0.0), writes=[zeros_ts],
          tag="init")

    f0_pp = [kb.sbuf(f"f0_{i}", [128, 1376], stack=stackA) for i in range(2)]
    ld_pp = [kb.sbuf(f"bgld{i}", [128, KC * 128], stack=stackA) for i in range(2)]
    q_pp = [kb.sbuf(f"bgq{i}", [128, KC * 128], BF16, stack=stackA)
            for i in range(2)]

    def bg_stream():
        pend = [None]

        def flush():
            if pend[0] is not None:
                pend[0]()
                pend[0] = None

        worder = [(wi, n, ec) for ec in range(KC) for wi, n in ((0, "wkT"),
                                                               (1, "wvT"))]
        worder += [(2, "wqT", ec) for ec in range(KC)]
        worder += [(3, "woT", ec) for ec in range(KC)]
        for i, (wi, n, ec) in enumerate(worder):
            buf = ld_pp[i % 2]

            def fnl(eng, semh, n=n, ec=ec, buf=buf):
                eng.dma_start(out=buf.ap(), in_=wT_r[n][:, :, ec * 128:(ec + 1) * 128]
                              ).then_inc(semh, 16)
            kb.dma("sync", buf.name, fnl, writes=[buf], tag=f"wspl_{n}{ec}")
            flush()

            def compute(wi=wi, n=n, ec=ec, buf=buf):
                split_pair(buf, buf.ap(), q_pp[0], q_pp[0].ap(),
                           q_pp[1], q_pp[1].ap(), f"wsp_{n}{ec}")
                base = ec * 1024 + wi * 256

                def fsh(eng, semh, base=base):
                    eng.dma_start(out=awc_r[:, :, base:base + 128],
                                  in_=q_pp[0].h.ap().rearrange(
                                      "p (kc c) -> p kc c", kc=KC)
                                  ).then_inc(semh, 16)
                kb.dma("gpsimd", q_pp[0].name, fsh, reads=[q_pp[0]],
                       writes=[awc_blk[(wi, ec)]], tag=f"wsps_{n}{ec}h")

                def fsl(eng, semh, base=base):
                    eng.dma_start(out=awc_r[:, :, base + 128:base + 256],
                                  in_=q_pp[1].h.ap().rearrange(
                                      "p (kc c) -> p kc c", kc=KC)
                                  ).then_inc(semh, 16)
                kb.dma("gpsimd", q_pp[1].name, fsl, reads=[q_pp[1]],
                       writes=[awc_blk[(wi, ec)]], tag=f"wsps_{n}{ec}l")
            pend[0] = compute
            yield
        flush()

        F0W = 1376
        col = 0
        f0_ncols = {0: [], 1: [], 2: []}
        for wi, wr, tot in ((0, w1T_r, HP), (1, w3T_r, HP), (2, w2T_r, D)):
            nch = wr.shape[1]
            for kc in range(nch):
                for o0 in range(0, tot, F0W):
                    width = min(o0 + F0W, tot) - o0
                    buf = f0_pp[col % 2]
                    src = wr[:, kc, o0:o0 + width]

                    def fnl(eng, semh, src=src, width=width, buf=buf):
                        eng.dma_start(out=buf.h[:, 0:width], in_=src
                                      ).then_inc(semh, 16)
                    kb.dma("sync", buf.name, fnl, writes=[buf], tag="f0_load")
                    flush()

                    def compute(width=width, col=col, buf=buf):
                        kb.op("vector", lambda e: e.reduce_sum(
                                  stats_cols.h[:, col:col + 1],
                                  buf.h[:, 0:width], axis=mybir.AxisListType.X,
                                  apply_absolute_value=True),
                              reads=[buf], writes=[stats_cols], tag="f0_red")
                    pend[0] = compute
                    f0_ncols[wi].append(col)
                    col += 1
                    yield
        flush()

        for wi in range(3):
            cols = f0_ncols[wi]
            lo_, hi_ = min(cols), max(cols) + 1
            kb.op("vector", lambda e, wi=wi, lo_=lo_, hi_=hi_: e.reduce_sum(
                      stats_red.h[:, wi:wi + 1], stats_cols.h[:, lo_:hi_],
                      axis=mybir.AxisListType.X),
                  reads=[stats_cols], writes=[stats_red], tag=f"f0red{wi}")
            kb.op("tensor", lambda e, wi=wi: e.matmul(
                      pb[0].h[0:1, wi:wi + 1], ones_c.ap(),
                      stats_red.h[:, wi:wi + 1], start=True, stop=True),
                  reads=[stats_red, ones_c], writes=[pb[0]], tag=f"f0tot{wi}")
        cnt = float(D) * H
        kb.op("vector", lambda e: e.tensor_scalar_mul(
                  sval_tmp.h[0:1, 0:3], pb[0].h[0:1, 0:3], 1.0 / cnt),
              reads=[pb[0]], writes=[sval_tmp], tag="f0_s")
        kb.op("vector", lambda e: e.reciprocal(sval_tmp.h[0:1, 3:6],
                                               sval_tmp.h[0:1, 0:3]),
              reads=[sval_tmp], writes=[sval_tmp], tag="f0_invs")
        kb.op("tensor", lambda e: e.matmul(pb[1].h[:, 0:6], ones_r.ap(),
                                           sval_tmp.h[0:1, 0:6],
                                           start=True, stop=True),
              reads=[sval_tmp, ones_r], writes=[pb[1]], tag="f0_bc")
        kb.op("vector", lambda e: e.tensor_copy(svals.h[:, 0:6], pb[1].h[:, 0:6]),
              reads=[pb[1]], writes=[svals], tag="f0_sv")
        yield

        def q_step(i, in_ap_fn, width, wi, store_fn, tag):
            buf = ld_pp[i % 2]
            qb = q_pp[i % 2]

            def fl(eng, semh, buf=buf):
                eng.dma_start(out=buf.h[:, 0:width].rearrange(
                                  "p (n c) -> p n c", c=128),
                              in_=in_ap_fn()).then_inc(semh, 16)
            kb.dma("sync", buf.name, fl, writes=[buf], tag=tag + "_l")
            flush()

            def compute(buf=buf, qb=qb, width=width, wi=wi):
                inv = svals.h[:, 3 + wi:4 + wi]
                sap = buf.h[:, 0:width]
                qap = qb.h[:, 0:width]
                kb.op("scalar", lambda e: e.activation(
                          sap, sap, AF.Identity, bias=const_sb.h[:, 1:2],
                          scale=inv),
                      reads=[buf, svals, const_sb], writes=[buf], tag=tag + "_q1")
                kb.op("scalar", lambda e: e.activation(
                          sap, sap, AF.Identity, bias=const_sb.h[:, 2:3]),
                      reads=[buf, const_sb], writes=[buf], tag=tag + "_q2")
                kb.op("vector", lambda e: e.tensor_scalar(
                          qap, sap, 1.0, -1.0, ALU.min, ALU.max),
                      reads=[buf], writes=[qb], tag=tag + "_q3")

                def fs(eng, semh, qb=qb):
                    eng.dma_start(out=store_fn(),
                                  in_=qb.h[:, 0:width].rearrange(
                                      "p (n c) -> p n c", c=128)
                                  ).then_inc(semh, 16)
                kb.dma("gpsimd", qb.name, fs, reads=[qb], writes=[fqc, w2qc],
                       tag=tag + "_s")
            pend[0] = compute

        i = 0
        for hc in range(HC):
            for wi, wr in ((0, w1T_r), (1, w3T_r)):
                q_step(i,
                       lambda hc=hc, wr=wr: wr[:, :, hc * 128:(hc + 1) * 128],
                       KC * 128, wi,
                       lambda hc=hc, wi=wi: fqc_r[
                           :, :, hc * 256 + wi * 128:hc * 256 + (wi + 1) * 128],
                       f"q13_{hc}_{wi}")
                i += 1
                yield
        HCW = KC
        for dc in range(KC):
            for h0 in range(0, HC, HCW):
                h1_ = min(h0 + HCW, HC)
                w_ = (h1_ - h0) * 128
                q_step(i,
                       lambda dc=dc, h0=h0, h1_=h1_: w2T_r[
                           :, h0:h1_, dc * 128:(dc + 1) * 128],
                       w_, 2,
                       lambda dc=dc, h0=h0, h1_=h1_: w2qc_r[
                           :, h0:h1_, dc * 128:(dc + 1) * 128],
                       f"q2_{dc}_{h0}")
                i += 1
                yield
        flush()

    bg = bg_stream()

    def bg_step(n=1):
        for _ in range(n):
            if next(bg, "END") == "END":
                return False
        return True

    def layer_norm(get_x, lnhi, lnlo, tag):
        for kc in range(KC):
            xb, xap = get_x(kc)
            sq = sq_pp[kc % 2]
            kb.op("scalar", lambda e, xap=xap, sq=sq: e.activation(
                      sq.ap(), xap, AF.Square),
                  reads=[xb], writes=[sq], tag=tag + "_sq")

            def fn_mm(e, xap=xap, sq=sq, kc=kc):
                return [
                    e.matmul(pb[0].h[0:1, 0:TS], ones_c.ap(), xap,
                             start=(kc == 0), stop=(kc == KC - 1)),
                    e.matmul(pb[1].h[0:1, 0:TS], ones_c.ap(), sq.ap(),
                             start=(kc == 0), stop=(kc == KC - 1)),
                ]
            kb.op("tensor", fn_mm, reads=[xb, sq, ones_c], writes=[pb[0], pb[1]],
                  tag=tag + "_sums")
        m = stat.h[0:1, 0:TS]
        esq = stat.h[0:1, TS:2 * TS]
        m2 = stat.h[0:1, 2 * TS:3 * TS]
        var = stat.h[0:1, 3 * TS:4 * TS]
        sd = stat.h[0:1, 2 * TS:3 * TS]
        rr = stat.h[0:1, 4 * TS:5 * TS]
        kb.op("vector", lambda e: e.tensor_scalar_mul(m, pb[0].h[0:1, 0:TS], 1.0 / D),
              reads=[pb[0]], writes=[stat], tag=tag + "_m")
        kb.op("vector", lambda e: e.tensor_scalar_mul(esq, pb[1].h[0:1, 0:TS],
                                                      1.0 / D),
              reads=[pb[1]], writes=[stat], tag=tag + "_esq")
        kb.op("vector", lambda e: e.tensor_mul(m2, m, m), reads=[stat], writes=[stat],
              tag=tag + "_m2")
        kb.op("vector", lambda e: e.tensor_sub(var, esq, m2), reads=[stat],
              writes=[stat], tag=tag + "_var")
        kb.op("scalar", lambda e: e.activation(sd, var, AF.Sqrt,
                                               bias=const_sb.h[0:1, 0:1]),
              reads=[stat, const_sb], writes=[stat], tag=tag + "_sd")
        kb.op("vector", lambda e: e.reciprocal(rr, sd),
              reads=[stat], writes=[stat], tag=tag + "_rstd")
        kb.op("tensor", lambda e: e.matmul(pb[2].h[:, 0:TS], ones_r.ap(), m,
                                           start=True, stop=True),
              reads=[stat, ones_r], writes=[pb[2]], tag=tag + "_bm")
        kb.op("tensor", lambda e: e.matmul(pb[3].h[:, 0:TS], ones_r.ap(), rr,
                                           start=True, stop=True),
              reads=[stat, ones_r], writes=[pb[3]], tag=tag + "_br")
        for kc in range(KC):
            xb, xap = get_x(kc)
            t1 = sq_pp[kc % 2]
            l32 = ln32_pp[kc % 2]
            kb.op("vector", lambda e, xap=xap, t1=t1: e.tensor_sub(
                      t1.ap(), xap, pb[2].h[:, 0:TS]),
                  reads=[xb, pb[2]], writes=[t1], tag=tag + "_ap1")
            kb.op("vector", lambda e, t1=t1, l32=l32: e.tensor_mul(
                      l32.ap(), t1.ap(), pb[3].h[:, 0:TS]),
                  reads=[t1, pb[3]], writes=[l32], tag=tag + "_ap2")
            split_pair(l32, l32.ap(), lnhi, lnhi.h[:, kc * TS:(kc + 1) * TS],
                       lnlo, lnlo.h[:, kc * TS:(kc + 1) * TS], tag + f"_sp{kc}")

    ln_hi = kb.sbuf("ln_hi", [128, KC * TS], BF16, stack=stackA)
    ln_lo = kb.sbuf("ln_lo", [128, KC * TS], BF16, stack=stackA)
    ctx_hi = kb.sbuf("ctx_hi", [128, KC * TS], BF16, stack=stackA)
    ctx_lo = kb.sbuf("ctx_lo", [128, KC * TS], BF16, stack=stackA)
    wkv_col = [kb.sbuf(f"wkv{i}", [128, KC * 512], BF16, stack=stackA)
               for i in range(2)]
    wq_col = [kb.sbuf(f"wq{i}", [128, KC * 256], BF16, stack=stackA)
              for i in range(2)]
    t_min = kb.sbuf("t_min", [128, TS], stack=stackA)
    t_exp = kb.sbuf("t_exp", [128, TS], stack=stackA)
    t_phik = kb.sbuf("t_phik", [128, TS], stack=stackA)
    t_kv = kb.sbuf("t_kv", [128, TS], stack=stackA)
    t_cum = kb.sbuf("t_cum", [128, TS], stack=stackA)
    t_ctx = kb.sbuf("t_ctx", [128, TS], stack=stackA)
    x2o = [kb.sbuf(f"x2o{i}", [128, 2 * TS], stack=stackA) for i in range(2)]

    def make_x_stream(src_r, c0, tag):
        def get(kc):
            xs = xsg[(kc // 2) % 2]
            if kc % 2 == 0:
                def fl(eng, semh, kc=kc, xs=xs):
                    eng.dma_start(
                        out=xs.h.ap().rearrange("p (g t) -> p g t", g=2),
                        in_=src_r[:, kc:kc + 2, c0:c0 + TS]).then_inc(semh, 16)
                kb.dma("sync", xs.name, fl, writes=[xs], tag=tag + f"_{kc}")
            return xs, xs.h[:, (kc % 2) * TS:(kc % 2 + 1) * TS]
        return get

    def load_wcol(buf, ec, blocks, base, width, tag):
        for wi in blocks:
            assert awc_blk[(wi, ec)].writes, f"awc block ({wi},{ec}) not written"

        def fh(eng, semh, base=base, width=width, buf=buf):
            eng.dma_start(out=buf.h.ap().rearrange("p (kc c) -> p kc c", kc=KC),
                          in_=awc_r[:, :, base:base + width]).then_inc(semh, 16)
        kb.dma("gpsimd", buf.name, fh,
               reads=[awc_blk[(wi, ec)] for wi in blocks], writes=[buf], tag=tag)

    def proj3(psum_b, buf, stride, off, a_hi_b, a_lo_b, tag):
        def fn(e, buf=buf, psum_b=psum_b):
            r = []
            for kc in range(KC):
                w_hi = buf.h[:, kc * stride + off:kc * stride + off + 128]
                w_lo = buf.h[:, kc * stride + off + 128:kc * stride + off + 256]
                a_hi = a_hi_b.h[:, kc * TS:(kc + 1) * TS]
                a_lo = a_lo_b.h[:, kc * TS:(kc + 1) * TS]
                r.append(e.matmul(psum_b.h[:, 0:TS], w_hi, a_hi,
                                  start=(kc == 0), stop=False))
                r.append(e.matmul(psum_b.h[:, 0:TS], w_hi, a_lo,
                                  start=False, stop=False))
                r.append(e.matmul(psum_b.h[:, 0:TS], w_lo, a_hi,
                                  start=False, stop=(kc == KC - 1)))
            return r
        kb.op("tensor", fn, reads=[buf, a_hi_b, a_lo_b], writes=[psum_b], tag=tag)

    def phi_chain(psum_b, bias_col, out_phi, tag):
        bap = bias_sb.h[:, bias_col:bias_col + 1]
        kb.op("vector", lambda e: e.tensor_scalar(
                  t_min.ap(), psum_b.h[:, 0:TS], bap, 0.0, ALU.add, ALU.min),
              reads=[psum_b, bias_sb], writes=[t_min], tag=tag + "_min")
        kb.op("scalar", lambda e: e.activation(t_exp.ap(), t_min.ap(), AF.Exp),
              reads=[t_min], writes=[t_exp], tag=tag + "_exp")
        kb.op("scalar", lambda e: e.activation(t_min.ap(), psum_b.h[:, 0:TS],
                                               AF.Relu, bias=bap),
              reads=[psum_b, bias_sb], writes=[t_min], tag=tag + "_relu")
        kb.op("vector", lambda e: e.tensor_add(out_phi.ap(), t_exp.ap(), t_min.ap()),
              reads=[t_exp, t_min], writes=[out_phi], tag=tag + "_phi")

    bg_step(4)
    n_slices = cfg.n_pre + cfg.n_own
    for s in range(n_slices):
        is_pre = s < cfg.n_pre
        c0 = (s - (0 if is_pre else cfg.n_pre)) * TS
        src_r = xTpre_r if is_pre else xT_r

        layer_norm(lambda kc, g=make_x_stream(src_r, c0, f"xs{s}"): g(kc),
                   ln_hi, ln_lo, f"ln1_s{s}")

        def emit_wload(ec, s=s, is_pre=is_pre):
            load_wcol(wkv_col[ec % 2], ec, (0, 1), ec * 1024, 512,
                      f"wkv_s{s}e{ec}")
            if not is_pre:
                load_wcol(wq_col[ec % 2], ec, (2,), ec * 1024 + 512, 256,
                          f"wq_s{s}e{ec}")
        emit_wload(0)
        for ec in range(KC):
            bg_step(2)
            wkv = wkv_col[ec % 2]
            wq = wq_col[ec % 2]
            if ec + 1 < KC:
                emit_wload(ec + 1)
            pbq = pb[4] if ec % 2 == 0 else pb[2]
            pbk = pb[5] if ec % 2 == 0 else pb[7]
            pbv = pb[6] if ec % 2 == 0 else pb[3]

            proj3(pbk, wkv, 512, 0, ln_hi, ln_lo, f"mmk_s{s}e{ec}")
            proj3(pbv, wkv, 512, 256, ln_hi, ln_lo, f"mmv_s{s}e{ec}")
            if not is_pre:
                proj3(pbq, wq, 256, 0, ln_hi, ln_lo, f"mmq_s{s}e{ec}")

            phi_chain(pbk, KC + ec, t_phik, f"pk_s{s}e{ec}")
            kb.op("vector", lambda e, ec=ec, pbv=pbv: e.scalar_tensor_tensor(
                      t_kv.ap(), pbv.h[:, 0:TS],
                      bias_sb.h[:, 2 * KC + ec:2 * KC + ec + 1],
                      t_phik.ap(), ALU.add, ALU.mult),
                  reads=[pbv, bias_sb, t_phik], writes=[t_kv], tag=f"kv_s{s}e{ec}")
            kb.op("vector", lambda e, ec=ec: e.tensor_tensor_scan(
                      t_cum.ap(), t_kv.ap(), zeros_ts.ap(), prefix.h[:, ec:ec + 1],
                      ALU.add, ALU.add),
                  reads=[t_kv, zeros_ts, prefix], writes=[t_cum],
                  tag=f"scan_s{s}e{ec}")
            if is_pre:
                kb.op("vector", lambda e, ec=ec: e.tensor_scalar_mul(
                          prefix.h[:, ec:ec + 1], t_cum.h[:, TS - 1:TS],
                          mask_sb.h[:, 0:1]),
                      reads=[t_cum, mask_sb], writes=[prefix], tag=f"pfx_s{s}e{ec}")
            else:
                kb.op("vector", lambda e, ec=ec: e.tensor_copy(
                          prefix.h[:, ec:ec + 1], t_cum.h[:, TS - 1:TS]),
                      reads=[t_cum], writes=[prefix], tag=f"pfx_s{s}e{ec}")

            if not is_pre:
                phi_chain(pbq, ec, t_phik, f"pq_s{s}e{ec}")
                kb.op("vector", lambda e: e.scalar_tensor_tensor(
                          t_ctx.ap(), t_cum.ap(), cfg.SCALE, t_phik.ap(),
                          ALU.mult, ALU.mult),
                      reads=[t_cum, t_phik], writes=[t_ctx], tag=f"ctx_s{s}e{ec}")
                split_pair(t_ctx, t_ctx.ap(),
                           ctx_hi, ctx_hi.h[:, ec * TS:(ec + 1) * TS],
                           ctx_lo, ctx_lo.h[:, ec * TS:(ec + 1) * TS],
                           f"ctxsp_s{s}e{ec}")

        if not is_pre:
            get_xr = make_x_stream(src_r, c0, f"xr{s}")
            load_wcol(wq_col[0], 0, (3,), 768, 256, f"wo_s{s}e0")
            for e2c in range(KC):
                wo = wq_col[e2c % 2]
                if e2c + 1 < KC:
                    load_wcol(wq_col[(e2c + 1) % 2], e2c + 1, (3,),
                              (e2c + 1) * 1024 + 768, 256, f"wo_s{s}e{e2c + 1}")
                pbo = pb[7] if e2c % 2 == 0 else pb[4]
                proj3(pbo, wo, 256, 0, ctx_hi, ctx_lo, f"mmo_s{s}e{e2c}")
                xres, xres_ap = get_xr(e2c)
                xo = x2o[(e2c // 2) % 2]
                sub = (e2c % 2) * TS
                kb.op("vector", lambda e, e2c=e2c, xo=xo, sub=sub, xres=xres,
                      xres_ap=xres_ap, pbo=pbo: e.scalar_tensor_tensor(
                          xo.h[:, sub:sub + TS], pbo.h[:, 0:TS],
                          bias_sb.h[:, 3 * KC + e2c:3 * KC + e2c + 1],
                          xres_ap, ALU.add, ALU.add),
                      reads=[pbo, bias_sb, xres], writes=[xo],
                      tag=f"x2_s{s}e{e2c}")
                if e2c % 2 == 1:
                    def fn_st(eng, semh, xo=xo, e2c=e2c, c0=c0):
                        eng.dma_start(
                            out=x2T_r[:, e2c - 1:e2c + 1, c0:c0 + TS],
                            in_=xo.h.ap().rearrange("p (g t) -> p g t", g=2)
                            ).then_inc(semh, 16)
                    kb.dma("gpsimd", xo.name, fn_st, reads=[xo], writes=[x2T],
                           tag=f"x2st_s{s}e{e2c}")

    while bg_step():
        pass

    kb.barrier()
    stackA.close()

    ln2_hi = kb.sbuf("ln2_hi", [128, KC * TS], BF16)
    ln2_lo = kb.sbuf("ln2_lo", [128, KC * TS], BF16)
    g_hi = kb.sbuf("g_hi", [128, HC * TS], BF16)
    g_lo = kb.sbuf("g_lo", [128, HC * TS], BF16)
    q13c = [kb.sbuf(f"q13c{i}", [128, KC * 256], BF16) for i in range(2)]
    w2qb = [kb.sbuf(f"w2qb{i}", [128, HC * 128], BF16) for i in range(2)]
    silu_b = kb.sbuf("silu_b", [128, TS])
    g32 = kb.sbuf("g32", [128, TS])
    x2rd = [kb.sbuf(f"x2rd{i}", [128, TS]) for i in range(2)]
    ob = [kb.sbuf(f"ob{i}", [128, 2 * TS]) for i in range(2)]

    for fs in range(cfg.n_own):
        c0 = fs * TS

        layer_norm(lambda kc, g=make_x_stream(x2T_r, c0, f"x2s{fs}"): g(kc),
                   ln2_hi, ln2_lo, f"ln2_s{fs}")

        for hc in range(HC):
            qb = q13c[hc % 2]

            def fn_q13(eng, semh, qb=qb, hc=hc):
                eng.dma_start(out=qb.h.ap().rearrange("p (kc c) -> p kc c", kc=KC),
                              in_=fqc_r[:, :, hc * 256:(hc + 1) * 256]
                              ).then_inc(semh, 16)
            kb.dma("gpsimd", qb.name, fn_q13, reads=[fqc], writes=[qb],
                   tag=f"q13_{fs}_{hc}")

            ph1 = pb[4 + (hc % 2) * 2]
            ph3 = pb[5 + (hc % 2) * 2]

            def fn_h(e, qb, p, off):
                r = []
                for kc in range(KC):
                    w_ap = qb.h[:, kc * 256 + off:kc * 256 + off + 128]
                    r.append(e.matmul(p.h[:, 0:TS], w_ap,
                                      ln2_hi.h[:, kc * TS:(kc + 1) * TS],
                                      start=(kc == 0), stop=False))
                    r.append(e.matmul(p.h[:, 0:TS], w_ap,
                                      ln2_lo.h[:, kc * TS:(kc + 1) * TS],
                                      start=False, stop=(kc == KC - 1)))
                return r
            kb.op("tensor", lambda e, qb=qb, p=ph1: fn_h(e, qb, p, 0),
                  reads=[qb, ln2_hi, ln2_lo], writes=[ph1], tag=f"mmh1_{fs}_{hc}")
            kb.op("tensor", lambda e, qb=qb, p=ph3: fn_h(e, qb, p, 128),
                  reads=[qb, ln2_hi, ln2_lo], writes=[ph3], tag=f"mmh3_{fs}_{hc}")

            kb.op("scalar", lambda e, ph1=ph1: e.activation(
                      silu_b.ap(), ph1.h[:, 0:TS], AF.Silu, scale=svals.h[:, 0:1]),
                  reads=[ph1, svals], writes=[silu_b], tag=f"silu_{fs}_{hc}")
            kb.op("vector", lambda e, ph3=ph3: e.scalar_tensor_tensor(
                      g32.ap(), ph3.h[:, 0:TS], svals.h[:, 1:2], silu_b.ap(),
                      ALU.mult, ALU.mult),
                  reads=[ph3, svals, silu_b], writes=[g32], tag=f"g_{fs}_{hc}")
            split_pair(g32, g32.ap(), g_hi, g_hi.h[:, hc * TS:(hc + 1) * TS],
                       g_lo, g_lo.h[:, hc * TS:(hc + 1) * TS], f"gsp_{fs}_{hc}")

        for dc in range(KC):
            wq2 = w2qb[dc % 2]

            def fn_w2(eng, semh, wq2=wq2, dc=dc):
                eng.dma_start(out=wq2.h.ap().rearrange("p (hc c) -> p hc c", c=128),
                              in_=w2qc_r[:, :, dc * 128:(dc + 1) * 128]
                              ).then_inc(semh, 16)
            kb.dma("gpsimd", wq2.name, fn_w2, reads=[w2qc], writes=[wq2],
                   tag=f"w2q_{fs}_{dc}")

            xr = x2rd[dc % 2]

            def fn_xr(eng, semh, xr=xr, dc=dc, c0=c0):
                eng.dma_start(out=xr.ap(), in_=x2T_r[:, dc, c0:c0 + TS]
                              ).then_inc(semh, 16)
            kb.dma("sync", xr.name, fn_xr, reads=[x2T], writes=[xr],
                   tag=f"xr{fs}_{dc}")

            pf = pb[dc % 4]

            def fn_f(e, pf=pf, wq2=wq2):
                r = []
                for hc in range(HC):
                    w_ap = wq2.h[:, hc * 128:(hc + 1) * 128]
                    r.append(e.matmul(pf.h[:, 0:TS], w_ap,
                                      g_hi.h[:, hc * TS:(hc + 1) * TS],
                                      start=(hc == 0), stop=False))
                    r.append(e.matmul(pf.h[:, 0:TS], w_ap,
                                      g_lo.h[:, hc * TS:(hc + 1) * TS],
                                      start=False, stop=(hc == HC - 1)))
                return r
            kb.op("tensor", fn_f, reads=[wq2, g_hi, g_lo], writes=[pf],
                  tag=f"mmf{fs}_{dc}")

            o = ob[(dc // 2) % 2]
            sub = (dc % 2) * TS
            kb.op("vector", lambda e, pf=pf, xr=xr, o=o, sub=sub:
                  e.scalar_tensor_tensor(
                      o.h[:, sub:sub + TS], pf.h[:, 0:TS], svals.h[:, 2:3],
                      xr.ap(), ALU.mult, ALU.add),
                  reads=[pf, svals, xr], writes=[o], tag=f"out{fs}_{dc}")
            if dc % 2 == 1:
                def fn_ost(eng, semh, o=o, dc=dc, c0=c0):
                    eng.dma_start(out=outT_r[:, dc - 1:dc + 1, c0:c0 + TS],
                                  in_=o.h.ap().rearrange("p (g t) -> p g t", g=2)
                                  ).then_inc(semh, 16)
                kb.dma("gpsimd", o.name, fn_ost, reads=[o], writes=[outT],
                       tag=f"ost{fs}_{dc}")

    kb.raw_wait("gpsimd", extra=[(s, kb.semc[s]) for s in
                                 ("d_ob0", "d_ob1", "d_x2o0", "d_x2o1")
                                 if s in kb.semc])
    kb.emit()
    return nc




def kernel(L0_attn_x, L0_attn_Wq, L0_attn_bq, L0_attn_Wk, L0_attn_bk,
           L0_attn_Wv, L0_attn_bv, L0_attn_Wo, L0_attn_bo,
           L0_ffn_w1, L0_ffn_w3, L0_ffn_w2):
    cfg = CFG
    D, T, P, H, HP = cfg.D, cfg.T, cfg.P, cfg.H, cfg.HP
    x = np.ascontiguousarray(np.asarray(L0_attn_x, dtype=np.float32))
    B, L, _ = x.shape

    def t32(a):
        return np.ascontiguousarray(np.asarray(a, dtype=np.float32).T)

    wqT, wkT, wvT, woT = (t32(L0_attn_Wq), t32(L0_attn_Wk),
                          t32(L0_attn_Wv), t32(L0_attn_Wo))
    w1T = np.zeros((D, HP), np.float32)
    w1T[:, :H] = t32(L0_ffn_w1)
    w3T = np.zeros((D, HP), np.float32)
    w3T[:, :H] = t32(L0_ffn_w3)
    w2T = np.zeros((HP, D), np.float32)
    w2T[:H, :] = t32(L0_ffn_w2)
    zeros_pre = np.zeros((D, P), np.float32)

    in_maps = []
    for c in range(8):
        b, h = c // 2, c % 2
        xTc = np.ascontiguousarray(x[b, h * T:(h + 1) * T, :].T)
        if h == 1:
            xTpre = np.ascontiguousarray(x[b, 0:P, :].T)
            mask = np.ones((128, 1), np.float32)
        else:
            xTpre = zeros_pre
            mask = np.zeros((128, 1), np.float32)
        in_maps.append({
            "xT": xTc, "xTpre": xTpre, "pre_mask": mask,
            "wqT": wqT, "wkT": wkT, "wvT": wvT, "woT": woT,
            "bq": np.asarray(L0_attn_bq, np.float32),
            "bk": np.asarray(L0_attn_bk, np.float32),
            "bv": np.asarray(L0_attn_bv, np.float32),
            "bo": np.asarray(L0_attn_bo, np.float32),
            "w1T": w1T, "w3T": w3T, "w2T": w2T,
        })

    nc = build_nc(cfg)
    res = run_bass_kernel_spmd(nc, in_maps, core_ids=list(range(8)))

    out = np.empty((B, L, D), np.float32)
    for c in range(8):
        b, h = c // 2, c % 2
        out[b, h * T:(h + 1) * T, :] = np.asarray(res.results[c]["outT"]).T
    return out


# version 27
# speedup vs baseline: 1.2045x; 1.0376x over previous
from contextlib import ExitStack

import numpy as np

import concourse.bass as bass
import concourse.mybir as mybir
from concourse.bass_utils import run_bass_kernel_spmd

F32 = mybir.dt.float32
BF16 = mybir.dt.bfloat16
AF = mybir.ActivationFunctionType
ALU = mybir.AluOpType


class Cfg:
    def __init__(self, D=2048, T=2048, P=2048, H=5461, TS=512):
        self.D, self.T, self.P, self.H, self.TS = D, T, P, H, TS
        self.KC = D // 128
        self.HC = (H + 127) // 128
        self.HP = self.HC * 128
        self.n_pre = P // TS
        self.n_own = T // TS
        self.SCALE = 128.0 ** -0.5
        self.EPS = 1e-5
        self.RC = 12582912.0


CFG = Cfg()

ENGINES = ("tensor", "vector", "scalar", "sync", "gpsimd")
ENG_SEM = {e: "E_" + e for e in ENGINES}


class Buf:

    def __init__(self, kb, handle, name):
        self.kb = kb
        self.h = handle
        self.name = name
        self.writes = {}
        self.readers = {}

    def ap(self):
        return self.h[:]

    def _merge(self, d, tk):
        s, v = tk
        if d.get(s, 0) < v:
            d[s] = v


class KB:
    def __init__(self, nc, stack):
        self.nc = nc
        self.stack = stack
        self.semh = {}
        self.semc = {}
        self.ops = {e: [] for e in ENGINES}
        self.waited = {e: {} for e in ENGINES}
        self.barrier_deps = {}
        for e in ENGINES:
            self._sem(ENG_SEM[e])

    def _sem(self, name):
        if name not in self.semh:
            self.semh[name] = self.stack.enter_context(self.nc.semaphore(name))
            self.semc[name] = 0
        return name

    def sbuf(self, name, shape, dtype=F32, stack=None):
        h = (stack or self.stack).enter_context(
            self.nc.sbuf_tensor(name, list(shape), dtype))
        return Buf(self, h, name)

    def psum(self, name, shape=(128, 512), dtype=F32):
        h = self.stack.enter_context(self.nc.psum_tensor(name, list(shape), dtype))
        return Buf(self, h, name)

    def dram_buf(self, name, shape, dtype=F32, kind=None):
        if kind is None:
            h = self.nc.dram_tensor(name, list(shape), dtype)
        else:
            h = self.nc.dram_tensor(name, list(shape), dtype, kind=kind)
        return Buf(self, h, name)

    def barrier(self):
        self.barrier_deps = dict(self.semc)

    def _collect(self, engine, reads, writes):
        deps = {}
        for s, v in self.barrier_deps.items():
            if deps.get(s, 0) < v:
                deps[s] = v
        for b in reads:
            for s, v in b.writes.items():
                if deps.get(s, 0) < v:
                    deps[s] = v
        for b in writes:
            for d in (b.writes, b.readers):
                for s, v in d.items():
                    if deps.get(s, 0) < v:
                        deps[s] = v
        waits = []
        wt = self.waited[engine]
        for s in sorted(deps):
            v = deps[s]
            if self.semc.get(s, 0) < v:
                raise RuntimeError(f"dep on future ticket {s}={v}")
            if wt.get(s, 0) < v:
                waits.append((s, v))
                wt[s] = v
        return waits

    def op(self, engine, fn, reads=(), writes=(), tag=""):
        waits = self._collect(engine, reads, writes)
        sem = ENG_SEM[engine]
        self.semc[sem] += 1
        tk = (sem, self.semc[sem])
        for b in writes:
            b._merge(b.writes, tk)
        for b in reads:
            b._merge(b.readers, tk)
        self.ops[engine].append((waits, fn, sem, 1, tag))
        return tk

    def dma(self, engine, buf_sem_owner, fn, reads=(), writes=(), tag=""):
        sem = self._sem("d_" + buf_sem_owner)
        waits = self._collect(engine, reads, writes)
        self.semc[sem] += 16
        tk = (sem, self.semc[sem])
        for b in writes:
            b._merge(b.writes, tk)
        for b in reads:
            b._merge(b.readers, tk)
        self.ops[engine].append((waits, fn, sem, None, tag))
        return tk

    def raw_wait(self, engine, extra=()):
        deps = {}
        for s, v in extra:
            if deps.get(s, 0) < v:
                deps[s] = v
        waits = []
        wt = self.waited[engine]
        for s in sorted(deps):
            v = deps[s]
            if wt.get(s, 0) < v:
                waits.append((s, v))
                wt[s] = v
        self.ops[engine].append((waits, None, None, None, "final_wait"))

    def emit(self):
        nc = self.nc
        with nc.Block() as block:
            def emitter(engine):
                def run(eng):
                    for waits, fn, sem, amt, tag in self.ops[engine]:
                        for s, v in waits:
                            eng.wait_ge(self.semh[s], v)
                        if fn is None:
                            continue
                        if amt is None:
                            fn(eng, self.semh[sem])
                        else:
                            r = fn(eng)
                            r = r[-1] if isinstance(r, (list, tuple)) else r
                            r.then_inc(self.semh[sem], amt)
                return run
            block.tensor(emitter("tensor"))
            block.vector(emitter("vector"))
            block.scalar(emitter("scalar"))
            block.sync(emitter("sync"))
            block.gpsimd(emitter("gpsimd"))




def build_nc(cfg=CFG):
    nc = bass.Bass("TRN2", target_bir_lowering=False)
    stack = ExitStack()
    kb = KB(nc, stack)
    D, T, P, H, TS = cfg.D, cfg.T, cfg.P, cfg.H, cfg.TS
    KC, HC, HP = cfg.KC, cfg.HC, cfg.HP

    xT = nc.dram_tensor("xT", [D, T], F32, kind="ExternalInput")
    xTpre = nc.dram_tensor("xTpre", [D, P], F32, kind="ExternalInput")
    wnames = ("wkT", "wvT", "wqT", "woT")
    wT = {n: nc.dram_tensor(n, [D, D], F32, kind="ExternalInput") for n in wnames}
    biases = {n: nc.dram_tensor(n, [D], F32, kind="ExternalInput")
              for n in ("bq", "bk", "bv", "bo")}
    w1T = nc.dram_tensor("w1T", [D, HP], F32, kind="ExternalInput")
    w3T = nc.dram_tensor("w3T", [D, HP], F32, kind="ExternalInput")
    w2T = nc.dram_tensor("w2T", [HP, D], F32, kind="ExternalInput")
    pre_mask = nc.dram_tensor("pre_mask", [128, 1], F32, kind="ExternalInput")
    outT = kb.dram_buf("outT", [D, T], kind="ExternalOutput")
    x2T = kb.dram_buf("x2T", [D, T])
    awc = nc.dram_tensor("awc", [D, 8 * D], BF16)
    awc_blk = {(wi, ec): Buf(kb, awc, f"awc{wi}_{ec}")
               for wi in range(4) for ec in range(KC)}
    fqc = kb.dram_buf("fqc", [D, 2 * HP], BF16)
    w2qc = kb.dram_buf("w2qc", [HP, D], BF16)

    def r3(ap, p=128):
        return ap.rearrange("(kc p) t -> p kc t", p=p)

    xT_r = r3(xT.ap())
    xTpre_r = r3(xTpre.ap())
    wT_r = {n: r3(wT[n].ap()) for n in wnames}
    awc_r = r3(awc.ap())
    fqc_r = r3(fqc.h.ap())
    w2qc_r = w2qc.h.ap().rearrange("(hc p) d -> p hc d", p=128)
    w1T_r = r3(w1T.ap())
    w3T_r = r3(w3T.ap())
    w2T_r = w2T.ap().rearrange("(hc p) d -> p hc d", p=128)
    x2T_r = r3(x2T.h.ap())
    outT_r = r3(outT.h.ap())

    ones_c = kb.sbuf("ones_c", [128, 1])
    ones_r = kb.sbuf("ones_r", [1, 128])
    bias_sb = kb.sbuf("bias_sb", [128, 4 * KC])
    mask_sb = kb.sbuf("mask_sb", [128, 1])
    prefix = kb.sbuf("prefix", [128, KC])
    stat = kb.sbuf("stat", [1, 5 * TS])
    svals = kb.sbuf("svals", [128, 8])
    sval_tmp = kb.sbuf("sval_tmp", [1, 8])
    stats_cols = kb.sbuf("stats_cols", [128, 224])
    stats_red = kb.sbuf("stats_red", [128, 3])
    const_sb = kb.sbuf("const_sb", [128, 3])
    sq_pp = [kb.sbuf(f"sq{i}", [128, TS]) for i in range(2)]
    ln32_pp = [kb.sbuf(f"ln32_{i}", [128, TS]) for i in range(2)]
    xsg = [kb.sbuf(f"xsg{i}", [128, 2 * TS]) for i in range(2)]

    pb = [kb.psum(f"pb{i}", (128, TS)) for i in range(8)]

    kb.op("vector", lambda e: e.memset(ones_c.ap(), 1.0), writes=[ones_c], tag="init")
    kb.op("vector", lambda e: e.memset(ones_r.ap(), 1.0), writes=[ones_r], tag="init")
    kb.op("vector", lambda e: e.memset(prefix.ap(), 0.0), writes=[prefix], tag="init")
    kb.op("vector", lambda e: e.memset(const_sb.h[:, 0:1], cfg.EPS),
          writes=[const_sb], tag="init")
    kb.op("vector", lambda e: e.memset(const_sb.h[:, 2:3], -cfg.RC),
          writes=[const_sb], tag="init")
    kb.op("vector", lambda e: e.memset(const_sb.h[:, 1:2], cfg.RC),
          writes=[const_sb], tag="init")

    def load_simple(buf, out_ap, in_ap, tag=""):
        def fn(eng, semh, out_ap=out_ap, in_ap=in_ap):
            eng.dma_start(out=out_ap, in_=in_ap,
                          allow_slow_non_contiguous=True).then_inc(semh, 16)
        return kb.dma("sync", buf.name, fn, writes=[buf], tag=tag)

    for j, n in enumerate(("bq", "bk", "bv", "bo")):
        load_simple(bias_sb, bias_sb.h[:, j * KC:(j + 1) * KC],
                    biases[n].ap().rearrange("(kc p) -> p kc", p=128), tag="bias")
    load_simple(mask_sb, mask_sb.ap(), pre_mask.ap(), tag="mask")

    def split_pair(src_buf, src_ap, hi_buf, hi_ap, lo_buf, lo_ap, tag):
        kb.op("scalar", lambda e: e.copy(hi_ap, src_ap),
              reads=[src_buf], writes=[hi_buf], tag=tag + "_hi")
        kb.op("vector",
              lambda e: e.scalar_tensor_tensor(lo_ap, src_ap, 1.0, hi_ap,
                                               ALU.mult, ALU.subtract),
              reads=[src_buf, hi_buf], writes=[lo_buf], tag=tag + "_lo")

    stackA = ExitStack()
    zeros_ts = kb.sbuf("zeros_ts", [128, TS], stack=stackA)
    kb.op("vector", lambda e: e.memset(zeros_ts.ap(), 0.0), writes=[zeros_ts],
          tag="init")

    f0_pp = [kb.sbuf(f"f0_{i}", [128, 1376], stack=stackA) for i in range(2)]
    ld_pp = [kb.sbuf(f"bgld{i}", [128, KC * 128], stack=stackA) for i in range(2)]
    q_pp = [kb.sbuf(f"bgq{i}", [128, KC * 128], BF16, stack=stackA)
            for i in range(2)]

    def bg_stream():
        pend = [None]

        def flush():
            if pend[0] is not None:
                pend[0]()
                pend[0] = None

        worder = [(wi, n, ec) for ec in range(KC) for wi, n in ((0, "wkT"),
                                                               (1, "wvT"))]
        worder += [(2, "wqT", ec) for ec in range(KC)]
        worder += [(3, "woT", ec) for ec in range(KC)]
        for i, (wi, n, ec) in enumerate(worder):
            buf = ld_pp[i % 2]

            def fnl(eng, semh, n=n, ec=ec, buf=buf):
                eng.dma_start(out=buf.ap(), in_=wT_r[n][:, :, ec * 128:(ec + 1) * 128]
                              ).then_inc(semh, 16)
            kb.dma("sync", buf.name, fnl, writes=[buf], tag=f"wspl_{n}{ec}")
            flush()

            def compute(wi=wi, n=n, ec=ec, buf=buf):
                split_pair(buf, buf.ap(), q_pp[0], q_pp[0].ap(),
                           q_pp[1], q_pp[1].ap(), f"wsp_{n}{ec}")
                base = ec * 1024 + wi * 256

                def fsh(eng, semh, base=base):
                    eng.dma_start(out=awc_r[:, :, base:base + 128],
                                  in_=q_pp[0].h.ap().rearrange(
                                      "p (kc c) -> p kc c", kc=KC)
                                  ).then_inc(semh, 16)
                kb.dma("gpsimd", q_pp[0].name, fsh, reads=[q_pp[0]],
                       writes=[awc_blk[(wi, ec)]], tag=f"wsps_{n}{ec}h")

                def fsl(eng, semh, base=base):
                    eng.dma_start(out=awc_r[:, :, base + 128:base + 256],
                                  in_=q_pp[1].h.ap().rearrange(
                                      "p (kc c) -> p kc c", kc=KC)
                                  ).then_inc(semh, 16)
                kb.dma("gpsimd", q_pp[1].name, fsl, reads=[q_pp[1]],
                       writes=[awc_blk[(wi, ec)]], tag=f"wsps_{n}{ec}l")
            pend[0] = compute
            yield
        flush()

        F0W = 1376
        col = 0
        f0_ncols = {0: [], 1: [], 2: []}
        for wi, wr, tot in ((0, w1T_r, HP), (1, w3T_r, HP), (2, w2T_r, D)):
            nch = wr.shape[1]
            for kc in range(nch):
                for o0 in range(0, tot, F0W):
                    width = min(o0 + F0W, tot) - o0
                    buf = f0_pp[col % 2]
                    src = wr[:, kc, o0:o0 + width]

                    def fnl(eng, semh, src=src, width=width, buf=buf):
                        eng.dma_start(out=buf.h[:, 0:width], in_=src
                                      ).then_inc(semh, 16)
                    kb.dma("sync", buf.name, fnl, writes=[buf], tag="f0_load")
                    flush()

                    def compute(width=width, col=col, buf=buf):
                        kb.op("vector", lambda e: e.reduce_sum(
                                  stats_cols.h[:, col:col + 1],
                                  buf.h[:, 0:width], axis=mybir.AxisListType.X,
                                  apply_absolute_value=True),
                              reads=[buf], writes=[stats_cols], tag="f0_red")
                    pend[0] = compute
                    f0_ncols[wi].append(col)
                    col += 1
                    yield
        flush()

        for wi in range(3):
            cols = f0_ncols[wi]
            lo_, hi_ = min(cols), max(cols) + 1
            kb.op("vector", lambda e, wi=wi, lo_=lo_, hi_=hi_: e.reduce_sum(
                      stats_red.h[:, wi:wi + 1], stats_cols.h[:, lo_:hi_],
                      axis=mybir.AxisListType.X),
                  reads=[stats_cols], writes=[stats_red], tag=f"f0red{wi}")
            kb.op("tensor", lambda e, wi=wi: e.matmul(
                      pb[7].h[0:1, wi:wi + 1], ones_c.ap(),
                      stats_red.h[:, wi:wi + 1], start=True, stop=True),
                  reads=[stats_red, ones_c], writes=[pb[7]], tag=f"f0tot{wi}")
        cnt = float(D) * H
        kb.op("vector", lambda e: e.tensor_scalar_mul(
                  sval_tmp.h[0:1, 0:3], pb[7].h[0:1, 0:3], 1.0 / cnt),
              reads=[pb[7]], writes=[sval_tmp], tag="f0_s")
        kb.op("vector", lambda e: e.reciprocal(sval_tmp.h[0:1, 3:6],
                                               sval_tmp.h[0:1, 0:3]),
              reads=[sval_tmp], writes=[sval_tmp], tag="f0_invs")
        kb.op("tensor", lambda e: e.matmul(pb[7].h[:, 8:14], ones_r.ap(),
                                           sval_tmp.h[0:1, 0:6],
                                           start=True, stop=True),
              reads=[sval_tmp, ones_r], writes=[pb[7]], tag="f0_bc")
        kb.op("vector", lambda e: e.tensor_copy(svals.h[:, 0:6], pb[7].h[:, 8:14]),
              reads=[pb[7]], writes=[svals], tag="f0_sv")
        yield

        def q_step(i, in_ap_fn, width, wi, store_fn, tag):
            buf = ld_pp[i % 2]
            qb = q_pp[i % 2]

            def fl(eng, semh, buf=buf):
                eng.dma_start(out=buf.h[:, 0:width].rearrange(
                                  "p (n c) -> p n c", c=128),
                              in_=in_ap_fn()).then_inc(semh, 16)
            kb.dma("sync", buf.name, fl, writes=[buf], tag=tag + "_l")
            flush()

            def compute(buf=buf, qb=qb, width=width, wi=wi):
                inv = svals.h[:, 3 + wi:4 + wi]
                sap = buf.h[:, 0:width]
                qap = qb.h[:, 0:width]
                kb.op("scalar", lambda e: e.activation(
                          sap, sap, AF.Identity, bias=const_sb.h[:, 1:2],
                          scale=inv),
                      reads=[buf, svals, const_sb], writes=[buf], tag=tag + "_q1")
                kb.op("scalar", lambda e: e.activation(
                          sap, sap, AF.Identity, bias=const_sb.h[:, 2:3]),
                      reads=[buf, const_sb], writes=[buf], tag=tag + "_q2")
                kb.op("vector", lambda e: e.tensor_scalar(
                          qap, sap, 1.0, -1.0, ALU.min, ALU.max),
                      reads=[buf], writes=[qb], tag=tag + "_q3")

                def fs(eng, semh, qb=qb):
                    eng.dma_start(out=store_fn(),
                                  in_=qb.h[:, 0:width].rearrange(
                                      "p (n c) -> p n c", c=128)
                                  ).then_inc(semh, 16)
                kb.dma("gpsimd", qb.name, fs, reads=[qb], writes=[fqc, w2qc],
                       tag=tag + "_s")
            pend[0] = compute

        i = 0
        for hc in range(HC):
            for wi, wr in ((0, w1T_r), (1, w3T_r)):
                q_step(i,
                       lambda hc=hc, wr=wr: wr[:, :, hc * 128:(hc + 1) * 128],
                       KC * 128, wi,
                       lambda hc=hc, wi=wi: fqc_r[
                           :, :, hc * 256 + wi * 128:hc * 256 + (wi + 1) * 128],
                       f"q13_{hc}_{wi}")
                i += 1
                yield
        HCW = KC
        for dc in range(KC):
            for h0 in range(0, HC, HCW):
                h1_ = min(h0 + HCW, HC)
                w_ = (h1_ - h0) * 128
                q_step(i,
                       lambda dc=dc, h0=h0, h1_=h1_: w2T_r[
                           :, h0:h1_, dc * 128:(dc + 1) * 128],
                       w_, 2,
                       lambda dc=dc, h0=h0, h1_=h1_: w2qc_r[
                           :, h0:h1_, dc * 128:(dc + 1) * 128],
                       f"q2_{dc}_{h0}")
                i += 1
                yield
        flush()

    bg = bg_stream()

    def bg_step(n=1):
        for _ in range(n):
            if next(bg, "END") == "END":
                return False
        return True

    def ln_load_sq(get_x, kc, tag):
        xb, xap = get_x(kc)
        sq = sq_pp[kc % 2]
        kb.op("scalar", lambda e, xap=xap, sq=sq: e.activation(
                  sq.ap(), xap, AF.Square),
              reads=[xb], writes=[sq], tag=tag + "_sq")

    def ln_mm(get_x, kc, tag):
        xb, xap = get_x(kc, load=False)
        sq = sq_pp[kc % 2]

        def fn_mm(e, xap=xap, sq=sq, kc=kc):
            return [
                e.matmul(pb[0].h[0:1, 0:TS], ones_c.ap(), xap,
                         start=(kc == 0), stop=(kc == KC - 1)),
                e.matmul(pb[1].h[0:1, 0:TS], ones_c.ap(), sq.ap(),
                         start=(kc == 0), stop=(kc == KC - 1)),
            ]
        kb.op("tensor", fn_mm, reads=[xb, sq, ones_c], writes=[pb[0], pb[1]],
              tag=tag + "_sums")

    def ln_finish(get_x, lnhi, lnlo, tag):
        m = stat.h[0:1, 0:TS]
        esq = stat.h[0:1, TS:2 * TS]
        m2 = stat.h[0:1, 2 * TS:3 * TS]
        var = stat.h[0:1, 3 * TS:4 * TS]
        sd = stat.h[0:1, 2 * TS:3 * TS]
        rr = stat.h[0:1, 4 * TS:5 * TS]
        kb.op("vector", lambda e: e.tensor_scalar_mul(m, pb[0].h[0:1, 0:TS], 1.0 / D),
              reads=[pb[0]], writes=[stat], tag=tag + "_m")
        kb.op("vector", lambda e: e.tensor_scalar_mul(esq, pb[1].h[0:1, 0:TS],
                                                      1.0 / D),
              reads=[pb[1]], writes=[stat], tag=tag + "_esq")
        kb.op("vector", lambda e: e.tensor_mul(m2, m, m), reads=[stat], writes=[stat],
              tag=tag + "_m2")
        kb.op("vector", lambda e: e.tensor_sub(var, esq, m2), reads=[stat],
              writes=[stat], tag=tag + "_var")
        kb.op("scalar", lambda e: e.activation(sd, var, AF.Sqrt,
                                               bias=const_sb.h[0:1, 0:1]),
              reads=[stat, const_sb], writes=[stat], tag=tag + "_sd")
        kb.op("vector", lambda e: e.reciprocal(rr, sd),
              reads=[stat], writes=[stat], tag=tag + "_rstd")
        kb.op("tensor", lambda e: e.matmul(pb[2].h[:, 0:TS], ones_r.ap(), m,
                                           start=True, stop=True),
              reads=[stat, ones_r], writes=[pb[2]], tag=tag + "_bm")
        kb.op("tensor", lambda e: e.matmul(pb[3].h[:, 0:TS], ones_r.ap(), rr,
                                           start=True, stop=True),
              reads=[stat, ones_r], writes=[pb[3]], tag=tag + "_br")
        for kc in range(KC):
            xb, xap = get_x(kc)
            t1 = sq_pp[kc % 2]
            l32 = ln32_pp[kc % 2]
            kb.op("vector", lambda e, xap=xap, t1=t1: e.tensor_sub(
                      t1.ap(), xap, pb[2].h[:, 0:TS]),
                  reads=[xb, pb[2]], writes=[t1], tag=tag + "_ap1")
            kb.op("vector", lambda e, t1=t1, l32=l32: e.tensor_mul(
                      l32.ap(), t1.ap(), pb[3].h[:, 0:TS]),
                  reads=[t1, pb[3]], writes=[l32], tag=tag + "_ap2")
            split_pair(l32, l32.ap(), lnhi, lnhi.h[:, kc * TS:(kc + 1) * TS],
                       lnlo, lnlo.h[:, kc * TS:(kc + 1) * TS], tag + f"_sp{kc}")

    def layer_norm(get_x, lnhi, lnlo, tag):
        for kc in range(KC):
            ln_load_sq(get_x, kc, tag)
            ln_mm(get_x, kc, tag)
        ln_finish(get_x, lnhi, lnlo, tag)

    ln_hi = kb.sbuf("ln_hi", [128, KC * TS], BF16, stack=stackA)
    ln_lo = kb.sbuf("ln_lo", [128, KC * TS], BF16, stack=stackA)
    ctx_hi = kb.sbuf("ctx_hi", [128, KC * TS], BF16, stack=stackA)
    ctx_lo = kb.sbuf("ctx_lo", [128, KC * TS], BF16, stack=stackA)
    wkv_col = [kb.sbuf(f"wkv{i}", [128, KC * 512], BF16, stack=stackA)
               for i in range(2)]
    wq_col = [kb.sbuf(f"wq{i}", [128, KC * 256], BF16, stack=stackA)
              for i in range(2)]
    t_min = kb.sbuf("t_min", [128, TS], stack=stackA)
    t_exp = kb.sbuf("t_exp", [128, TS], stack=stackA)
    t_phik = kb.sbuf("t_phik", [128, TS], stack=stackA)
    t_kv = kb.sbuf("t_kv", [128, TS], stack=stackA)
    t_cum = kb.sbuf("t_cum", [128, TS], stack=stackA)
    t_ctx = kb.sbuf("t_ctx", [128, TS], stack=stackA)
    x2o = [kb.sbuf(f"x2o{i}", [128, 2 * TS], stack=stackA) for i in range(2)]

    xsg_owner = [None, None]
    xs_stream_ctr = [0]

    def make_x_stream(src_r, c0, tag):
        xs_stream_ctr[0] += 1
        sid = xs_stream_ctr[0]
        state = {"n": 0}

        def get(kc, load=True):
            g = kc // 2
            xs = xsg[g % 2]
            if load and xsg_owner[g % 2] != (sid, g):
                xsg_owner[g % 2] = (sid, g)
                i = state["n"]
                state["n"] += 1

                def fl(eng, semh, g=g, xs=xs):
                    eng.dma_start(
                        out=xs.h.ap().rearrange("p (g t) -> p g t", g=2),
                        in_=src_r[:, 2 * g:2 * g + 2, c0:c0 + TS]
                        ).then_inc(semh, 16)
                kb.dma("sync", xs.name, fl, writes=[xs], tag=tag + f"_{i}")
            return xs, xs.h[:, (kc % 2) * TS:(kc % 2 + 1) * TS]
        return get

    def load_wcol(buf, ec, blocks, base, width, tag):
        for wi in blocks:
            assert awc_blk[(wi, ec)].writes, f"awc block ({wi},{ec}) not written"

        def fh(eng, semh, base=base, width=width, buf=buf):
            eng.dma_start(out=buf.h.ap().rearrange("p (kc c) -> p kc c", kc=KC),
                          in_=awc_r[:, :, base:base + width]).then_inc(semh, 16)
        kb.dma("gpsimd", buf.name, fh,
               reads=[awc_blk[(wi, ec)] for wi in blocks], writes=[buf], tag=tag)

    def proj3(psum_b, buf, stride, off, a_hi_b, a_lo_b, tag):
        def fn(e, buf=buf, psum_b=psum_b):
            r = []
            for kc in range(KC):
                w_hi = buf.h[:, kc * stride + off:kc * stride + off + 128]
                w_lo = buf.h[:, kc * stride + off + 128:kc * stride + off + 256]
                a_hi = a_hi_b.h[:, kc * TS:(kc + 1) * TS]
                a_lo = a_lo_b.h[:, kc * TS:(kc + 1) * TS]
                r.append(e.matmul(psum_b.h[:, 0:TS], w_hi, a_hi,
                                  start=(kc == 0), stop=False))
                r.append(e.matmul(psum_b.h[:, 0:TS], w_hi, a_lo,
                                  start=False, stop=False))
                r.append(e.matmul(psum_b.h[:, 0:TS], w_lo, a_hi,
                                  start=False, stop=(kc == KC - 1)))
            return r
        kb.op("tensor", fn, reads=[buf, a_hi_b, a_lo_b], writes=[psum_b], tag=tag)

    def phi_chain(psum_b, bias_col, out_phi, tag):
        bap = bias_sb.h[:, bias_col:bias_col + 1]
        kb.op("vector", lambda e: e.tensor_scalar(
                  t_min.ap(), psum_b.h[:, 0:TS], bap, 0.0, ALU.add, ALU.min),
              reads=[psum_b, bias_sb], writes=[t_min], tag=tag + "_min")
        kb.op("scalar", lambda e: e.activation(t_exp.ap(), t_min.ap(), AF.Exp),
              reads=[t_min], writes=[t_exp], tag=tag + "_exp")
        kb.op("scalar", lambda e: e.activation(t_min.ap(), psum_b.h[:, 0:TS],
                                               AF.Relu, bias=bap),
              reads=[psum_b, bias_sb], writes=[t_min], tag=tag + "_relu")
        kb.op("vector", lambda e: e.tensor_add(out_phi.ap(), t_exp.ap(), t_min.ap()),
              reads=[t_exp, t_min], writes=[out_phi], tag=tag + "_phi")

    bg_step(4)
    n_slices = cfg.n_pre + cfg.n_own

    def slice_src(s):
        is_pre = s < cfg.n_pre
        c0 = (s - (0 if is_pre else cfg.n_pre)) * TS
        return is_pre, c0, (xTpre_r if is_pre else xT_r)

    for s in range(n_slices):
        is_pre, c0, src_r = slice_src(s)

        if s == 0:
            layer_norm(make_x_stream(src_r, c0, "xs0"), ln_hi, ln_lo, "ln1_s0")
        if s + 1 < n_slices:
            _, c1, src1 = slice_src(s + 1)
            gx_next = make_x_stream(src1, c1, f"xs{s + 1}")
            tg_next = f"ln1_s{s + 1}"
        else:
            gx_next = make_x_stream(x2T_r, 0, "x2s0")
            tg_next = "ln2_s0"

        def emit_wload(ec, s=s, is_pre=is_pre):
            load_wcol(wkv_col[ec % 2], ec, (0, 1), ec * 1024, 512,
                      f"wkv_s{s}e{ec}")
            if not is_pre:
                load_wcol(wq_col[ec % 2], ec, (2,), ec * 1024 + 512, 256,
                          f"wq_s{s}e{ec}")
        emit_wload(0)
        for ec in range(KC):
            if ec >= 2:
                ln_mm(gx_next, ec - 2, tg_next)
            ln_load_sq(gx_next, ec, tg_next)
            bg_step(4 if s == 0 else (2 if is_pre else 3))
            wkv = wkv_col[ec % 2]
            wq = wq_col[ec % 2]
            if ec + 1 < KC:
                emit_wload(ec + 1)
            pbq = pb[4] if ec % 2 == 0 else pb[2]
            pbk = pb[5] if ec % 2 == 0 else pb[7]
            pbv = pb[6] if ec % 2 == 0 else pb[3]

            proj3(pbk, wkv, 512, 0, ln_hi, ln_lo, f"mmk_s{s}e{ec}")
            proj3(pbv, wkv, 512, 256, ln_hi, ln_lo, f"mmv_s{s}e{ec}")
            if not is_pre:
                proj3(pbq, wq, 256, 0, ln_hi, ln_lo, f"mmq_s{s}e{ec}")

            phi_chain(pbk, KC + ec, t_phik, f"pk_s{s}e{ec}")
            kb.op("vector", lambda e, ec=ec, pbv=pbv: e.scalar_tensor_tensor(
                      t_kv.ap(), pbv.h[:, 0:TS],
                      bias_sb.h[:, 2 * KC + ec:2 * KC + ec + 1],
                      t_phik.ap(), ALU.add, ALU.mult),
                  reads=[pbv, bias_sb, t_phik], writes=[t_kv], tag=f"kv_s{s}e{ec}")
            kb.op("vector", lambda e, ec=ec: e.tensor_tensor_scan(
                      t_cum.ap(), t_kv.ap(), zeros_ts.ap(), prefix.h[:, ec:ec + 1],
                      ALU.add, ALU.add),
                  reads=[t_kv, zeros_ts, prefix], writes=[t_cum],
                  tag=f"scan_s{s}e{ec}")
            if is_pre:
                kb.op("vector", lambda e, ec=ec: e.tensor_scalar_mul(
                          prefix.h[:, ec:ec + 1], t_cum.h[:, TS - 1:TS],
                          mask_sb.h[:, 0:1]),
                      reads=[t_cum, mask_sb], writes=[prefix], tag=f"pfx_s{s}e{ec}")
            else:
                kb.op("vector", lambda e, ec=ec: e.tensor_copy(
                          prefix.h[:, ec:ec + 1], t_cum.h[:, TS - 1:TS]),
                      reads=[t_cum], writes=[prefix], tag=f"pfx_s{s}e{ec}")

            if not is_pre:
                phi_chain(pbq, ec, t_phik, f"pq_s{s}e{ec}")
                kb.op("vector", lambda e: e.scalar_tensor_tensor(
                          t_ctx.ap(), t_cum.ap(), cfg.SCALE, t_phik.ap(),
                          ALU.mult, ALU.mult),
                      reads=[t_cum, t_phik], writes=[t_ctx], tag=f"ctx_s{s}e{ec}")
                split_pair(t_ctx, t_ctx.ap(),
                           ctx_hi, ctx_hi.h[:, ec * TS:(ec + 1) * TS],
                           ctx_lo, ctx_lo.h[:, ec * TS:(ec + 1) * TS],
                           f"ctxsp_s{s}e{ec}")

        ln_mm(gx_next, KC - 2, tg_next)
        ln_mm(gx_next, KC - 1, tg_next)
        if s + 1 < n_slices:
            ln_finish(gx_next, ln_hi, ln_lo, tg_next)

        if not is_pre:
            get_xr = make_x_stream(src_r, c0, f"xr{s}")
            load_wcol(wq_col[0], 0, (3,), 768, 256, f"wo_s{s}e0")
            for e2c in range(KC):
                bg_step(1)
                wo = wq_col[e2c % 2]
                if e2c + 1 < KC:
                    load_wcol(wq_col[(e2c + 1) % 2], e2c + 1, (3,),
                              (e2c + 1) * 1024 + 768, 256, f"wo_s{s}e{e2c + 1}")
                pbo = pb[7] if e2c % 2 == 0 else pb[4]
                proj3(pbo, wo, 256, 0, ctx_hi, ctx_lo, f"mmo_s{s}e{e2c}")
                xres, xres_ap = get_xr(e2c)
                xo = x2o[(e2c // 2) % 2]
                sub = (e2c % 2) * TS
                kb.op("vector", lambda e, e2c=e2c, xo=xo, sub=sub, xres=xres,
                      xres_ap=xres_ap, pbo=pbo: e.scalar_tensor_tensor(
                          xo.h[:, sub:sub + TS], pbo.h[:, 0:TS],
                          bias_sb.h[:, 3 * KC + e2c:3 * KC + e2c + 1],
                          xres_ap, ALU.add, ALU.add),
                      reads=[pbo, bias_sb, xres], writes=[xo],
                      tag=f"x2_s{s}e{e2c}")
                if e2c % 2 == 1:
                    def fn_st(eng, semh, xo=xo, e2c=e2c, c0=c0):
                        eng.dma_start(
                            out=x2T_r[:, e2c - 1:e2c + 1, c0:c0 + TS],
                            in_=xo.h.ap().rearrange("p (g t) -> p g t", g=2)
                            ).then_inc(semh, 16)
                    kb.dma("gpsimd", xo.name, fn_st, reads=[xo], writes=[x2T],
                           tag=f"x2st_s{s}e{e2c}")

    while bg_step():
        pass

    kb.barrier()
    stackA.close()

    ln2_hi = kb.sbuf("ln2_hi", [128, KC * TS], BF16)
    ln2_lo = kb.sbuf("ln2_lo", [128, KC * TS], BF16)
    g_hi = kb.sbuf("g_hi", [128, HC * TS], BF16)
    g_lo = kb.sbuf("g_lo", [128, HC * TS], BF16)
    q13c = [kb.sbuf(f"q13c{i}", [128, KC * 256], BF16) for i in range(2)]
    w2qb = [kb.sbuf(f"w2qb{i}", [128, HC * 128], BF16) for i in range(2)]
    silu_b = kb.sbuf("silu_b", [128, TS])
    g32 = kb.sbuf("g32", [128, TS])
    x2rd = [kb.sbuf(f"x2rd{i}", [128, TS]) for i in range(2)]
    ob = [kb.sbuf(f"ob{i}", [128, 2 * TS]) for i in range(2)]

    for fs in range(cfg.n_own):
        c0 = fs * TS

        if fs == 0:
            ln_finish(gx_next, ln2_hi, ln2_lo, "ln2_s0")
        if fs + 1 < cfg.n_own:
            gx2_next = make_x_stream(x2T_r, (fs + 1) * TS, f"x2s{fs + 1}")

        n_mm_in = max(0, min(KC, HC - 2))
        for hc in range(HC):
            if fs + 1 < cfg.n_own:
                if 2 <= hc < n_mm_in + 2:
                    ln_mm(gx2_next, hc - 2, f"ln2_s{fs + 1}")
                if hc < KC:
                    ln_load_sq(gx2_next, hc, f"ln2_s{fs + 1}")
            qb = q13c[hc % 2]

            def fn_q13(eng, semh, qb=qb, hc=hc):
                eng.dma_start(out=qb.h.ap().rearrange("p (kc c) -> p kc c", kc=KC),
                              in_=fqc_r[:, :, hc * 256:(hc + 1) * 256]
                              ).then_inc(semh, 16)
            kb.dma("gpsimd", qb.name, fn_q13, reads=[fqc], writes=[qb],
                   tag=f"q13_{fs}_{hc}")

            ph1 = pb[4 + (hc % 2) * 2]
            ph3 = pb[5 + (hc % 2) * 2]

            def fn_h(e, qb, p, off):
                r = []
                for kc in range(KC):
                    w_ap = qb.h[:, kc * 256 + off:kc * 256 + off + 128]
                    r.append(e.matmul(p.h[:, 0:TS], w_ap,
                                      ln2_hi.h[:, kc * TS:(kc + 1) * TS],
                                      start=(kc == 0), stop=False))
                    r.append(e.matmul(p.h[:, 0:TS], w_ap,
                                      ln2_lo.h[:, kc * TS:(kc + 1) * TS],
                                      start=False, stop=(kc == KC - 1)))
                return r
            kb.op("tensor", lambda e, qb=qb, p=ph1: fn_h(e, qb, p, 0),
                  reads=[qb, ln2_hi, ln2_lo], writes=[ph1], tag=f"mmh1_{fs}_{hc}")
            kb.op("tensor", lambda e, qb=qb, p=ph3: fn_h(e, qb, p, 128),
                  reads=[qb, ln2_hi, ln2_lo], writes=[ph3], tag=f"mmh3_{fs}_{hc}")

            kb.op("scalar", lambda e, ph1=ph1: e.activation(
                      silu_b.ap(), ph1.h[:, 0:TS], AF.Silu, scale=svals.h[:, 0:1]),
                  reads=[ph1, svals], writes=[silu_b], tag=f"silu_{fs}_{hc}")
            kb.op("vector", lambda e, ph3=ph3: e.scalar_tensor_tensor(
                      g32.ap(), ph3.h[:, 0:TS], svals.h[:, 1:2], silu_b.ap(),
                      ALU.mult, ALU.mult),
                  reads=[ph3, svals, silu_b], writes=[g32], tag=f"g_{fs}_{hc}")
            split_pair(g32, g32.ap(), g_hi, g_hi.h[:, hc * TS:(hc + 1) * TS],
                       g_lo, g_lo.h[:, hc * TS:(hc + 1) * TS], f"gsp_{fs}_{hc}")

        if fs + 1 < cfg.n_own:
            for kc in range(n_mm_in, KC):
                ln_mm(gx2_next, kc, f"ln2_s{fs + 1}")
            ln_finish(gx2_next, ln2_hi, ln2_lo, f"ln2_s{fs + 1}")

        for dc in range(KC):
            wq2 = w2qb[dc % 2]

            def fn_w2(eng, semh, wq2=wq2, dc=dc):
                eng.dma_start(out=wq2.h.ap().rearrange("p (hc c) -> p hc c", c=128),
                              in_=w2qc_r[:, :, dc * 128:(dc + 1) * 128]
                              ).then_inc(semh, 16)
            kb.dma("gpsimd", wq2.name, fn_w2, reads=[w2qc], writes=[wq2],
                   tag=f"w2q_{fs}_{dc}")

            xr = x2rd[dc % 2]

            def fn_xr(eng, semh, xr=xr, dc=dc, c0=c0):
                eng.dma_start(out=xr.ap(), in_=x2T_r[:, dc, c0:c0 + TS]
                              ).then_inc(semh, 16)
            kb.dma("sync", xr.name, fn_xr, reads=[x2T], writes=[xr],
                   tag=f"xr{fs}_{dc}")

            pf = pb[dc % 4]

            def fn_f(e, pf=pf, wq2=wq2):
                r = []
                for hc in range(HC):
                    w_ap = wq2.h[:, hc * 128:(hc + 1) * 128]
                    r.append(e.matmul(pf.h[:, 0:TS], w_ap,
                                      g_hi.h[:, hc * TS:(hc + 1) * TS],
                                      start=(hc == 0), stop=False))
                    r.append(e.matmul(pf.h[:, 0:TS], w_ap,
                                      g_lo.h[:, hc * TS:(hc + 1) * TS],
                                      start=False, stop=(hc == HC - 1)))
                return r
            kb.op("tensor", fn_f, reads=[wq2, g_hi, g_lo], writes=[pf],
                  tag=f"mmf{fs}_{dc}")

            o = ob[(dc // 2) % 2]
            sub = (dc % 2) * TS
            kb.op("vector", lambda e, pf=pf, xr=xr, o=o, sub=sub:
                  e.scalar_tensor_tensor(
                      o.h[:, sub:sub + TS], pf.h[:, 0:TS], svals.h[:, 2:3],
                      xr.ap(), ALU.mult, ALU.add),
                  reads=[pf, svals, xr], writes=[o], tag=f"out{fs}_{dc}")
            if dc % 2 == 1:
                def fn_ost(eng, semh, o=o, dc=dc, c0=c0):
                    eng.dma_start(out=outT_r[:, dc - 1:dc + 1, c0:c0 + TS],
                                  in_=o.h.ap().rearrange("p (g t) -> p g t", g=2)
                                  ).then_inc(semh, 16)
                kb.dma("gpsimd", o.name, fn_ost, reads=[o], writes=[outT],
                       tag=f"ost{fs}_{dc}")

    kb.raw_wait("gpsimd", extra=[(s, kb.semc[s]) for s in
                                 ("d_ob0", "d_ob1", "d_x2o0", "d_x2o1")
                                 if s in kb.semc])
    kb.emit()
    return nc




def kernel(L0_attn_x, L0_attn_Wq, L0_attn_bq, L0_attn_Wk, L0_attn_bk,
           L0_attn_Wv, L0_attn_bv, L0_attn_Wo, L0_attn_bo,
           L0_ffn_w1, L0_ffn_w3, L0_ffn_w2):
    cfg = CFG
    D, T, P, H, HP = cfg.D, cfg.T, cfg.P, cfg.H, cfg.HP
    x = np.ascontiguousarray(np.asarray(L0_attn_x, dtype=np.float32))
    B, L, _ = x.shape

    def t32(a):
        return np.ascontiguousarray(np.asarray(a, dtype=np.float32).T)

    wqT, wkT, wvT, woT = (t32(L0_attn_Wq), t32(L0_attn_Wk),
                          t32(L0_attn_Wv), t32(L0_attn_Wo))
    w1T = np.zeros((D, HP), np.float32)
    w1T[:, :H] = t32(L0_ffn_w1)
    w3T = np.zeros((D, HP), np.float32)
    w3T[:, :H] = t32(L0_ffn_w3)
    w2T = np.zeros((HP, D), np.float32)
    w2T[:H, :] = t32(L0_ffn_w2)
    zeros_pre = np.zeros((D, P), np.float32)

    in_maps = []
    for c in range(8):
        b, h = c // 2, c % 2
        xTc = np.ascontiguousarray(x[b, h * T:(h + 1) * T, :].T)
        if h == 1:
            xTpre = np.ascontiguousarray(x[b, 0:P, :].T)
            mask = np.ones((128, 1), np.float32)
        else:
            xTpre = zeros_pre
            mask = np.zeros((128, 1), np.float32)
        in_maps.append({
            "xT": xTc, "xTpre": xTpre, "pre_mask": mask,
            "wqT": wqT, "wkT": wkT, "wvT": wvT, "woT": woT,
            "bq": np.asarray(L0_attn_bq, np.float32),
            "bk": np.asarray(L0_attn_bk, np.float32),
            "bv": np.asarray(L0_attn_bv, np.float32),
            "bo": np.asarray(L0_attn_bo, np.float32),
            "w1T": w1T, "w3T": w3T, "w2T": w2T,
        })

    nc = build_nc(cfg)
    res = run_bass_kernel_spmd(nc, in_maps, core_ids=list(range(8)))

    out = np.empty((B, L, D), np.float32)
    for c in range(8):
        b, h = c // 2, c % 2
        out[b, h * T:(h + 1) * T, :] = np.asarray(res.results[c]["outT"]).T
    return out


# version 28
# speedup vs baseline: 1.2887x; 1.0699x over previous
from contextlib import ExitStack

import numpy as np

import concourse.bass as bass
import concourse.mybir as mybir
from concourse.bass_utils import run_bass_kernel_spmd

F32 = mybir.dt.float32
BF16 = mybir.dt.bfloat16
AF = mybir.ActivationFunctionType
ALU = mybir.AluOpType


class Cfg:
    def __init__(self, D=2048, T=2048, P=2048, H=5461, TS=512):
        self.D, self.T, self.P, self.H, self.TS = D, T, P, H, TS
        self.KC = D // 128
        self.HC = (H + 127) // 128
        self.HP = self.HC * 128
        self.n_pre = P // TS
        self.n_own = T // TS
        self.SCALE = 128.0 ** -0.5
        self.EPS = 1e-5
        self.RC = 12582912.0


CFG = Cfg()

ENGINES = ("tensor", "vector", "scalar", "sync", "gpsimd")
ENG_SEM = {e: "E_" + e for e in ENGINES}


class Buf:

    def __init__(self, kb, handle, name):
        self.kb = kb
        self.h = handle
        self.name = name
        self.writes = {}
        self.readers = {}

    def ap(self):
        return self.h[:]

    def _merge(self, d, tk):
        s, v = tk
        if d.get(s, 0) < v:
            d[s] = v


class KB:
    def __init__(self, nc, stack):
        self.nc = nc
        self.stack = stack
        self.semh = {}
        self.semc = {}
        self.ops = {e: [] for e in ENGINES}
        self.waited = {e: {} for e in ENGINES}
        self.barrier_deps = {}
        for e in ENGINES:
            self._sem(ENG_SEM[e])

    def _sem(self, name):
        if name not in self.semh:
            self.semh[name] = self.stack.enter_context(self.nc.semaphore(name))
            self.semc[name] = 0
        return name

    def sbuf(self, name, shape, dtype=F32, stack=None):
        h = (stack or self.stack).enter_context(
            self.nc.sbuf_tensor(name, list(shape), dtype))
        return Buf(self, h, name)

    def psum(self, name, shape=(128, 512), dtype=F32):
        h = self.stack.enter_context(self.nc.psum_tensor(name, list(shape), dtype))
        return Buf(self, h, name)

    def dram_buf(self, name, shape, dtype=F32, kind=None):
        if kind is None:
            h = self.nc.dram_tensor(name, list(shape), dtype)
        else:
            h = self.nc.dram_tensor(name, list(shape), dtype, kind=kind)
        return Buf(self, h, name)

    def barrier(self):
        self.barrier_deps = dict(self.semc)

    def _collect(self, engine, reads, writes):
        deps = {}
        for s, v in self.barrier_deps.items():
            if deps.get(s, 0) < v:
                deps[s] = v
        for b in reads:
            for s, v in b.writes.items():
                if deps.get(s, 0) < v:
                    deps[s] = v
        for b in writes:
            for d in (b.writes, b.readers):
                for s, v in d.items():
                    if deps.get(s, 0) < v:
                        deps[s] = v
        waits = []
        wt = self.waited[engine]
        for s in sorted(deps):
            v = deps[s]
            if self.semc.get(s, 0) < v:
                raise RuntimeError(f"dep on future ticket {s}={v}")
            if wt.get(s, 0) < v:
                waits.append((s, v))
                wt[s] = v
        return waits

    def op(self, engine, fn, reads=(), writes=(), tag=""):
        waits = self._collect(engine, reads, writes)
        sem = ENG_SEM[engine]
        self.semc[sem] += 1
        tk = (sem, self.semc[sem])
        for b in writes:
            b._merge(b.writes, tk)
        for b in reads:
            b._merge(b.readers, tk)
        self.ops[engine].append((waits, fn, sem, 1, tag))
        return tk

    def dma(self, engine, buf_sem_owner, fn, reads=(), writes=(), tag=""):
        sem = self._sem("d_" + buf_sem_owner)
        waits = self._collect(engine, reads, writes)
        self.semc[sem] += 16
        tk = (sem, self.semc[sem])
        for b in writes:
            b._merge(b.writes, tk)
        for b in reads:
            b._merge(b.readers, tk)
        self.ops[engine].append((waits, fn, sem, None, tag))
        return tk

    def raw_wait(self, engine, extra=()):
        deps = {}
        for s, v in extra:
            if deps.get(s, 0) < v:
                deps[s] = v
        waits = []
        wt = self.waited[engine]
        for s in sorted(deps):
            v = deps[s]
            if wt.get(s, 0) < v:
                waits.append((s, v))
                wt[s] = v
        self.ops[engine].append((waits, None, None, None, "final_wait"))

    def emit(self):
        nc = self.nc
        with nc.Block() as block:
            def emitter(engine):
                def run(eng):
                    for waits, fn, sem, amt, tag in self.ops[engine]:
                        for s, v in waits:
                            eng.wait_ge(self.semh[s], v)
                        if fn is None:
                            continue
                        if amt is None:
                            fn(eng, self.semh[sem])
                        else:
                            r = fn(eng)
                            r = r[-1] if isinstance(r, (list, tuple)) else r
                            r.then_inc(self.semh[sem], amt)
                return run
            block.tensor(emitter("tensor"))
            block.vector(emitter("vector"))
            block.scalar(emitter("scalar"))
            block.sync(emitter("sync"))
            block.gpsimd(emitter("gpsimd"))




def build_nc(cfg=CFG):
    nc = bass.Bass("TRN2", target_bir_lowering=False)
    stack = ExitStack()
    kb = KB(nc, stack)
    D, T, P, H, TS = cfg.D, cfg.T, cfg.P, cfg.H, cfg.TS
    KC, HC, HP = cfg.KC, cfg.HC, cfg.HP

    xT = nc.dram_tensor("xT", [D, T], F32, kind="ExternalInput")
    xTpre = nc.dram_tensor("xTpre", [D, P], F32, kind="ExternalInput")
    wnames = ("wkT", "wvT", "wqT", "woT")
    wT = {n: nc.dram_tensor(n, [D, D], F32, kind="ExternalInput") for n in wnames}
    biases = {n: nc.dram_tensor(n, [D], F32, kind="ExternalInput")
              for n in ("bq", "bk", "bv", "bo")}
    w1T = nc.dram_tensor("w1T", [D, HP], F32, kind="ExternalInput")
    w3T = nc.dram_tensor("w3T", [D, HP], F32, kind="ExternalInput")
    w2T = nc.dram_tensor("w2T", [HP, D], F32, kind="ExternalInput")
    pre_mask = nc.dram_tensor("pre_mask", [128, 1], F32, kind="ExternalInput")
    outT = kb.dram_buf("outT", [D, T], kind="ExternalOutput")
    x2T = kb.dram_buf("x2T", [D, T])
    awc = nc.dram_tensor("awc", [D, 8 * D], BF16)
    awc_blk = {(wi, ec): Buf(kb, awc, f"awc{wi}_{ec}")
               for wi in range(4) for ec in range(KC)}
    fqc = kb.dram_buf("fqc", [D, 2 * HP], BF16)
    w2qc = kb.dram_buf("w2qc", [HP, D], BF16)

    def r3(ap, p=128):
        return ap.rearrange("(kc p) t -> p kc t", p=p)

    xT_r = r3(xT.ap())
    xTpre_r = r3(xTpre.ap())
    wT_r = {n: r3(wT[n].ap()) for n in wnames}
    awc_r = r3(awc.ap())
    fqc_r = r3(fqc.h.ap())
    w2qc_r = w2qc.h.ap().rearrange("(hc p) d -> p hc d", p=128)
    w1T_r = r3(w1T.ap())
    w3T_r = r3(w3T.ap())
    w2T_r = w2T.ap().rearrange("(hc p) d -> p hc d", p=128)
    x2T_r = r3(x2T.h.ap())
    outT_r = r3(outT.h.ap())

    ones_c = kb.sbuf("ones_c", [128, 1])
    ones_r = kb.sbuf("ones_r", [1, 128])
    bias_sb = kb.sbuf("bias_sb", [128, 4 * KC])
    mask_sb = kb.sbuf("mask_sb", [128, 1])
    prefix = kb.sbuf("prefix", [128, KC])
    stat = kb.sbuf("stat", [1, 5 * TS])
    svals = kb.sbuf("svals", [128, 8])
    sval_tmp = kb.sbuf("sval_tmp", [1, 8])
    stats_cols = kb.sbuf("stats_cols", [128, 224])
    stats_red = kb.sbuf("stats_red", [128, 3])
    const_sb = kb.sbuf("const_sb", [128, 3])
    sq_pp = [kb.sbuf(f"sq{i}", [128, TS]) for i in range(2)]
    ln32_pp = [kb.sbuf(f"ln32_{i}", [128, TS]) for i in range(2)]
    xsg = [kb.sbuf(f"xsg{i}", [128, 2 * TS]) for i in range(4)]

    pb = [kb.psum(f"pb{i}", (128, TS)) for i in range(8)]

    kb.op("vector", lambda e: e.memset(ones_c.ap(), 1.0), writes=[ones_c], tag="init")
    kb.op("vector", lambda e: e.memset(ones_r.ap(), 1.0), writes=[ones_r], tag="init")
    kb.op("vector", lambda e: e.memset(prefix.ap(), 0.0), writes=[prefix], tag="init")
    kb.op("vector", lambda e: e.memset(const_sb.h[:, 0:1], cfg.EPS),
          writes=[const_sb], tag="init")
    kb.op("vector", lambda e: e.memset(const_sb.h[:, 2:3], -cfg.RC),
          writes=[const_sb], tag="init")
    kb.op("vector", lambda e: e.memset(const_sb.h[:, 1:2], cfg.RC),
          writes=[const_sb], tag="init")

    def load_simple(buf, out_ap, in_ap, tag=""):
        def fn(eng, semh, out_ap=out_ap, in_ap=in_ap):
            eng.dma_start(out=out_ap, in_=in_ap,
                          allow_slow_non_contiguous=True).then_inc(semh, 16)
        return kb.dma("sync", buf.name, fn, writes=[buf], tag=tag)

    for j, n in enumerate(("bq", "bk", "bv", "bo")):
        load_simple(bias_sb, bias_sb.h[:, j * KC:(j + 1) * KC],
                    biases[n].ap().rearrange("(kc p) -> p kc", p=128), tag="bias")
    load_simple(mask_sb, mask_sb.ap(), pre_mask.ap(), tag="mask")

    def split_pair(src_buf, src_ap, hi_buf, hi_ap, lo_buf, lo_ap, tag):
        kb.op("scalar", lambda e: e.copy(hi_ap, src_ap),
              reads=[src_buf], writes=[hi_buf], tag=tag + "_hi")
        kb.op("vector",
              lambda e: e.scalar_tensor_tensor(lo_ap, src_ap, 1.0, hi_ap,
                                               ALU.mult, ALU.subtract),
              reads=[src_buf, hi_buf], writes=[lo_buf], tag=tag + "_lo")

    stackA = ExitStack()
    zeros_ts = kb.sbuf("zeros_ts", [128, TS], stack=stackA)
    kb.op("vector", lambda e: e.memset(zeros_ts.ap(), 0.0), writes=[zeros_ts],
          tag="init")

    f0_pp = [kb.sbuf(f"f0_{i}", [128, 1376], stack=stackA) for i in range(2)]
    ld_pp = [kb.sbuf(f"bgld{i}", [128, KC * 128], stack=stackA) for i in range(2)]
    q_pp = [kb.sbuf(f"bgq{i}", [128, KC * 128], BF16, stack=stackA)
            for i in range(2)]

    def bg_stream():
        pend = [None]

        def flush():
            if pend[0] is not None:
                pend[0]()
                pend[0] = None

        worder = [(wi, n, ec) for ec in range(KC) for wi, n in ((0, "wkT"),
                                                               (1, "wvT"))]
        worder += [(2, "wqT", ec) for ec in range(KC)]
        worder += [(3, "woT", ec) for ec in range(KC)]
        for i, (wi, n, ec) in enumerate(worder):
            buf = ld_pp[i % 2]

            def fnl(eng, semh, n=n, ec=ec, buf=buf):
                eng.dma_start(out=buf.ap(), in_=wT_r[n][:, :, ec * 128:(ec + 1) * 128]
                              ).then_inc(semh, 16)
            kb.dma("sync", buf.name, fnl, writes=[buf], tag=f"wspl_{n}{ec}")
            flush()

            def compute(wi=wi, n=n, ec=ec, buf=buf):
                split_pair(buf, buf.ap(), q_pp[0], q_pp[0].ap(),
                           q_pp[1], q_pp[1].ap(), f"wsp_{n}{ec}")
                base = ec * 1024 + wi * 256

                def fsh(eng, semh, base=base):
                    eng.dma_start(out=awc_r[:, :, base:base + 128],
                                  in_=q_pp[0].h.ap().rearrange(
                                      "p (kc c) -> p kc c", kc=KC)
                                  ).then_inc(semh, 16)
                kb.dma("gpsimd", q_pp[0].name, fsh, reads=[q_pp[0]],
                       writes=[awc_blk[(wi, ec)]], tag=f"wsps_{n}{ec}h")

                def fsl(eng, semh, base=base):
                    eng.dma_start(out=awc_r[:, :, base + 128:base + 256],
                                  in_=q_pp[1].h.ap().rearrange(
                                      "p (kc c) -> p kc c", kc=KC)
                                  ).then_inc(semh, 16)
                kb.dma("gpsimd", q_pp[1].name, fsl, reads=[q_pp[1]],
                       writes=[awc_blk[(wi, ec)]], tag=f"wsps_{n}{ec}l")
            pend[0] = compute
            yield
        flush()

        F0W = 1376
        col = 0
        f0_ncols = {0: [], 1: [], 2: []}
        for wi, wr, tot in ((0, w1T_r, HP), (1, w3T_r, HP), (2, w2T_r, D)):
            nch = wr.shape[1]
            for kc in range(nch):
                for o0 in range(0, tot, F0W):
                    width = min(o0 + F0W, tot) - o0
                    buf = f0_pp[col % 2]
                    src = wr[:, kc, o0:o0 + width]

                    def fnl(eng, semh, src=src, width=width, buf=buf):
                        eng.dma_start(out=buf.h[:, 0:width], in_=src
                                      ).then_inc(semh, 16)
                    kb.dma("sync", buf.name, fnl, writes=[buf], tag="f0_load")
                    flush()

                    def compute(width=width, col=col, buf=buf):
                        kb.op("vector", lambda e: e.reduce_sum(
                                  stats_cols.h[:, col:col + 1],
                                  buf.h[:, 0:width], axis=mybir.AxisListType.X,
                                  apply_absolute_value=True),
                              reads=[buf], writes=[stats_cols], tag="f0_red")
                    pend[0] = compute
                    f0_ncols[wi].append(col)
                    col += 1
                    yield
        flush()

        for wi in range(3):
            cols = f0_ncols[wi]
            lo_, hi_ = min(cols), max(cols) + 1
            kb.op("vector", lambda e, wi=wi, lo_=lo_, hi_=hi_: e.reduce_sum(
                      stats_red.h[:, wi:wi + 1], stats_cols.h[:, lo_:hi_],
                      axis=mybir.AxisListType.X),
                  reads=[stats_cols], writes=[stats_red], tag=f"f0red{wi}")
            kb.op("tensor", lambda e, wi=wi: e.matmul(
                      pb[0].h[0:1, wi:wi + 1], ones_c.ap(),
                      stats_red.h[:, wi:wi + 1], start=True, stop=True),
                  reads=[stats_red, ones_c], writes=[pb[0]], tag=f"f0tot{wi}")
        cnt = float(D) * H
        kb.op("vector", lambda e: e.tensor_scalar_mul(
                  sval_tmp.h[0:1, 0:3], pb[0].h[0:1, 0:3], 1.0 / cnt),
              reads=[pb[0]], writes=[sval_tmp], tag="f0_s")
        kb.op("vector", lambda e: e.reciprocal(sval_tmp.h[0:1, 3:6],
                                               sval_tmp.h[0:1, 0:3]),
              reads=[sval_tmp], writes=[sval_tmp], tag="f0_invs")
        kb.op("tensor", lambda e: e.matmul(pb[1].h[:, 0:6], ones_r.ap(),
                                           sval_tmp.h[0:1, 0:6],
                                           start=True, stop=True),
              reads=[sval_tmp, ones_r], writes=[pb[1]], tag="f0_bc")
        kb.op("vector", lambda e: e.tensor_copy(svals.h[:, 0:6], pb[1].h[:, 0:6]),
              reads=[pb[1]], writes=[svals], tag="f0_sv")
        yield

        def q_step(i, in_ap_fn, width, wi, store_fn, tag):
            buf = ld_pp[i % 2]
            qb = q_pp[i % 2]

            def fl(eng, semh, buf=buf):
                eng.dma_start(out=buf.h[:, 0:width].rearrange(
                                  "p (n c) -> p n c", c=128),
                              in_=in_ap_fn()).then_inc(semh, 16)
            kb.dma("sync", buf.name, fl, writes=[buf], tag=tag + "_l")
            flush()

            def compute(buf=buf, qb=qb, width=width, wi=wi):
                inv = svals.h[:, 3 + wi:4 + wi]
                sap = buf.h[:, 0:width]
                qap = qb.h[:, 0:width]
                kb.op("scalar", lambda e: e.activation(
                          sap, sap, AF.Identity, bias=const_sb.h[:, 1:2],
                          scale=inv),
                      reads=[buf, svals, const_sb], writes=[buf], tag=tag + "_q1")
                kb.op("scalar", lambda e: e.activation(
                          sap, sap, AF.Identity, bias=const_sb.h[:, 2:3]),
                      reads=[buf, const_sb], writes=[buf], tag=tag + "_q2")
                kb.op("vector", lambda e: e.tensor_scalar(
                          qap, sap, 1.0, -1.0, ALU.min, ALU.max),
                      reads=[buf], writes=[qb], tag=tag + "_q3")

                def fs(eng, semh, qb=qb):
                    eng.dma_start(out=store_fn(),
                                  in_=qb.h[:, 0:width].rearrange(
                                      "p (n c) -> p n c", c=128)
                                  ).then_inc(semh, 16)
                kb.dma("gpsimd", qb.name, fs, reads=[qb], writes=[fqc, w2qc],
                       tag=tag + "_s")
            pend[0] = compute

        i = 0
        for hc in range(HC):
            for wi, wr in ((0, w1T_r), (1, w3T_r)):
                q_step(i,
                       lambda hc=hc, wr=wr: wr[:, :, hc * 128:(hc + 1) * 128],
                       KC * 128, wi,
                       lambda hc=hc, wi=wi: fqc_r[
                           :, :, hc * 256 + wi * 128:hc * 256 + (wi + 1) * 128],
                       f"q13_{hc}_{wi}")
                i += 1
                yield
        HCW = KC
        for dc in range(KC):
            for h0 in range(0, HC, HCW):
                h1_ = min(h0 + HCW, HC)
                w_ = (h1_ - h0) * 128
                q_step(i,
                       lambda dc=dc, h0=h0, h1_=h1_: w2T_r[
                           :, h0:h1_, dc * 128:(dc + 1) * 128],
                       w_, 2,
                       lambda dc=dc, h0=h0, h1_=h1_: w2qc_r[
                           :, h0:h1_, dc * 128:(dc + 1) * 128],
                       f"q2_{dc}_{h0}")
                i += 1
                yield
        flush()

    bg = bg_stream()

    def bg_step(n=1):
        for _ in range(n):
            if next(bg, "END") == "END":
                return False
        return True

    def layer_norm(get_x, lnhi, lnlo, tag):
        for kc in range(KC):
            xb, xap = get_x(kc)
            sq = sq_pp[kc % 2]
            kb.op("scalar", lambda e, xap=xap, sq=sq: e.activation(
                      sq.ap(), xap, AF.Square),
                  reads=[xb], writes=[sq], tag=tag + "_sq")

            def fn_mm(e, xap=xap, sq=sq, kc=kc):
                return [
                    e.matmul(pb[0].h[0:1, 0:TS], ones_c.ap(), xap,
                             start=(kc == 0), stop=(kc == KC - 1)),
                    e.matmul(pb[1].h[0:1, 0:TS], ones_c.ap(), sq.ap(),
                             start=(kc == 0), stop=(kc == KC - 1)),
                ]
            kb.op("tensor", fn_mm, reads=[xb, sq, ones_c], writes=[pb[0], pb[1]],
                  tag=tag + "_sums")
        m = stat.h[0:1, 0:TS]
        esq = stat.h[0:1, TS:2 * TS]
        m2 = stat.h[0:1, 2 * TS:3 * TS]
        var = stat.h[0:1, 3 * TS:4 * TS]
        sd = stat.h[0:1, 2 * TS:3 * TS]
        rr = stat.h[0:1, 4 * TS:5 * TS]
        kb.op("vector", lambda e: e.tensor_scalar_mul(m, pb[0].h[0:1, 0:TS], 1.0 / D),
              reads=[pb[0]], writes=[stat], tag=tag + "_m")
        kb.op("vector", lambda e: e.tensor_scalar_mul(esq, pb[1].h[0:1, 0:TS],
                                                      1.0 / D),
              reads=[pb[1]], writes=[stat], tag=tag + "_esq")
        kb.op("vector", lambda e: e.tensor_mul(m2, m, m), reads=[stat], writes=[stat],
              tag=tag + "_m2")
        kb.op("vector", lambda e: e.tensor_sub(var, esq, m2), reads=[stat],
              writes=[stat], tag=tag + "_var")
        kb.op("scalar", lambda e: e.activation(sd, var, AF.Sqrt,
                                               bias=const_sb.h[0:1, 0:1]),
              reads=[stat, const_sb], writes=[stat], tag=tag + "_sd")
        kb.op("vector", lambda e: e.reciprocal(rr, sd),
              reads=[stat], writes=[stat], tag=tag + "_rstd")
        kb.op("tensor", lambda e: e.matmul(pb[2].h[:, 0:TS], ones_r.ap(), m,
                                           start=True, stop=True),
              reads=[stat, ones_r], writes=[pb[2]], tag=tag + "_bm")
        kb.op("tensor", lambda e: e.matmul(pb[3].h[:, 0:TS], ones_r.ap(), rr,
                                           start=True, stop=True),
              reads=[stat, ones_r], writes=[pb[3]], tag=tag + "_br")
        for kc in range(KC):
            xb, xap = get_x(kc)
            t1 = sq_pp[kc % 2]
            l32 = ln32_pp[kc % 2]
            kb.op("vector", lambda e, xap=xap, t1=t1: e.tensor_sub(
                      t1.ap(), xap, pb[2].h[:, 0:TS]),
                  reads=[xb, pb[2]], writes=[t1], tag=tag + "_ap1")
            kb.op("vector", lambda e, t1=t1, l32=l32: e.tensor_mul(
                      l32.ap(), t1.ap(), pb[3].h[:, 0:TS]),
                  reads=[t1, pb[3]], writes=[l32], tag=tag + "_ap2")
            split_pair(l32, l32.ap(), lnhi, lnhi.h[:, kc * TS:(kc + 1) * TS],
                       lnlo, lnlo.h[:, kc * TS:(kc + 1) * TS], tag + f"_sp{kc}")

    ln_hi = kb.sbuf("ln_hi", [128, KC * TS], BF16, stack=stackA)
    ln_lo = kb.sbuf("ln_lo", [128, KC * TS], BF16, stack=stackA)
    ctx_hi = kb.sbuf("ctx_hi", [128, KC * TS], BF16, stack=stackA)
    ctx_lo = kb.sbuf("ctx_lo", [128, KC * TS], BF16, stack=stackA)
    wkv_col = [kb.sbuf(f"wkv{i}", [128, KC * 512], BF16, stack=stackA)
               for i in range(2)]
    wq_col = [kb.sbuf(f"wq{i}", [128, KC * 256], BF16, stack=stackA)
              for i in range(2)]
    t_min = kb.sbuf("t_min", [128, TS], stack=stackA)
    t_exp = kb.sbuf("t_exp", [128, TS], stack=stackA)
    t_phik = kb.sbuf("t_phik", [128, TS], stack=stackA)
    t_kv = kb.sbuf("t_kv", [128, TS], stack=stackA)
    t_cum = kb.sbuf("t_cum", [128, TS], stack=stackA)
    t_ctx = kb.sbuf("t_ctx", [128, TS], stack=stackA)
    x2o = [kb.sbuf(f"x2o{i}", [128, 2 * TS], stack=stackA) for i in range(2)]

    def make_x_stream(src_r, c0, tag):
        def get(kc):
            xs = xsg[(kc // 2) % 4]
            if kc % 2 == 0:
                def fl(eng, semh, kc=kc, xs=xs):
                    eng.dma_start(
                        out=xs.h.ap().rearrange("p (g t) -> p g t", g=2),
                        in_=src_r[:, kc:kc + 2, c0:c0 + TS]).then_inc(semh, 16)
                kb.dma("sync", xs.name, fl, writes=[xs], tag=tag + f"_{kc}")
            return xs, xs.h[:, (kc % 2) * TS:(kc % 2 + 1) * TS]
        return get

    def load_wcol(buf, ec, blocks, base, width, tag):
        for wi in blocks:
            assert awc_blk[(wi, ec)].writes, f"awc block ({wi},{ec}) not written"

        def fh(eng, semh, base=base, width=width, buf=buf):
            eng.dma_start(out=buf.h.ap().rearrange("p (kc c) -> p kc c", kc=KC),
                          in_=awc_r[:, :, base:base + width]).then_inc(semh, 16)
        kb.dma("gpsimd", buf.name, fh,
               reads=[awc_blk[(wi, ec)] for wi in blocks], writes=[buf], tag=tag)

    def proj3(psum_b, buf, stride, off, a_hi_b, a_lo_b, tag):
        def fn(e, buf=buf, psum_b=psum_b):
            r = []
            for kc in range(KC):
                w_hi = buf.h[:, kc * stride + off:kc * stride + off + 128]
                w_lo = buf.h[:, kc * stride + off + 128:kc * stride + off + 256]
                a_hi = a_hi_b.h[:, kc * TS:(kc + 1) * TS]
                a_lo = a_lo_b.h[:, kc * TS:(kc + 1) * TS]
                r.append(e.matmul(psum_b.h[:, 0:TS], w_hi, a_hi,
                                  start=(kc == 0), stop=False))
                r.append(e.matmul(psum_b.h[:, 0:TS], w_hi, a_lo,
                                  start=False, stop=False))
                r.append(e.matmul(psum_b.h[:, 0:TS], w_lo, a_hi,
                                  start=False, stop=(kc == KC - 1)))
            return r
        kb.op("tensor", fn, reads=[buf, a_hi_b, a_lo_b], writes=[psum_b], tag=tag)

    def phi_chain(psum_b, bias_col, out_phi, tag):
        bap = bias_sb.h[:, bias_col:bias_col + 1]
        kb.op("vector", lambda e: e.tensor_scalar(
                  t_min.ap(), psum_b.h[:, 0:TS], bap, 0.0, ALU.add, ALU.min),
              reads=[psum_b, bias_sb], writes=[t_min], tag=tag + "_min")
        kb.op("scalar", lambda e: e.activation(t_exp.ap(), t_min.ap(), AF.Exp),
              reads=[t_min], writes=[t_exp], tag=tag + "_exp")
        kb.op("scalar", lambda e: e.activation(t_min.ap(), psum_b.h[:, 0:TS],
                                               AF.Relu, bias=bap),
              reads=[psum_b, bias_sb], writes=[t_min], tag=tag + "_relu")
        kb.op("vector", lambda e: e.tensor_add(out_phi.ap(), t_exp.ap(), t_min.ap()),
              reads=[t_exp, t_min], writes=[out_phi], tag=tag + "_phi")

    bg_step(4)
    n_slices = cfg.n_pre + cfg.n_own
    for s in range(n_slices):
        is_pre = s < cfg.n_pre
        c0 = (s - (0 if is_pre else cfg.n_pre)) * TS
        src_r = xTpre_r if is_pre else xT_r

        layer_norm(lambda kc, g=make_x_stream(src_r, c0, f"xs{s}"): g(kc),
                   ln_hi, ln_lo, f"ln1_s{s}")

        def emit_wload(ec, s=s, is_pre=is_pre):
            load_wcol(wkv_col[ec % 2], ec, (0, 1), ec * 1024, 512,
                      f"wkv_s{s}e{ec}")
            if not is_pre:
                load_wcol(wq_col[ec % 2], ec, (2,), ec * 1024 + 512, 256,
                          f"wq_s{s}e{ec}")
        emit_wload(0)
        for ec in range(KC):
            bg_step(4 if s == 0 else (2 if is_pre else 3))
            wkv = wkv_col[ec % 2]
            wq = wq_col[ec % 2]
            if ec + 1 < KC:
                emit_wload(ec + 1)
            pbq = pb[4] if ec % 2 == 0 else pb[2]
            pbk = pb[5] if ec % 2 == 0 else pb[7]
            pbv = pb[6] if ec % 2 == 0 else pb[3]

            proj3(pbk, wkv, 512, 0, ln_hi, ln_lo, f"mmk_s{s}e{ec}")
            proj3(pbv, wkv, 512, 256, ln_hi, ln_lo, f"mmv_s{s}e{ec}")
            if not is_pre:
                proj3(pbq, wq, 256, 0, ln_hi, ln_lo, f"mmq_s{s}e{ec}")

            phi_chain(pbk, KC + ec, t_phik, f"pk_s{s}e{ec}")
            kb.op("vector", lambda e, ec=ec, pbv=pbv: e.scalar_tensor_tensor(
                      t_kv.ap(), pbv.h[:, 0:TS],
                      bias_sb.h[:, 2 * KC + ec:2 * KC + ec + 1],
                      t_phik.ap(), ALU.add, ALU.mult),
                  reads=[pbv, bias_sb, t_phik], writes=[t_kv], tag=f"kv_s{s}e{ec}")
            kb.op("vector", lambda e, ec=ec: e.tensor_tensor_scan(
                      t_cum.ap(), t_kv.ap(), zeros_ts.ap(), prefix.h[:, ec:ec + 1],
                      ALU.add, ALU.add),
                  reads=[t_kv, zeros_ts, prefix], writes=[t_cum],
                  tag=f"scan_s{s}e{ec}")
            if is_pre:
                kb.op("vector", lambda e, ec=ec: e.tensor_scalar_mul(
                          prefix.h[:, ec:ec + 1], t_cum.h[:, TS - 1:TS],
                          mask_sb.h[:, 0:1]),
                      reads=[t_cum, mask_sb], writes=[prefix], tag=f"pfx_s{s}e{ec}")
            else:
                kb.op("vector", lambda e, ec=ec: e.tensor_copy(
                          prefix.h[:, ec:ec + 1], t_cum.h[:, TS - 1:TS]),
                      reads=[t_cum], writes=[prefix], tag=f"pfx_s{s}e{ec}")

            if not is_pre:
                phi_chain(pbq, ec, t_phik, f"pq_s{s}e{ec}")
                kb.op("vector", lambda e: e.scalar_tensor_tensor(
                          t_ctx.ap(), t_cum.ap(), cfg.SCALE, t_phik.ap(),
                          ALU.mult, ALU.mult),
                      reads=[t_cum, t_phik], writes=[t_ctx], tag=f"ctx_s{s}e{ec}")
                split_pair(t_ctx, t_ctx.ap(),
                           ctx_hi, ctx_hi.h[:, ec * TS:(ec + 1) * TS],
                           ctx_lo, ctx_lo.h[:, ec * TS:(ec + 1) * TS],
                           f"ctxsp_s{s}e{ec}")

        if not is_pre:
            get_xr = make_x_stream(src_r, c0, f"xr{s}")
            load_wcol(wq_col[0], 0, (3,), 768, 256, f"wo_s{s}e0")
            for e2c in range(KC):
                bg_step(1)
                wo = wq_col[e2c % 2]
                if e2c + 1 < KC:
                    load_wcol(wq_col[(e2c + 1) % 2], e2c + 1, (3,),
                              (e2c + 1) * 1024 + 768, 256, f"wo_s{s}e{e2c + 1}")
                pbo = pb[7] if e2c % 2 == 0 else pb[4]
                proj3(pbo, wo, 256, 0, ctx_hi, ctx_lo, f"mmo_s{s}e{e2c}")
                xres, xres_ap = get_xr(e2c)
                xo = x2o[(e2c // 2) % 2]
                sub = (e2c % 2) * TS
                kb.op("vector", lambda e, e2c=e2c, xo=xo, sub=sub, xres=xres,
                      xres_ap=xres_ap, pbo=pbo: e.scalar_tensor_tensor(
                          xo.h[:, sub:sub + TS], pbo.h[:, 0:TS],
                          bias_sb.h[:, 3 * KC + e2c:3 * KC + e2c + 1],
                          xres_ap, ALU.add, ALU.add),
                      reads=[pbo, bias_sb, xres], writes=[xo],
                      tag=f"x2_s{s}e{e2c}")
                if e2c % 2 == 1:
                    def fn_st(eng, semh, xo=xo, e2c=e2c, c0=c0):
                        eng.dma_start(
                            out=x2T_r[:, e2c - 1:e2c + 1, c0:c0 + TS],
                            in_=xo.h.ap().rearrange("p (g t) -> p g t", g=2)
                            ).then_inc(semh, 16)
                    kb.dma("gpsimd", xo.name, fn_st, reads=[xo], writes=[x2T],
                           tag=f"x2st_s{s}e{e2c}")

    while bg_step():
        pass

    kb.barrier()
    stackA.close()

    ln2_hi = kb.sbuf("ln2_hi", [128, KC * TS], BF16)
    ln2_lo = kb.sbuf("ln2_lo", [128, KC * TS], BF16)
    g_hi = kb.sbuf("g_hi", [128, HC * TS], BF16)
    g_lo = kb.sbuf("g_lo", [128, HC * TS], BF16)
    q13c = [kb.sbuf(f"q13c{i}", [128, KC * 256], BF16) for i in range(2)]
    w2qb = [kb.sbuf(f"w2qb{i}", [128, HC * 128], BF16) for i in range(2)]
    silu_b = kb.sbuf("silu_b", [128, TS])
    g32 = kb.sbuf("g32", [128, TS])
    x2rd = [kb.sbuf(f"x2rd{i}", [128, TS]) for i in range(2)]
    ob = [kb.sbuf(f"ob{i}", [128, 2 * TS]) for i in range(2)]

    for fs in range(cfg.n_own):
        c0 = fs * TS

        layer_norm(lambda kc, g=make_x_stream(x2T_r, c0, f"x2s{fs}"): g(kc),
                   ln2_hi, ln2_lo, f"ln2_s{fs}")

        for hc in range(HC):
            qb = q13c[hc % 2]

            def fn_q13(eng, semh, qb=qb, hc=hc):
                eng.dma_start(out=qb.h.ap().rearrange("p (kc c) -> p kc c", kc=KC),
                              in_=fqc_r[:, :, hc * 256:(hc + 1) * 256]
                              ).then_inc(semh, 16)
            kb.dma("gpsimd", qb.name, fn_q13, reads=[fqc], writes=[qb],
                   tag=f"q13_{fs}_{hc}")

            ph1 = pb[4 + (hc % 2) * 2]
            ph3 = pb[5 + (hc % 2) * 2]

            def fn_h(e, qb, p, off):
                r = []
                for kc in range(KC):
                    w_ap = qb.h[:, kc * 256 + off:kc * 256 + off + 128]
                    r.append(e.matmul(p.h[:, 0:TS], w_ap,
                                      ln2_hi.h[:, kc * TS:(kc + 1) * TS],
                                      start=(kc == 0), stop=False))
                    r.append(e.matmul(p.h[:, 0:TS], w_ap,
                                      ln2_lo.h[:, kc * TS:(kc + 1) * TS],
                                      start=False, stop=(kc == KC - 1)))
                return r
            kb.op("tensor", lambda e, qb=qb, p=ph1: fn_h(e, qb, p, 0),
                  reads=[qb, ln2_hi, ln2_lo], writes=[ph1], tag=f"mmh1_{fs}_{hc}")
            kb.op("tensor", lambda e, qb=qb, p=ph3: fn_h(e, qb, p, 128),
                  reads=[qb, ln2_hi, ln2_lo], writes=[ph3], tag=f"mmh3_{fs}_{hc}")

            kb.op("scalar", lambda e, ph1=ph1: e.activation(
                      silu_b.ap(), ph1.h[:, 0:TS], AF.Silu, scale=svals.h[:, 0:1]),
                  reads=[ph1, svals], writes=[silu_b], tag=f"silu_{fs}_{hc}")
            kb.op("vector", lambda e, ph3=ph3: e.scalar_tensor_tensor(
                      g32.ap(), ph3.h[:, 0:TS], svals.h[:, 1:2], silu_b.ap(),
                      ALU.mult, ALU.mult),
                  reads=[ph3, svals, silu_b], writes=[g32], tag=f"g_{fs}_{hc}")
            split_pair(g32, g32.ap(), g_hi, g_hi.h[:, hc * TS:(hc + 1) * TS],
                       g_lo, g_lo.h[:, hc * TS:(hc + 1) * TS], f"gsp_{fs}_{hc}")

        for dc in range(KC):
            wq2 = w2qb[dc % 2]

            def fn_w2(eng, semh, wq2=wq2, dc=dc):
                eng.dma_start(out=wq2.h.ap().rearrange("p (hc c) -> p hc c", c=128),
                              in_=w2qc_r[:, :, dc * 128:(dc + 1) * 128]
                              ).then_inc(semh, 16)
            kb.dma("gpsimd", wq2.name, fn_w2, reads=[w2qc], writes=[wq2],
                   tag=f"w2q_{fs}_{dc}")

            xr = x2rd[dc % 2]

            def fn_xr(eng, semh, xr=xr, dc=dc, c0=c0):
                eng.dma_start(out=xr.ap(), in_=x2T_r[:, dc, c0:c0 + TS]
                              ).then_inc(semh, 16)
            kb.dma("sync", xr.name, fn_xr, reads=[x2T], writes=[xr],
                   tag=f"xr{fs}_{dc}")

            pf = pb[dc % 4]

            def fn_f(e, pf=pf, wq2=wq2):
                r = []
                for hc in range(HC):
                    w_ap = wq2.h[:, hc * 128:(hc + 1) * 128]
                    r.append(e.matmul(pf.h[:, 0:TS], w_ap,
                                      g_hi.h[:, hc * TS:(hc + 1) * TS],
                                      start=(hc == 0), stop=False))
                    r.append(e.matmul(pf.h[:, 0:TS], w_ap,
                                      g_lo.h[:, hc * TS:(hc + 1) * TS],
                                      start=False, stop=(hc == HC - 1)))
                return r
            kb.op("tensor", fn_f, reads=[wq2, g_hi, g_lo], writes=[pf],
                  tag=f"mmf{fs}_{dc}")

            o = ob[(dc // 2) % 2]
            sub = (dc % 2) * TS
            kb.op("vector", lambda e, pf=pf, xr=xr, o=o, sub=sub:
                  e.scalar_tensor_tensor(
                      o.h[:, sub:sub + TS], pf.h[:, 0:TS], svals.h[:, 2:3],
                      xr.ap(), ALU.mult, ALU.add),
                  reads=[pf, svals, xr], writes=[o], tag=f"out{fs}_{dc}")
            if dc % 2 == 1:
                def fn_ost(eng, semh, o=o, dc=dc, c0=c0):
                    eng.dma_start(out=outT_r[:, dc - 1:dc + 1, c0:c0 + TS],
                                  in_=o.h.ap().rearrange("p (g t) -> p g t", g=2)
                                  ).then_inc(semh, 16)
                kb.dma("gpsimd", o.name, fn_ost, reads=[o], writes=[outT],
                       tag=f"ost{fs}_{dc}")

    kb.raw_wait("gpsimd", extra=[(s, kb.semc[s]) for s in
                                 ("d_ob0", "d_ob1", "d_x2o0", "d_x2o1")
                                 if s in kb.semc])
    kb.emit()
    return nc




def kernel(L0_attn_x, L0_attn_Wq, L0_attn_bq, L0_attn_Wk, L0_attn_bk,
           L0_attn_Wv, L0_attn_bv, L0_attn_Wo, L0_attn_bo,
           L0_ffn_w1, L0_ffn_w3, L0_ffn_w2):
    cfg = CFG
    D, T, P, H, HP = cfg.D, cfg.T, cfg.P, cfg.H, cfg.HP
    x = np.ascontiguousarray(np.asarray(L0_attn_x, dtype=np.float32))
    B, L, _ = x.shape

    def t32(a):
        return np.ascontiguousarray(np.asarray(a, dtype=np.float32).T)

    wqT, wkT, wvT, woT = (t32(L0_attn_Wq), t32(L0_attn_Wk),
                          t32(L0_attn_Wv), t32(L0_attn_Wo))
    w1T = np.zeros((D, HP), np.float32)
    w1T[:, :H] = t32(L0_ffn_w1)
    w3T = np.zeros((D, HP), np.float32)
    w3T[:, :H] = t32(L0_ffn_w3)
    w2T = np.zeros((HP, D), np.float32)
    w2T[:H, :] = t32(L0_ffn_w2)
    zeros_pre = np.zeros((D, P), np.float32)

    in_maps = []
    for c in range(8):
        b, h = c // 2, c % 2
        xTc = np.ascontiguousarray(x[b, h * T:(h + 1) * T, :].T)
        if h == 1:
            xTpre = np.ascontiguousarray(x[b, 0:P, :].T)
            mask = np.ones((128, 1), np.float32)
        else:
            xTpre = zeros_pre
            mask = np.zeros((128, 1), np.float32)
        in_maps.append({
            "xT": xTc, "xTpre": xTpre, "pre_mask": mask,
            "wqT": wqT, "wkT": wkT, "wvT": wvT, "woT": woT,
            "bq": np.asarray(L0_attn_bq, np.float32),
            "bk": np.asarray(L0_attn_bk, np.float32),
            "bv": np.asarray(L0_attn_bv, np.float32),
            "bo": np.asarray(L0_attn_bo, np.float32),
            "w1T": w1T, "w3T": w3T, "w2T": w2T,
        })

    nc = build_nc(cfg)
    res = run_bass_kernel_spmd(nc, in_maps, core_ids=list(range(8)))

    out = np.empty((B, L, D), np.float32)
    for c in range(8):
        b, h = c // 2, c % 2
        out[b, h * T:(h + 1) * T, :] = np.asarray(res.results[c]["outT"]).T
    return out
